# version 01
# baseline (speedup 1.0000x reference)
import os
os.environ.setdefault("TRN_ALLOW_ICI_ALL_TO_ALL", "1")

import numpy as np
import concourse.bass as bass
import concourse.mybir as mybir
from concourse.bass_utils import run_bass_kernel_spmd

F32 = mybir.dt.float32
F16 = mybir.dt.float16
AF = mybir.ActivationFunctionType
ALU = mybir.AluOpType
AX = mybir.AxisListType

RC = 12582912.0
IQA = 1.0 / 32767.0
IQW = 1.0 / 7.0


class CFG:
    def __init__(self, S=4096, debug=False):
        self.debug = debug
        self.B = 4
        self.S = S
        self.D = 1024
        self.I = 3072
        self.TM = self.B * S // 8
        self.CH = S // 512
        self.TT = self.TM // 128
        self.TG = self.TM // 512
        self.IT = self.I // 128
        self.FT = self.D // 128
        self.TC = self.TM // 512


class Prog:
    ENGINES = ("sync", "gpsimd", "tensor", "scalar", "vector")

    def __init__(self):
        self.ops = []

    def op(self, engine, emit, waits=(), signal=None, amount=1, sem=None):
        self.ops.append((engine, emit, tuple(waits), signal, amount, sem))

    def build_and_emit(self, sems, engines):
        counts = {k: 0 for k in sems}
        table = {}
        for engine, emit, waits, signal, amount, semk in self.ops:
            if signal is not None:
                k = semk if semk is not None else "c_" + engine
                counts[k] += amount
                assert signal not in table, f"dup event {signal}"
                table[signal] = (k, counts[k])
        for engine, emit, waits, signal, amount, semk in self.ops:
            for w in waits:
                assert w in table, f"missing event {w}"
        watermark = {e: {k: 0 for k in sems} for e in self.ENGINES}

        def run(engine):
            eng_obj = engines[engine]
            for e2, emit, waits, signal, amount, semk in self.ops:
                if e2 != engine:
                    continue
                need = {}
                for w in waits:
                    k, v = table[w]
                    need[k] = max(need.get(k, 0), v)
                for k, v in need.items():
                    if watermark[engine][k] < v:
                        eng_obj.wait_ge(sems[k], v)
                        watermark[engine][k] = v
                inst = emit(eng_obj)
                if signal is not None:
                    k = semk if semk is not None else "c_" + engine
                    assert inst is not None, f"event {signal} has no instruction"
                    inst.then_inc(sems[k], amount)
        return run


def mem_map(cfg):
    M, o = {}, 17408

    def al(name, nbytes, at=None):
        nonlocal o
        if at is not None:
            M[name] = at
            return at
        M[name] = o
        o += (nbytes + 31) // 32 * 32
        return M[name]

    al("ident", 256); al("maskb", 256)
    al("wq16", cfg.FT * 256); al("wk16", cfg.FT * 256); al("wv16", cfg.FT * 256)
    al("wo16", cfg.FT * 2048)
    al("scal", 256)
    al("dqs", 128); al("dqr", 128)
    al("bcrow", 512); al("bcones", 512)
    P1 = o
    al("xtb", 2 * cfg.FT * 1024)
    al("qt", 2 * cfg.S * 2)
    al("ktp0", 2 * cfg.S * 2)
    al("ktp1", 2 * cfg.S * 2)
    al("vtt", 2 * (cfg.S // 128) * 264)
    al("expst", 6 * 2048)
    al("ctxq", 2 * (cfg.S // 128) * 256)
    al("dqz", 4096)
    al("dqw32", 2 * 4096)
    al("dq16", 2 * 2048)
    p1_end = o
    al("ctxT", 0, at=P1)
    al("xq16", 0, at=P1)
    ph2 = P1 + max(cfg.FT * cfg.TM * 2, cfg.TT * 2048)
    al("y32", cfg.TT * 4096, at=ph2); ph2 += cfg.TT * 4096
    al("xmst", 0, at=ph2); ph2 += 2 * 4096
    al("lnsc", 0, at=ph2); ph2 += 2 * 4096
    al("lnsq", 0, at=ph2); ph2 += 4096
    al("lnst", 0, at=ph2); ph2 += 64
    al("lnst2", 0, at=ph2); ph2 += 64
    al("ytq", 0, at=P1)
    al("ot", 0, at=P1)
    base3 = P1 + max(cfg.FT * cfg.TM * 2, cfg.IT * 1024)
    al("wtg", 0, at=base3); base3 += cfg.FT * 6144
    al("wtd", 2 * 2048, at=base3); base3 += 2 * 2048
    al("stg1", 0, at=base3); base3 += 2 * cfg.TM * 2
    al("stg2", 0, at=base3); base3 += 2 * cfg.TM * 2
    al("stg3", 0, at=base3); base3 += 2 * cfg.TM * 2
    al("fA", 0, at=base3); base3 += cfg.TM * 4
    al("fB", 0, at=base3); base3 += cfg.TM * 4
    al("fC", 0, at=base3); base3 += cfg.TM * 4
    al("yrl", 0, at=base3); base3 += 2 * 4096
    al("outb", 0, at=base3); base3 += 2 * 4096
    al("dnsc", 0, at=base3); base3 += 2 * 4096
    M["_END"] = max(p1_end, ph2, base3)
    return M


def build_nc(cfg):
    nc = bass.Bass("TRN2", target_bir_lowering=False,
                   detect_race_conditions=False)
    B, S, D, I = cfg.B, cfg.S, cfg.D, cfg.I
    TM, CH, TT, TG, IT, FT, TC = (cfg.TM, cfg.CH, cfg.TT, cfg.TG, cfg.IT,
                                  cfg.FT, cfg.TC)
    M = mem_map(cfg)
    assert M["_END"] <= 200 * 1024, M["_END"]

    xt = nc.dram_tensor("xt", [D, B * S], F32, kind="ExternalInput").ap()
    xm = nc.dram_tensor("xm", [TM, D], F32, kind="ExternalInput").ap()
    wq = nc.dram_tensor("wq", [D, 128], F32, kind="ExternalInput").ap()
    wk = nc.dram_tensor("wk", [D, 128], F32, kind="ExternalInput").ap()
    wv = nc.dram_tensor("wv", [D, 128], F32, kind="ExternalInput").ap()
    wo = nc.dram_tensor("wo", [D, D], F32, kind="ExternalInput").ap()
    wgn = nc.dram_tensor("wgn", [I, D], F32, kind="ExternalInput").ap()
    wun = nc.dram_tensor("wun", [I, D], F32, kind="ExternalInput").ap()
    wdn = nc.dram_tensor("wdn", [D, I], F32, kind="ExternalInput").ap()
    maskc = nc.dram_tensor("maskc", [128, 128], F16, kind="ExternalInput").ap()
    identc = nc.dram_tensor("identc", [128, 128], F16, kind="ExternalInput").ap()
    out = nc.dram_tensor("out", [TM, D], F32, kind="ExternalOutput").ap()

    dbg = "ExternalOutput" if cfg.debug else "Internal"
    a2a_i = nc.dram_tensor("a2a_i", [8, TM, 128], F16)
    a2a_o = nc.dram_tensor("a2a_o", [8, TM, 128], F16)
    mxi = [nc.dram_tensor(f"mxi{k}", [1, 1], F32) for k in range(5)]
    mxo = [nc.dram_tensor(f"mxo{k}", [1, 1], F32) for k in range(5)]
    mxp = nc.dram_tensor("mxp", [128], F32)
    wg16d = nc.dram_tensor("wg16d", [I, D], F16, kind=dbg)
    wu16d = nc.dram_tensor("wu16d", [I, D], F16, kind=dbg)
    wd16d = nc.dram_tensor("wd16d", [D, I], F16, kind=dbg)
    graw = nc.dram_tensor("graw", [I, TM], F16, kind=dbg)
    uraw = nc.dram_tensor("uraw", [I, TM], F16, kind=dbg)
    uqd = nc.dram_tensor("uqd", [I, TM], F16, kind=dbg)
    prodd = nc.dram_tensor("prodd", [I, TM], F16, kind=dbg)
    qsd = nc.dram_tensor("qsd", [I, TM], F16, kind=dbg)
    xqd = nc.dram_tensor("xqd", [TM, D], F16, kind=dbg)
    yd = nc.dram_tensor("yd", [TM, D], F32, kind=dbg)

    def t16(name, *shape, key=None):
        return nc.alloc_sbuf_tensor_at(name, [128, *shape], F16, offset=M[key or name])

    def t32(name, *shape, key=None):
        return nc.alloc_sbuf_tensor_at(name, [128, *shape], F32, offset=M[key or name])

    ident = t16("ident", 128)
    maskb = t16("maskb", 128)
    wq16 = t16("wq16", FT, 128)
    wk16 = t16("wk16", FT, 128)
    wv16 = t16("wv16", FT, 128)
    wo16 = t16("wo16", FT, 1024)
    scal = t32("scal", 64)
    dqs = t32("dqs", 32)
    dqr = t32("dqr", 32)
    bcrow = nc.alloc_sbuf_tensor_at("bcrow", [1, 128], F32, offset=M["bcrow"])
    bcones = nc.alloc_sbuf_tensor_at("bcones", [1, 128], F32, offset=M["bcones"])
    xtb = t16("xtb", 2, FT, 512)
    qt = t16("qt", 2, S)
    ktp0 = t16("ktp0", 2, S)
    ktp1 = t16("ktp1", 2, S)
    vtt = t16("vtt", 2, S // 128, 132)
    expst = t16("expst", 6, 1024)
    ctxq = t16("ctxq", 2, S // 128, 128)
    dqz = t32("dqz", 1024)
    dqw32 = t32("dqw32", 2, 1024)
    dq16 = t16("dq16", 2, 1024)
    ctxT = t16("ctxT", FT, TM)
    xq16 = t16("xq16", TT, 1024)
    y32 = t32("y32", TT, 1024)
    xmst = t32("xmst", 2, 1024)
    lnsc = t32("lnsc", 2, 1024)
    lnsq = t32("lnsq", 1024)
    lnst = t32("lnst", 16)
    lnst2 = t32("lnst2", 16)
    ytq = t16("ytq", FT, TM)
    ot = t16("ot", IT, 512)
    wtg = t16("wtg", FT, 3072)
    wtd = t16("wtd", 2, 1024)
    stg1 = t16("stg1", 2, TM)
    stg2 = t16("stg2", 2, TM)
    stg3 = t16("stg3", 2, TM)
    fA = t32("fA", TM)
    fB = t32("fB", TM)
    fC = t32("fC", TM)
    yrl = t32("yrl", 2, 1024)
    outb = t32("outb", 2, 1024)
    dnsc = t32("dnsc", 2, 1024)

    psQ = nc.alloc_psum_tensor("psQ", [128, 1024], F32)
    psK = nc.alloc_psum_tensor("psK", [128, 1024], F32)
    psC = nc.alloc_psum_tensor("psC", [128, 1024], F32)
    psX = nc.alloc_psum_tensor("psX", [128, 1024], F32)

    SL = dict(r1=0, s1=1, r3=2, s3=3, c13=4, r2=5, s2=6, r4=7, s4=8, r5=9,
              s5=10, row=16, m1a=24, m3a=25, m2a=26, m4a=27, m5a=28, tmp=29)

    P = Prog()
    vself = {"n": 0, "sem": None}

    def vsync(v, inst):
        vself["n"] += 1
        inst.then_inc(vself["sem"], 1)
        v.wait_ge(vself["sem"], vself["n"])

    def vreduce(v, out, in_, **kw):
        inst = v.tensor_reduce(out, in_, **kw)
        vsync(v, inst)
        return inst

    def vrecip(v, out, in_):
        inst = v.reciprocal(out, in_)
        vsync(v, inst)
        return inst

    def dma_g(name, o_, i_, waits=(), ch="gW"):
        P.op("gpsimd", lambda g, o_=o_, i_=i_: g.dma_start(out=o_, in_=i_),
             waits, signal=name, amount=16, sem=ch)

    def dma_s(name, o_, i_, waits=(), tr=False, ch="ini"):
        if tr:
            P.op("sync", lambda s, o_=o_, i_=i_: s.dma_start(o_, i_, transpose=True),
                 waits, signal=name, amount=16, sem=ch)
        else:
            P.op("sync", lambda s, o_=o_, i_=i_: s.dma_start(out=o_, in_=i_),
                 waits, signal=name, amount=16, sem=ch)

    def xre(dram_ap, p=128):
        return dram_ap.rearrange("(a p) n -> p a n", p=p)

    dma_s("ld_mask", maskb[:, :], maskc)
    dma_s("ld_ident", ident[:, :], identc)
    dma_g("ld_wq", wq16[:, :, :], xre(wq))
    dma_g("ld_wk", wk16[:, :, :], xre(wk))
    dma_g("ld_wv", wv16[:, :, :], xre(wv))
    dma_g("ld_wo", wo16[:, :, :], xre(wo))

    def v_init(v):
        v.memset(bcones[:, :], 1.0)
        v.memset(ktp0[64:128, :, :], 0.0)
        v.memset(ktp1[0:64, :, :], 0.0)
        v.memset(vtt[:, :, :, 64:65], 1.0)
        v.memset(vtt[:, :, :, 129:130], 1.0)
        return v.memset(scal[:, SL["m1a"]:SL["m5a"] + 1], 0.0)
    P.op("vector", v_init, (), signal="v_init")

    nchunks = B * CH
    for i in range(nchunks):
        b, c = divmod(i, CH)
        w = [f"pj_mm{i-2}"] if i >= 2 else []
        dma_g(f"ld_xt{i}", xtb[:, i % 2, :, :],
              xre(xt)[:, :, b * S + 512 * c: b * S + 512 * c + 512], waits=w,
              ch=f"g{i % 2}")

    ndq = IT + IT + 3 * (D // 128)

    def dq_src(i):
        if i < IT:
            return xre(wgn)[:, i, :], xre(wg16d)[:, i, :]
        if i < 2 * IT:
            return xre(wun)[:, i - IT, :], xre(wu16d)[:, i - IT, :]
        r, cc = divmod(i - 2 * IT, 3)
        return (xre(wdn)[:, r, 1024 * cc:1024 * cc + 1024],
                xre(wd16d)[:, r, 1024 * cc:1024 * cc + 1024])

    def emit_dq_ld(i):
        w = [f"dq_done{i-2}"] if i >= 2 else []
        dma_s(f"dq_ld{i}", dqw32[:, i % 2, :], dq_src(i)[0], waits=w,
              ch=f"dqL{i % 2}")

    def emit_dq_st(i):
        dma_s(f"dq_st{i}", dq_src(i)[1], dq16[:, i % 2, :], waits=[f"dq_done{i}"],
              ch=f"dqS{i % 2}")

    dq_queue = []
    for i in range(ndq):
        def dq_ops(v, i=i):
            wt = dqw32[:, i % 2, :]
            for hh in range(2):
                vreduce(v, dqs[:, 16 * hh:16 * hh + 16].unsqueeze(2),
                        wt[:, 512 * hh:512 * hh + 512].rearrange("p (a n) -> p a n", n=32),
                        axis=AX.X, op=ALU.max, apply_absolute_value=True)
            vsync(v, v.tensor_scalar(dqs[:, :], dqs[:, :], IQW, 1e-12,
                                     op0=ALU.mult, op1=ALU.max))
            vrecip(v, dqr[:, :], dqs[:, :])
            v.tensor_tensor(dqz[:, :].rearrange("p (a n) -> p a n", n=32),
                            wt.rearrange("p (a n) -> p a n", n=32),
                            dqr[:, :].unsqueeze(2).broadcast_to([128, 32, 32]),
                            op=ALU.mult)
            v.tensor_scalar(dqz[:, :], dqz[:, :], RC, None, op0=ALU.add)
            return v.scalar_tensor_tensor(
                dq16[:, i % 2, :].rearrange("p (a n) -> p a n", n=32),
                dqz[:, :].rearrange("p (a n) -> p a n", n=32), -RC,
                dqs[:, :].unsqueeze(2).broadcast_to([128, 32, 32]),
                op0=ALU.add, op1=ALU.mult)
        wts = [f"dq_ld{i}"] + ([f"dq_st{i-2}"] if i >= 2 else [])
        dq_queue.append((dq_ops, wts, f"dq_done{i}"))

    dq_state = {"pop": 0, "ld": 0}

    def pop_dq(n):
        for _ in range(n):
            if dq_state["pop"] >= ndq:
                return
            i = dq_state["pop"]
            dq_state["pop"] += 1
            while dq_state["ld"] <= min(i + 2, ndq - 1):
                emit_dq_ld(dq_state["ld"])
                dq_state["ld"] += 1
            fn, w, sig = dq_queue[i]
            P.op("vector", fn, w, signal=sig)
            emit_dq_st(i)

    def av_targets(c, kt):
        return range(max(0, kt - 4 * c), 4)

    def ctx_off(m):
        return 65 * m if m < 7 else 512

    nsc = 0
    for b in range(B):
        for c in range(CH):
            i = b * CH + c

            def emit_pj(t, i=i, b=b, c=c):
                qb = psQ[:, 512 * (i % 2):512 * (i % 2) + 512]
                kb = psK[:, 512 * (i % 2):512 * (i % 2) + 512]
                vb = psC[:, 512 * (i % 2):512 * (i % 2) + 512]
                for ft in range(FT):
                    t.matmul(qb, wq16[:, ft, :], xtb[:, i % 2, ft, :],
                             start=(ft == 0), stop=(ft == FT - 1))
                for ft in range(FT):
                    t.matmul(kb, wk16[:, ft, :], xtb[:, i % 2, ft, :],
                             start=(ft == 0), stop=(ft == FT - 1))
                last = None
                for q in range(4):
                    for ft in range(FT):
                        last = t.matmul(vb[:, 128 * q:128 * q + 128],
                                        xtb[:, i % 2, ft, 128 * q:128 * q + 128],
                                        wv16[:, ft, :],
                                        start=(ft == 0), stop=(ft == FT - 1))
                return last
            w = [f"ld_xt{i}"]
            if i == 0:
                w += ["ld_wq", "ld_wk", "ld_wv", "ld_wo", "v_init",
                      "ld_mask", "ld_ident"]
            if i >= 2:
                w += [f"pj_ev{i-2}"]
            if c == 0 and b >= 1:
                w += [f"cx_ev{2 * b * CH - 1}"]
            P.op("tensor", emit_pj, w, signal=f"pj_mm{i}")

            def emit_ev(a, i=i, b=b, c=c):
                qb = psQ[:, 512 * (i % 2):512 * (i % 2) + 512]
                kb = psK[:, 512 * (i % 2):512 * (i % 2) + 512]
                vb = psC[:, 512 * (i % 2):512 * (i % 2) + 512]
                a.activation(qt[:, b % 2, 512 * c:512 * c + 512], qb, AF.Tanh)
                a.activation(ktp0[0:64, b % 2, 512 * c:512 * c + 512], kb[0:64, :], AF.Tanh)
                a.activation(ktp1[64:128, b % 2, 512 * c:512 * c + 512], kb[64:128, :], AF.Tanh)
                a.activation(vtt[:, b % 2, 4 * c:4 * c + 4, 0:64],
                             vb.rearrange("p (a n) -> p a n", n=128)[:, :, 0:64], AF.Tanh)
                return a.activation(vtt[:, b % 2, 4 * c:4 * c + 4, 65:129],
                                    vb.rearrange("p (a n) -> p a n", n=128)[:, :, 64:128],
                                    AF.Tanh)
            P.op("scalar", emit_ev, [f"pj_mm{i}"], signal=f"pj_ev{i}")

        def avbank(q):
            return (psX if q < 2 else psC)[:, 512 * (q % 2):512 * (q % 2) + 512]

        for c in range(CH):
            uidx = b * CH + c

            for h in range(2):
                blk = 2 * uidx + h
                pend = []

                def emit_av(t, ph, pg, pglob, b=b, c=c):
                    last = None
                    for ktl in (2 * pg, 2 * pg + 1):
                        slot = pglob % 6
                        half = ktl % 2
                        for q in av_targets(c, ktl):
                            last = t.matmul(
                                avbank(q)[:, 0:65],
                                expst[:, slot, 512 * half + 128 * q:512 * half + 128 * q + 128],
                                vtt[:, b % 2, ktl, 65 * ph:65 * ph + 65],
                                start=(ktl == 0), stop=(ktl == 4 * c + q))
                    return last

                for g in range(2 * c + 2):
                    glob = nsc

                    def emit_sc(t, h=h, g=g, glob=glob, b=b, c=c):
                        last = None
                        for ktl in (2 * g, 2 * g + 1):
                            bank = (psQ if glob % 2 == 0 else psK)[
                                :, 512 * (ktl % 2):512 * (ktl % 2) + 512]
                            ktp = ktp0 if h == 0 else ktp1
                            diag = ktl >= 4 * c
                            last = t.matmul(bank, ktp[:, b % 2, 128 * ktl:128 * ktl + 128],
                                            qt[:, b % 2, 512 * c:512 * c + 512],
                                            start=True, stop=not diag)
                            if diag:
                                r = ktl - 4 * c
                                last = t.matmul(bank[:, 128 * r:128 * r + 128],
                                                ident[:, :], maskb[:, :],
                                                start=False, stop=True)
                        return last
                    w = []
                    if g == 0 and blk == 0:
                        w += [f"pj_ev{CH - 1}"]
                    elif g == 0 and h == 0:
                        w += [f"pj_ev{(b + 1) * CH - 1}"]
                    if glob >= 2:
                        w += [f"at_ex{glob-2}"]
                    P.op("tensor", emit_sc, w, signal=f"at_sc{glob}")

                    def emit_ex(a, glob=glob):
                        return a.activation(expst[:, glob % 6, :],
                                            (psQ if glob % 2 == 0 else psK)[:, 0:1024],
                                            AF.Exp, scale=0.125)
                    we = [f"at_sc{glob}"]
                    if glob >= 6:
                        we += [f"at_av{glob-6}"]
                    P.op("scalar", emit_ex, we, signal=f"at_ex{glob}")

                    pend.append((h, g, glob))
                    nsc += 1
                    if len(pend) >= 2:
                        ph_, pg_, pglob_ = pend.pop(0)
                        wv = [f"at_ex{pglob_}"]
                        if pg_ == 0 and blk > 0:
                            wv += [f"cx_ev{blk-1}"]
                        P.op("tensor",
                             lambda t, ph=ph_, pg=pg_, pglob=pglob_, fn=emit_av: fn(t, ph, pg, pglob),
                             wv, signal=f"at_av{pglob_}")
                ph_, pg_, pglob_ = pend.pop(0)
                wv = [f"at_ex{pglob_}"]
                if pg_ == 0 and blk > 0:
                    wv += [f"cx_ev{blk-1}"]
                P.op("tensor",
                     lambda t, ph=ph_, pg=pg_, pglob=pglob_, fn=emit_av: fn(t, ph, pg, pglob),
                     wv, signal=f"at_av{pglob_}")

                def emit_cx(v, b=b, c=c, h=h):
                    last = None
                    for q in range(4):
                        j = 4 * c + q
                        bank = avbank(q)
                        vrecip(v, scal[:, SL["tmp"]:SL["tmp"] + 1],
                               bank[:, 64:65])
                        last = v.tensor_scalar(
                            ctxq[:, b % 2, j, 64 * h:64 * h + 64],
                            bank[:, 0:64],
                            scal[:, SL["tmp"]:SL["tmp"] + 1], None, op0=ALU.mult)
                    return last
                wcx = [f"at_av{nsc-1}"]
                if b >= 2:
                    wcx += [f"cxst{b-2}b"]
                P.op("vector", emit_cx, wcx, signal=f"cx_ev{blk}")
                pop_dq(1)

        dma_s(f"cxst{b}a", a2a_i[2 * b, :, :].rearrange("(a p) d -> p a d", p=128),
              ctxq[:, b % 2, 0:S // 256, :], waits=[f"cx_ev{2 * (b + 1) * CH - 1}"],
              ch="cx")
        dma_s(f"cxst{b}b", a2a_i[2 * b + 1, :, :].rearrange("(a p) d -> p a d", p=128),
              ctxq[:, b % 2, S // 256:S // 128, :], waits=[f"cx_ev{2 * (b + 1) * CH - 1}"],
              ch="cx")

    n_groups = nsc

    P.op("gpsimd", lambda g: g.collective_compute(
        "AllToAll", ALU.bypass, replica_groups=[list(range(8))],
        ins=[a2a_i.ap().opt()], outs=[a2a_o.ap().opt()]),
        [f"cxst{B-1}b"], signal="a2a", amount=1, sem="cc")

    for j in range(FT):
        dma_s(f"ctxT{j}", ctxT[:, j, :], a2a_o[j, :, :], waits=["a2a"], tr=True,
              ch="cT")

    pop_dq(len(dq_queue))

    for tt in range(TT):
        w = ([f"ln_c{tt-2}"] if tt >= 2 else [])
        dma_s(f"xm{tt}", xmst[:, tt % 2, :], xre(xm)[:, tt, :], waits=w,
              ch=f"xm{tt % 2}")

    for tt in range(TT):
        def emit_wo(t, tt=tt):
            ps = psQ if tt % 2 == 0 else psK
            last = None
            for ch in range(2):
                bank = ps[:, 512 * ch:512 * ch + 512]
                for j in range(FT):
                    last = t.matmul(bank, ctxT[:, j, 128 * tt:128 * tt + 128],
                                    wo16[:, j, 512 * ch:512 * ch + 512],
                                    start=(j == 0), stop=(j == FT - 1))
            return last
        if tt == 0:
            w = [f"ctxT{FT-1}", f"at_ex{n_groups-1}"]
        elif tt == 1:
            w = [f"ctxT{FT-1}"]
        else:
            w = [f"wo_ev{tt-2}"]
        P.op("tensor", emit_wo, w, signal=f"wo_mm{tt}")

        def emit_woev(a, tt=tt):
            ps = psQ if tt % 2 == 0 else psK
            a.activation(lnsc[:, tt % 2, 0:512], ps[:, 0:512], AF.Tanh)
            return a.activation(lnsc[:, tt % 2, 512:1024], ps[:, 512:1024], AF.Tanh)
        P.op("scalar", emit_woev, [f"wo_mm{tt}"], signal=f"wo_ev{tt}")

        def emit_res(v, tt=tt):
            v.tensor_tensor(y32[:, tt, :], lnsc[:, tt % 2, :], xmst[:, tt % 2, :],
                            op=ALU.add)
            return v.reduce_sum(lnst[:, tt:tt + 1], y32[:, tt, :], axis=AX.X)
        P.op("vector", emit_res, [f"wo_ev{tt}", f"xm{tt}"], signal=f"ln_c{tt}")

        def emit_sq(a, tt=tt):
            return a.activation(lnsq[:, :], y32[:, tt, :], AF.Square,
                                accum_out=lnst2[:, tt:tt + 1])
        P.op("scalar", emit_sq, [f"ln_c{tt}"], signal=f"ln_sq{tt}")

        def emit_var(v, tt=tt):
            m = lnst[:, tt:tt + 1]
            s2 = lnst2[:, tt:tt + 1]
            v.tensor_scalar(m, m, 1.0 / D, None, op0=ALU.mult)
            v.tensor_tensor(scal[:, SL["tmp"]:SL["tmp"] + 1], m, m, op=ALU.mult)
            v.tensor_scalar(s2, s2, 1.0 / D, None, op0=ALU.mult)
            v.tensor_tensor(s2, s2, scal[:, SL["tmp"]:SL["tmp"] + 1],
                            op=ALU.subtract)
            return v.tensor_scalar(s2, s2, 1e-5, None, op0=ALU.add)
        P.op("vector", emit_var, [f"ln_sq{tt}"], signal=f"ln_v{tt}")

        def emit_rs(a, tt=tt):
            return a.activation(lnst2[:, tt:tt + 1], lnst2[:, tt:tt + 1],
                                AF.Sqrt)
        P.op("scalar", emit_rs, [f"ln_v{tt}"], signal=f"ln_rs{tt}")

        def emit_y(v, tt=tt):
            vrecip(v, lnst2[:, tt:tt + 1], lnst2[:, tt:tt + 1])
            v.tensor_scalar(y32[:, tt, :], y32[:, tt, :], lnst[:, tt:tt + 1],
                            lnst2[:, tt:tt + 1], op0=ALU.subtract, op1=ALU.mult)
            vreduce(v, scal[:, SL["tmp"]:SL["tmp"] + 1], y32[:, tt, :],
                    axis=AX.X, op=ALU.max, apply_absolute_value=True)
            return v.tensor_tensor(scal[:, SL["m1a"]:SL["m1a"] + 1],
                                   scal[:, SL["m1a"]:SL["m1a"] + 1],
                                   scal[:, SL["tmp"]:SL["tmp"] + 1], op=ALU.max)
        P.op("vector", emit_y, [f"ln_rs{tt}"], signal=f"ln_y{tt}")

    dma_s("y_spill", xre(yd), y32[:, :, :], waits=[f"ln_y{TT-1}"], ch="sp")

    def mbarrier(idx, name, acc_slot, extra_c13, waits):
        dma_g(f"{name}_p1", mxp.ap().rearrange("(p n) -> p n", n=1),
              scal[:, acc_slot:acc_slot + 1], waits=waits, ch="gM")
        dma_g(f"{name}_p2", bcrow[:, :],
              mxp.ap().rearrange("(n p) -> n p", p=128), waits=[f"{name}_p1"],
              ch="gM")

        def vpar(v):
            vreduce(v, scal[0:1, SL["tmp"] + 1:SL["tmp"] + 2],
                    bcrow[:, :], axis=AX.X, op=ALU.max,
                    apply_absolute_value=True)
            return v.nop()
        P.op("vector", vpar, [f"{name}_p2"], signal=f"{name}_par")
        dma_g(f"{name}_o", mxi[idx][:, :], scal[0:1, SL["tmp"] + 1:SL["tmp"] + 2],
              waits=[f"{name}_par"], ch="gM")
        P.op("gpsimd", lambda g: g.collective_compute(
            "AllReduce", ALU.max, replica_groups=[list(range(8))],
            ins=[mxi[idx].ap().opt()], outs=[mxo[idx].ap().opt()]),
            [f"{name}_o"], signal=f"{name}_ar", amount=1, sem="cc")
        dma_g(f"{name}_i", scal[0:1, SL["row"]:SL["row"] + 1], mxo[idx][:, :],
              waits=[f"{name}_ar"], ch="gM")

        def vscale(v):
            r = SL["row"]
            vsync(v, v.tensor_scalar(scal[0:1, r + 1:r + 2], scal[0:1, r:r + 1],
                                     IQA, 1e-12, op0=ALU.mult, op1=ALU.max))
            vrecip(v, scal[0:1, r + 2:r + 3], scal[0:1, r + 1:r + 2])
            if extra_c13:
                return v.tensor_scalar(scal[0:1, r + 3:r + 4], scal[0:1, r + 1:r + 2],
                                       1.0 / 65536.0, None, op0=ALU.mult)
            return v.nop()
        P.op("vector", vscale, [f"{name}_i"], signal=f"{name}_vs")

    def mb_bcast(name, slots):
        def bcmm(t):
            last = None
            for n, (dst, ro) in enumerate(slots):
                last = t.matmul(psX[:, 960 + n:960 + n + 1], bcones[:, :],
                                scal[0:1, SL["row"] + ro:SL["row"] + ro + 1],
                                start=True, stop=True)
            return last
        P.op("tensor", bcmm, [f"{name}_vs"], signal=f"{name}_bm")

        def bccp(v):
            last = None
            for n, (dst, ro) in enumerate(slots):
                last = v.tensor_copy(scal[:, dst:dst + 1],
                                     psX[:, 960 + n:960 + n + 1])
            vsync(v, last)
            return v.nop()
        P.op("vector", bccp, [f"{name}_bm"], signal=f"{name}_bc")

    mbarrier(0, "m1", SL["m1a"], False, [f"ln_y{TT-1}"])
    mb_bcast("m1", [(SL["s1"], 1), (SL["r1"], 2)])

    for tt in range(TT):
        def emit_xq(v, tt=tt):
            v.tensor_scalar(lnsq[:, :], y32[:, tt, :], scal[:, SL["r1"]:SL["r1"] + 1],
                            RC, op0=ALU.mult, op1=ALU.add)
            return v.tensor_scalar(xq16[:, tt, :], lnsq[:, :], -RC,
                                   scal[:, SL["s1"]:SL["s1"] + 1],
                                   op0=ALU.add, op1=ALU.mult)
        P.op("vector", emit_xq, ["m1_bc"] if tt == 0 else [], signal=f"xq{tt}")
    dma_s("xq_spill", xre(xqd), xq16[:, :, :], waits=[f"xq{TT-1}"], ch="sp")

    for ft in range(FT):
        dma_s(f"ytq{ft}", ytq[:, ft, :], xqd[:, 128 * ft:128 * ft + 128],
              waits=["xq_spill"], tr=True, ch="ytq")
    for ft in range(FT):
        dma_s(f"wtgG{ft}", wtg[:, ft, :], wg16d[:, 128 * ft:128 * ft + 128],
              waits=[f"dq_st{IT-2}", f"dq_st{IT-1}"], tr=True, ch="wgG")

    def bank4(n):
        return (psQ if (n % 4) < 2 else psK)[:, 512 * (n % 2):512 * (n % 2) + 512]

    nmm = 0
    for wsel in range(2):
        for it in range(IT):
            for tc in range(TC):
                def emit_ug(t, it=it, tc=tc, nmm=nmm):
                    bk = bank4(nmm)
                    last = None
                    for ft in range(FT):
                        last = t.matmul(bk, wtg[:, ft, 128 * it:128 * it + 128],
                                        ytq[:, ft, 512 * tc:512 * tc + 512],
                                        start=(ft == 0), stop=(ft == FT - 1))
                    return last
                w = []
                if nmm == 0:
                    w = [f"ytq{FT-1}", f"wtgG{FT-1}", f"wo_ev{TT-1}"]
                if wsel == 1 and it == 0 and tc == 0:
                    w += [f"wtgU{FT-1}"]
                if nmm >= 4:
                    w += [f"ug_ev{nmm-4}"]
                P.op("tensor", emit_ug, w, signal=f"ug_mm{nmm}")

                def emit_ugev(a, wsel=wsel, it=it, tc=tc, nmm=nmm):
                    stg = stg1 if wsel == 0 else stg2
                    return a.activation(stg[:, it % 2, 512 * tc:512 * tc + 512],
                                        bank4(nmm), AF.Copy)
                wev = [f"ug_mm{nmm}"]
                if nmm >= 2:
                    wev += [f"ug_mx{nmm-2}"]
                P.op("scalar", emit_ugev, wev, signal=f"ug_ev{nmm}")

                def emit_ugmx(v, wsel=wsel, it=it, tc=tc):
                    stg = stg1 if wsel == 0 else stg2
                    acc = SL["m3a"] if wsel == 0 else SL["m2a"]
                    vreduce(v, scal[:, SL["tmp"]:SL["tmp"] + 1],
                            stg[:, it % 2, 512 * tc:512 * tc + 512],
                            axis=AX.X, op=ALU.max, apply_absolute_value=True)
                    return v.tensor_tensor(scal[:, acc:acc + 1], scal[:, acc:acc + 1],
                                           scal[:, SL["tmp"]:SL["tmp"] + 1], op=ALU.max)
                P.op("vector", emit_ugmx, [f"ug_ev{nmm}"], signal=f"ug_mx{nmm}")
                nmm += 1
            stgd = graw if wsel == 0 else uraw
            stgs = stg1 if wsel == 0 else stg2
            dma_s(f"rawsp{wsel}_{it}", xre(stgd)[:, it, :], stgs[:, it % 2, :],
                  waits=[f"ug_ev{nmm-1}"], ch="rwG" if wsel == 0 else "rwU")
        if wsel == 0:
            for ft in range(FT):
                dma_s(f"wtgU{ft}", wtg[:, ft, :], wu16d[:, 128 * ft:128 * ft + 128],
                      waits=[f"ug_mm{IT * TC - 1}", f"dq_st{2*IT-2}", f"dq_st{2*IT-1}"],
                      tr=True, ch="wgU")

    mbarrier(1, "m3", SL["m3a"], True, [f"ug_mx{IT * TC - 1}"])
    mb_bcast("m3", [(SL["s3"], 1), (SL["r3"], 2), (SL["c13"], 3)])
    mbarrier(2, "m2", SL["m2a"], False, [f"ug_mx{2 * IT * TC - 1}"])
    mb_bcast("m2", [(SL["s2"], 1), (SL["r2"], 2)])

    for it in range(IT):
        w = [f"rawsp0_{IT-1}"]
        if it >= 2:
            w += [f"s1v{it-2}"]
        dma_s(f"s1g{it}", stg1[:, it % 2, :], xre(graw)[:, it, :], waits=w,
              ch=f"fI{it % 2}")

        def emit_s1a(v, it=it):
            v.tensor_scalar(fA[:, :], stg1[:, it % 2, :], scal[:, SL["r3"]:SL["r3"] + 1],
                            RC, op0=ALU.mult, op1=ALU.add)
            return v.tensor_scalar(fA[:, :], fA[:, :], -RC, None, op0=ALU.add)
        w = [f"s1g{it}", "m3_bc"]
        if it >= 1:
            w += [f"s1sg{it-1}"]
        P.op("vector", emit_s1a, w, signal=f"s1a{it}")

        def emit_sig(a, it=it):
            return a.activation(fB[:, :], fA[:, :], AF.Sigmoid,
                                scale=scal[:, SL["s3"]:SL["s3"] + 1])
        wsg = [f"s1a{it}"]
        if it >= 1:
            wsg += [f"s1v{it-1}"]
        P.op("scalar", emit_sig, wsg, signal=f"s1sg{it}")

        def emit_s1b(v, it=it):
            v.tensor_scalar(fB[:, :], fB[:, :], 65536.0, RC, op0=ALU.mult, op1=ALU.add)
            v.tensor_scalar(fB[:, :], fB[:, :], -RC, 65535.0, op0=ALU.add, op1=ALU.min)
            v.scalar_tensor_tensor(fC[:, :], fA[:, :], scal[:, SL["c13"]:SL["c13"] + 1],
                                   fB[:, :], op0=ALU.mult, op1=ALU.mult)
            vreduce(v, scal[:, SL["tmp"]:SL["tmp"] + 1], fC[:, :],
                    axis=AX.X, op=ALU.max, apply_absolute_value=True)
            v.tensor_tensor(scal[:, SL["m4a"]:SL["m4a"] + 1],
                            scal[:, SL["m4a"]:SL["m4a"] + 1],
                            scal[:, SL["tmp"]:SL["tmp"] + 1], op=ALU.max)
            return v.tensor_copy(stg3[:, it % 2, :], fC[:, :])
        w = [f"s1sg{it}"]
        if it >= 2:
            w += [f"s1p{it-2}"]
        P.op("vector", emit_s1b, w, signal=f"s1v{it}")
        dma_s(f"s1p{it}", xre(prodd)[:, it, :], stg3[:, it % 2, :], waits=[f"s1v{it}"],
              ch=f"fO{it % 2}")

    for it in range(IT):
        w = [f"rawsp1_{IT-1}"]
        if it >= 2:
            w += [f"s15v{it-2}"]
        else:
            w += [f"s1v{IT-1}"]
        dma_s(f"s15i{it}", stg2[:, it % 2, :], xre(uraw)[:, it, :], waits=w,
              ch=f"fI{it % 2}")

        def emit_s15(v, it=it):
            v.tensor_scalar(fA[:, :], stg2[:, it % 2, :], scal[:, SL["r2"]:SL["r2"] + 1],
                            RC, op0=ALU.mult, op1=ALU.add)
            return v.tensor_scalar(stg2[:, it % 2, :], fA[:, :], -RC,
                                   scal[:, SL["s2"]:SL["s2"] + 1],
                                   op0=ALU.add, op1=ALU.mult)
        P.op("vector", emit_s15, [f"s15i{it}", "m2_bc"], signal=f"s15v{it}")
        dma_s(f"s15o{it}", xre(uqd)[:, it, :], stg2[:, it % 2, :], waits=[f"s15v{it}"],
              ch=f"fO{it % 2}")

    mbarrier(3, "m4", SL["m4a"], False, [f"s1v{IT-1}"])
    mb_bcast("m4", [(SL["s4"], 1), (SL["r4"], 2)])

    for it in range(IT):
        w = [f"s1p{IT-2}", f"s1p{IT-1}"]
        if it >= 2:
            w += [f"s2v{it-2}"]
        else:
            w += [f"s1v{IT-1}"]
        dma_s(f"s2i{it}", stg3[:, it % 2, :], xre(prodd)[:, it, :], waits=w,
              ch=f"fI{it % 2}")
        w2 = [f"s15o{IT-2}", f"s15o{IT-1}"]
        if it >= 2:
            w2 += [f"s2v{it-2}"]
        else:
            w2 += [f"s15v{IT-1}"]
        dma_s(f"s2u{it}", stg2[:, it % 2, :], xre(uqd)[:, it, :], waits=w2,
              ch=f"fJ{it % 2}")

        def emit_s2(v, it=it):
            v.tensor_scalar(fA[:, :], stg3[:, it % 2, :], scal[:, SL["r4"]:SL["r4"] + 1],
                            RC, op0=ALU.mult, op1=ALU.add)
            v.tensor_scalar(fA[:, :], fA[:, :], -RC, scal[:, SL["s4"]:SL["s4"] + 1],
                            op0=ALU.add, op1=ALU.mult)
            v.tensor_tensor(fB[:, :], fA[:, :], stg2[:, it % 2, :], op=ALU.mult)
            vreduce(v, scal[:, SL["tmp"]:SL["tmp"] + 1], fB[:, :],
                    axis=AX.X, op=ALU.max, apply_absolute_value=True)
            v.tensor_tensor(scal[:, SL["m5a"]:SL["m5a"] + 1],
                            scal[:, SL["m5a"]:SL["m5a"] + 1],
                            scal[:, SL["tmp"]:SL["tmp"] + 1], op=ALU.max)
            return v.tensor_copy(stg1[:, it % 2, :], fB[:, :])
        w3 = [f"s2i{it}", f"s2u{it}", "m4_bc", f"s15v{IT-1}"]
        if it >= 2:
            w3 += [f"s2o{it-2}"]
        P.op("vector", emit_s2, w3, signal=f"s2v{it}")
        dma_s(f"s2o{it}", xre(qsd)[:, it, :], stg1[:, it % 2, :], waits=[f"s2v{it}"],
              ch=f"fO{it % 2}")

    mbarrier(4, "m5", SL["m5a"], False, [f"s2v{IT-1}"])
    mb_bcast("m5", [(SL["s5"], 1), (SL["r5"], 2)])

    def bank8(idx):
        tensors = [psQ, psQ, psK, psK, psC, psC, psX, psX]
        return tensors[idx][:, 512 * (idx % 2):512 * (idx % 2) + 512]

    for tg in range(TG):
        for it in range(IT):
            gi = tg * IT + it
            w = [f"s2o{IT-2}", f"s2o{IT-1}"]
            if gi >= 2:
                w += [f"s3v{gi-2}"]
            else:
                w += [f"s2v{IT-1}"]
            dma_s(f"s3i{gi}", stg1[:, it % 2, 0:512],
                  xre(qsd)[:, it, 512 * tg:512 * tg + 512], waits=w,
                  ch=f"fI{gi % 2}")
            ww = [f"dq_st{ndq-2}", f"dq_st{ndq-1}"]
            if gi >= 2:
                ww += [f"dn_it{(gi-2)//IT}_{(gi-2)%IT}"]
            dma_s(f"s3w{gi}", wtd[:, it % 2, :], wd16d[:, 128 * it:128 * it + 128],
                  waits=ww, tr=True, ch=f"fJ{gi % 2}")

            def emit_s3(v, it=it, gi=gi, tg=tg):
                v.tensor_scalar(fA[:, 0:512], stg1[:, it % 2, 0:512],
                                scal[:, SL["r5"]:SL["r5"] + 1], RC,
                                op0=ALU.mult, op1=ALU.add)
                return v.tensor_scalar(ot[:, it, :], fA[:, 0:512], -RC,
                                       scal[:, SL["s5"]:SL["s5"] + 1],
                                       op0=ALU.add, op1=ALU.mult)
            w2 = [f"s3i{gi}", "m5_bc"]
            if tg >= 1:
                w2 += [f"dn_it{tg-1}_{IT-1}"]
            P.op("vector", emit_s3, w2, signal=f"s3v{gi}")

        for it in range(IT):
            gi = tg * IT + it

            def emit_dn(t, tg=tg, it=it, gi=gi):
                last = None
                for q in range(4):
                    for ch in range(2):
                        last = t.matmul(bank8(2 * q + ch),
                                        ot[:, it, 128 * q:128 * q + 128],
                                        wtd[:, it % 2, 512 * ch:512 * ch + 512],
                                        start=(it == 0), stop=(it == IT - 1))
                return last
            w = [f"s3v{gi}", f"s3w{gi}"]
            if it == 0:
                if tg == 0:
                    w += [f"ug_ev{2 * IT * TC - 1}", f"cx_ev{2 * B * CH - 1}",
                          f"at_ex{n_groups-1}"]
                else:
                    w += [f"dn_ev{4*(tg-1)+q}" for q in range(4)]
            P.op("tensor", emit_dn, w, signal=f"dn_it{tg}_{it}")

        for q in range(4):
            tt = 4 * tg + q

            def emit_dnev(a, q=q, tt=tt):
                a.activation(dnsc[:, tt % 2, 0:512], bank8(2 * q), AF.Copy)
                return a.activation(dnsc[:, tt % 2, 512:1024], bank8(2 * q + 1), AF.Copy)
            w = [f"dn_it{tg}_{IT-1}"]
            if tt >= 2:
                w += [f"o_y{tt-2}"]
            P.op("scalar", emit_dnev, w, signal=f"dn_ev{tt}")

            w = ["xq_spill"] + ([f"o_res{tt-2}"] if tt >= 2 else [])
            dma_s(f"yr{tt}", yrl[:, tt % 2, :], xre(yd)[:, tt, :], waits=w,
                  ch=f"yr{tt % 2}")

            st0 = 2 * (tt % 2)

            def emit_ores(v, tt=tt, st0=st0):
                v.tensor_tensor(dnsc[:, tt % 2, :], dnsc[:, tt % 2, :],
                                yrl[:, tt % 2, :], op=ALU.add)
                return v.reduce_sum(lnst[:, st0:st0 + 1], dnsc[:, tt % 2, :], axis=AX.X)
            P.op("vector", emit_ores, [f"dn_ev{tt}", f"yr{tt}"], signal=f"o_res{tt}")

            def emit_osq(a, tt=tt, st0=st0):
                return a.activation(lnsq[:, :], dnsc[:, tt % 2, :], AF.Square,
                                    accum_out=lnst[:, st0 + 1:st0 + 2])
            P.op("scalar", emit_osq, [f"o_res{tt}"], signal=f"o_sq{tt}")

            def emit_ovar(v, tt=tt, st0=st0):
                m = lnst[:, st0:st0 + 1]
                s2 = lnst[:, st0 + 1:st0 + 2]
                v.tensor_scalar(m, m, 1.0 / D, None, op0=ALU.mult)
                v.tensor_tensor(scal[:, SL["tmp"]:SL["tmp"] + 1], m, m, op=ALU.mult)
                v.tensor_scalar(s2, s2, 1.0 / D, None, op0=ALU.mult)
                v.tensor_tensor(s2, s2, scal[:, SL["tmp"]:SL["tmp"] + 1],
                                op=ALU.subtract)
                return v.tensor_scalar(s2, s2, 1e-5, None, op0=ALU.add)
            P.op("vector", emit_ovar, [f"o_sq{tt}"], signal=f"o_var{tt}")

            def emit_ors(a, tt=tt, st0=st0):
                return a.activation(lnst[:, st0 + 1:st0 + 2], lnst[:, st0 + 1:st0 + 2],
                                    AF.Sqrt)
            P.op("scalar", emit_ors, [f"o_var{tt}"], signal=f"o_rs{tt}")

            def emit_oy(v, tt=tt, st0=st0):
                vrecip(v, lnst[:, st0 + 1:st0 + 2], lnst[:, st0 + 1:st0 + 2])
                return v.tensor_scalar(outb[:, tt % 2, :], dnsc[:, tt % 2, :],
                                       lnst[:, st0:st0 + 1], lnst[:, st0 + 1:st0 + 2],
                                       op0=ALU.subtract, op1=ALU.mult)
            w = [f"o_rs{tt}"] + ([f"ost{tt-2}"] if tt >= 2 else [])
            P.op("vector", emit_oy, w, signal=f"o_y{tt}")
            dma_s(f"ost{tt}", xre(out)[:, tt, :], outb[:, tt % 2, :], waits=[f"o_y{tt}"],
                  ch=f"os{tt % 2}")

    P.op("sync", lambda s: s.nop(), [f"ost{TT-2}", f"ost{TT-1}"], signal=None)

    chans = ["vself", "ini", "cx", "cT", "sp", "ytq", "wgG", "wgU", "rwG", "rwU",
             "xm0", "xm1", "dqL0", "dqL1", "dqS0", "dqS1", "fI0", "fI1",
             "fJ0", "fJ1", "fO0", "fO1", "yr0", "yr1", "os0", "os1",
             "gW", "gM", "g0", "g1", "cc",
             "c_tensor", "c_scalar", "c_vector", "c_gpsimd", "c_sync"]
    import contextlib
    with contextlib.ExitStack() as stack:
        sems = {k: stack.enter_context(nc.semaphore(k)) for k in chans}
        vself["sem"] = sems["vself"]
        block = stack.enter_context(nc.Block())
        engines = {}

        def reg(name):
            def f(e):
                engines[name] = e
                run(name)
            return f
        run = P.build_and_emit(sems, engines)
        block.sync(reg("sync"))
        block.gpsimd(reg("gpsimd"))
        block.tensor(reg("tensor"))
        block.scalar(reg("scalar"))
        block.vector(reg("vector"))
    return nc


def _in_maps(cfg, inputs):
    st = np.asarray(inputs["L0_attn_states"], dtype=np.float32)
    B, S, D = st.shape
    flat = np.ascontiguousarray(st.reshape(B * S, D))
    xt = np.ascontiguousarray(flat.T)
    wq = np.asarray(inputs["L0_attn_Wq"], np.float32)
    wk = np.asarray(inputs["L0_attn_Wk"], np.float32)
    wv = np.asarray(inputs["L0_attn_Wv"], np.float32)
    wo = np.ascontiguousarray(np.asarray(inputs["L0_attn_Wo"], np.float32))
    wgn = np.ascontiguousarray(np.asarray(inputs["L0_ffn_w_gate"], np.float32))
    wun = np.ascontiguousarray(np.asarray(inputs["L0_ffn_w_up"], np.float32))
    wdn = np.ascontiguousarray(np.asarray(inputs["L0_ffn_w_down"], np.float32))
    tri = np.triu(np.ones((128, 128), np.float32))
    maskc = ((tri - 1.0) * 30000.0).astype(np.float16)
    identc = np.eye(128, dtype=np.float16)
    TM = cfg.TM
    maps = []
    for c in range(8):
        maps.append(dict(
            xt=xt,
            xm=np.ascontiguousarray(flat[TM * c:TM * (c + 1)]),
            wq=np.ascontiguousarray(wq[:, 128 * c:128 * c + 128]),
            wk=np.ascontiguousarray(wk[:, 128 * c:128 * c + 128]),
            wv=np.ascontiguousarray(wv[:, 128 * c:128 * c + 128]),
            wo=wo, wgn=wgn, wun=wun, wdn=wdn,
            maskc=maskc, identc=identc,
        ))
    return maps


def kernel(**inputs):
    st = np.asarray(inputs["L0_attn_states"])
    B, S, D = st.shape
    cfg = CFG(S=S)
    nc = build_nc(cfg)
    maps = _in_maps(cfg, inputs)
    res = run_bass_kernel_spmd(nc, maps, core_ids=list(range(8)))
    outp = np.concatenate([np.asarray(r["out"]) for r in res.results], axis=0)
    return outp.reshape(B, S, D).astype(np.float32)


# version 03
# speedup vs baseline: 1.0783x; 1.0783x over previous
import os
os.environ.setdefault("TRN_ALLOW_ICI_ALL_TO_ALL", "1")

import numpy as np
import concourse.bass as bass
import concourse.mybir as mybir
from concourse.bass_utils import run_bass_kernel_spmd

F32 = mybir.dt.float32
F16 = mybir.dt.float16
AF = mybir.ActivationFunctionType
ALU = mybir.AluOpType
AX = mybir.AxisListType

RC = 12582912.0
IQA = 1.0 / 32767.0
IQW = 1.0 / 7.0


class CFG:
    def __init__(self, S=4096, debug=False):
        self.debug = debug
        self.B = 4
        self.S = S
        self.D = 1024
        self.I = 3072
        self.TM = self.B * S // 8
        self.CH = S // 512
        self.TT = self.TM // 128
        self.TG = self.TM // 512
        self.IT = self.I // 128
        self.FT = self.D // 128
        self.TC = self.TM // 512


class Prog:
    ENGINES = ("sync", "gpsimd", "tensor", "scalar", "vector")

    def __init__(self):
        self.ops = []

    def op(self, engine, emit, waits=(), signal=None, amount=1, sem=None):
        self.ops.append((engine, emit, tuple(waits), signal, amount, sem))

    def build_and_emit(self, sems, engines):
        counts = {k: 0 for k in sems}
        table = {}
        for engine, emit, waits, signal, amount, semk in self.ops:
            if signal is not None:
                k = semk if semk is not None else "c_" + engine
                counts[k] += amount
                assert signal not in table, f"dup event {signal}"
                table[signal] = (k, counts[k])
        for engine, emit, waits, signal, amount, semk in self.ops:
            for w in waits:
                assert w in table, f"missing event {w}"
        watermark = {e: {k: 0 for k in sems} for e in self.ENGINES}

        def run(engine):
            eng_obj = engines[engine]
            for e2, emit, waits, signal, amount, semk in self.ops:
                if e2 != engine:
                    continue
                need = {}
                for w in waits:
                    k, v = table[w]
                    need[k] = max(need.get(k, 0), v)
                for k, v in need.items():
                    if watermark[engine][k] < v:
                        eng_obj.wait_ge(sems[k], v)
                        watermark[engine][k] = v
                inst = emit(eng_obj)
                if signal is not None:
                    k = semk if semk is not None else "c_" + engine
                    assert inst is not None, f"event {signal} has no instruction"
                    inst.then_inc(sems[k], amount)
        return run


def mem_map(cfg):
    M, o = {}, 17408

    def al(name, nbytes, at=None):
        nonlocal o
        if at is not None:
            M[name] = at
            return at
        M[name] = o
        o += (nbytes + 31) // 32 * 32
        return M[name]

    al("ident", 256); al("maskb", 256)
    al("wq16", cfg.FT * 256); al("wk16", cfg.FT * 256); al("wv16", cfg.FT * 256)
    al("wo16", cfg.FT * 2048)
    al("scal", 256)
    al("dqs", 128); al("dqr", 128)
    al("bcrow", 512); al("bcones", 512)
    al("lnsq", 4096)
    al("lnst", 64)
    al("lnst2", 64)
    al("dqz", 4096)
    al("dqw32", 2 * 4096)
    al("dq16", 2 * 2048)
    P1 = o
    al("xtb", 2 * cfg.FT * 1024)
    al("qt", 2 * cfg.S * 2)
    al("ktp0", 2 * cfg.S * 2)
    al("ktp1", 2 * cfg.S * 2)
    al("vtt", 2 * (cfg.S // 128) * 264)
    al("expst", 6 * 2048)
    al("ctxq", 2 * (cfg.S // 128) * 256)
    p1_end = o
    al("ctxT", 0, at=P1)
    al("xq16", 0, at=P1)
    ph2 = P1 + max(cfg.FT * cfg.TM * 2, cfg.TT * 2048)
    al("y32", cfg.TT * 4096, at=ph2); ph2 += cfg.TT * 4096
    al("xmst", 0, at=ph2); ph2 += 2 * 4096
    al("lnsc", 0, at=ph2); ph2 += 2 * 4096
    al("ytq", 0, at=P1)
    al("ot", 0, at=P1)
    base3 = P1 + max(cfg.FT * cfg.TM * 2, cfg.IT * 1024)
    al("wtg", 0, at=base3); base3 += cfg.FT * 6144
    al("wtd", 2 * 2048, at=base3); base3 += 2 * 2048
    al("stg1", 0, at=base3); base3 += 2 * cfg.TM * 2
    al("fA", 0, at=base3); base3 += 2 * cfg.TM * 4
    dead0 = base3
    al("stg2", 0, at=base3); base3 += 2 * cfg.TM * 2
    al("stg3", 0, at=base3); base3 += 2 * cfg.TM * 2
    al("fB", 0, at=base3); base3 += 2 * cfg.TM * 4
    al("fC", 0, at=base3); base3 += cfg.TM * 4
    al("fD", 0, at=M["fC"])
    o3 = dead0
    al("yrl", 0, at=o3); o3 += 2 * 4096
    al("outb", 0, at=o3); o3 += 2 * 4096
    al("dnsc", 0, at=o3); o3 += 4 * 4096
    M["_END"] = max(p1_end, ph2, base3, o3)
    return M


def build_nc(cfg):
    nc = bass.Bass("TRN2", target_bir_lowering=False,
                   detect_race_conditions=False)
    B, S, D, I = cfg.B, cfg.S, cfg.D, cfg.I
    TM, CH, TT, TG, IT, FT, TC = (cfg.TM, cfg.CH, cfg.TT, cfg.TG, cfg.IT,
                                  cfg.FT, cfg.TC)
    M = mem_map(cfg)
    assert M["_END"] <= 229344, M["_END"]

    xt = nc.dram_tensor("xt", [D, B * S], F32, kind="ExternalInput").ap()
    xm = nc.dram_tensor("xm", [TM, D], F32, kind="ExternalInput").ap()
    wq = nc.dram_tensor("wq", [D, 128], F32, kind="ExternalInput").ap()
    wk = nc.dram_tensor("wk", [D, 128], F32, kind="ExternalInput").ap()
    wv = nc.dram_tensor("wv", [D, 128], F32, kind="ExternalInput").ap()
    wo = nc.dram_tensor("wo", [D, D], F32, kind="ExternalInput").ap()
    wgn = nc.dram_tensor("wgn", [I, D], F32, kind="ExternalInput").ap()
    wun = nc.dram_tensor("wun", [I, D], F32, kind="ExternalInput").ap()
    wdn = nc.dram_tensor("wdn", [D, I], F32, kind="ExternalInput").ap()
    maskc = nc.dram_tensor("maskc", [128, 128], F16, kind="ExternalInput").ap()
    identc = nc.dram_tensor("identc", [128, 128], F16, kind="ExternalInput").ap()
    out = nc.dram_tensor("out", [TM, D], F32, kind="ExternalOutput").ap()

    dbg = "ExternalOutput" if cfg.debug else "Internal"
    a2a_i = nc.dram_tensor("a2a_i", [8, TM, 128], F16)
    a2a_o = nc.dram_tensor("a2a_o", [8, TM, 128], F16)
    mxi = [nc.dram_tensor(f"mxi{k}", [1, 1], F32) for k in range(5)]
    mxo = [nc.dram_tensor(f"mxo{k}", [1, 1], F32) for k in range(5)]
    mxp = nc.dram_tensor("mxp", [128], F32)
    wg16d = nc.dram_tensor("wg16d", [I, D], F16, kind=dbg)
    wu16d = nc.dram_tensor("wu16d", [I, D], F16, kind=dbg)
    wd16d = nc.dram_tensor("wd16d", [D, I], F16, kind=dbg)
    graw = nc.dram_tensor("graw", [I, TM], F16, kind=dbg)
    uraw = nc.dram_tensor("uraw", [I, TM], F16, kind=dbg)
    uqd = nc.dram_tensor("uqd", [I, TM], F16, kind=dbg)
    prodd = nc.dram_tensor("prodd", [I, TM], F16, kind=dbg)
    qsd = nc.dram_tensor("qsd", [I, TM], F16, kind=dbg)
    xqd = nc.dram_tensor("xqd", [TM, D], F16, kind=dbg)
    yd = nc.dram_tensor("yd", [TM, D], F32, kind=dbg)

    def t16(name, *shape, key=None):
        return nc.alloc_sbuf_tensor_at(name, [128, *shape], F16, offset=M[key or name])

    def t32(name, *shape, key=None):
        return nc.alloc_sbuf_tensor_at(name, [128, *shape], F32, offset=M[key or name])

    ident = t16("ident", 128)
    maskb = t16("maskb", 128)
    wq16 = t16("wq16", FT, 128)
    wk16 = t16("wk16", FT, 128)
    wv16 = t16("wv16", FT, 128)
    wo16 = t16("wo16", FT, 1024)
    scal = t32("scal", 64)
    dqs = t32("dqs", 32)
    dqr = t32("dqr", 32)
    bcrow = nc.alloc_sbuf_tensor_at("bcrow", [1, 128], F32, offset=M["bcrow"])
    bcones = nc.alloc_sbuf_tensor_at("bcones", [1, 128], F32, offset=M["bcones"])
    xtb = t16("xtb", 2, FT, 512)
    qt = t16("qt", 2, S)
    ktp0 = t16("ktp0", 2, S)
    ktp1 = t16("ktp1", 2, S)
    vtt = t16("vtt", 2, S // 128, 132)
    expst = t16("expst", 6, 1024)
    ctxq = t16("ctxq", 2, S // 128, 128)
    dqz = t32("dqz", 1024)
    dqw32 = t32("dqw32", 2, 1024)
    dq16 = t16("dq16", 2, 1024)
    ctxT = t16("ctxT", FT, TM)
    xq16 = t16("xq16", TT, 1024)
    y32 = t32("y32", TT, 1024)
    xmst = t32("xmst", 2, 1024)
    lnsc = t32("lnsc", 2, 1024)
    lnsq = t32("lnsq", 1024)
    lnst = t32("lnst", 16)
    lnst2 = t32("lnst2", 16)
    ytq = t16("ytq", FT, TM)
    ot = t16("ot", IT, 512)
    wtg = t16("wtg", FT, 3072)
    wtd = t16("wtd", 2, 1024)
    stg1 = t16("stg1", 2, TM)
    stg2 = t16("stg2", 2, TM)
    stg3 = t16("stg3", 2, TM)
    fA = t32("fA", 2, TM)
    fB = t32("fB", 2, TM)
    fC = t32("fC", TM)
    fD = t32("fD", TM)
    yrl = t32("yrl", 2, 1024)
    outb = t32("outb", 2, 1024)
    dnsc = t32("dnsc", 4, 1024)

    psQ = nc.alloc_psum_tensor("psQ", [128, 1024], F32)
    psK = nc.alloc_psum_tensor("psK", [128, 1024], F32)
    psC = nc.alloc_psum_tensor("psC", [128, 1024], F32)
    psX = nc.alloc_psum_tensor("psX", [128, 1024], F32)

    SL = dict(r1=0, s1=1, r3=2, s3=3, c13=4, r2=5, s2=6, r4=7, s4=8, r5=9,
              s5=10, row=16, m1a=24, m3a=25, m2a=26, m4a=27, m5a=28, tmp=29)

    P = Prog()
    vself = {"n": 0, "sem": None}

    def vsync(v, inst):
        vself["n"] += 1
        inst.then_inc(vself["sem"], 1)
        v.wait_ge(vself["sem"], vself["n"])

    def vreduce(v, out, in_, **kw):
        inst = v.tensor_reduce(out, in_, **kw)
        vsync(v, inst)
        return inst

    def vrecip(v, out, in_):
        inst = v.reciprocal(out, in_)
        vsync(v, inst)
        return inst

    def dma_g(name, o_, i_, waits=(), ch="gW"):
        P.op("gpsimd", lambda g, o_=o_, i_=i_: g.dma_start(out=o_, in_=i_),
             waits, signal=name, amount=16, sem=ch)

    def dma_s(name, o_, i_, waits=(), tr=False, ch="ini"):
        if tr:
            P.op("sync", lambda s, o_=o_, i_=i_: s.dma_start(o_, i_, transpose=True),
                 waits, signal=name, amount=16, sem=ch)
        else:
            P.op("sync", lambda s, o_=o_, i_=i_: s.dma_start(out=o_, in_=i_),
                 waits, signal=name, amount=16, sem=ch)

    def xre(dram_ap, p=128):
        return dram_ap.rearrange("(a p) n -> p a n", p=p)

    dma_s("ld_mask", maskb[:, :], maskc)
    dma_s("ld_ident", ident[:, :], identc)
    dma_g("ld_wq", wq16[:, :, :], xre(wq))
    dma_g("ld_wk", wk16[:, :, :], xre(wk))
    dma_g("ld_wv", wv16[:, :, :], xre(wv))
    dma_g("ld_wo", wo16[:, :, :], xre(wo))

    def v_init(v):
        v.memset(bcones[:, :], 1.0)
        v.memset(ktp0[64:128, :, :], 0.0)
        v.memset(ktp1[0:64, :, :], 0.0)
        v.memset(vtt[:, :, :, 64:65], 1.0)
        v.memset(vtt[:, :, :, 129:130], 1.0)
        return v.memset(scal[:, SL["m1a"]:SL["m5a"] + 1], 0.0)
    P.op("vector", v_init, (), signal="v_init")

    nchunks = B * CH
    for i in range(nchunks):
        b, c = divmod(i, CH)
        w = [f"pj_mm{i-2}"] if i >= 2 else []
        dma_g(f"ld_xt{i}", xtb[:, i % 2, :, :],
              xre(xt)[:, :, b * S + 512 * c: b * S + 512 * c + 512], waits=w,
              ch=f"g{i % 2}")

    ndq = IT + IT + 3 * (D // 128)

    def dq_src(i):
        if i < IT:
            return xre(wgn)[:, i, :], xre(wg16d)[:, i, :]
        if i < 2 * IT:
            return xre(wun)[:, i - IT, :], xre(wu16d)[:, i - IT, :]
        r, cc = divmod(i - 2 * IT, 3)
        return (xre(wdn)[:, r, 1024 * cc:1024 * cc + 1024],
                xre(wd16d)[:, r, 1024 * cc:1024 * cc + 1024])

    def emit_dq_ld(i):
        w = [f"dq_done{i-2}"] if i >= 2 else []
        dma_s(f"dq_ld{i}", dqw32[:, i % 2, :], dq_src(i)[0], waits=w,
              ch=f"dqL{i % 2}")

    def emit_dq_st(i):
        dma_s(f"dq_st{i}", dq_src(i)[1], dq16[:, i % 2, :], waits=[f"dq_done{i}"],
              ch=f"dqS{i % 2}")

    dq_queue = []
    for i in range(ndq):
        def dq_ops(v, i=i):
            wt = dqw32[:, i % 2, :]
            for hh in range(2):
                vreduce(v, dqs[:, 16 * hh:16 * hh + 16].unsqueeze(2),
                        wt[:, 512 * hh:512 * hh + 512].rearrange("p (a n) -> p a n", n=32),
                        axis=AX.X, op=ALU.max, apply_absolute_value=True)
            vsync(v, v.tensor_scalar(dqs[:, :], dqs[:, :], IQW, 1e-12,
                                     op0=ALU.mult, op1=ALU.max))
            vrecip(v, dqr[:, :], dqs[:, :])
            v.tensor_tensor(dqz[:, :].rearrange("p (a n) -> p a n", n=32),
                            wt.rearrange("p (a n) -> p a n", n=32),
                            dqr[:, :].unsqueeze(2).broadcast_to([128, 32, 32]),
                            op=ALU.mult)
            v.tensor_scalar(dqz[:, :], dqz[:, :], RC, None, op0=ALU.add)
            return v.scalar_tensor_tensor(
                dq16[:, i % 2, :].rearrange("p (a n) -> p a n", n=32),
                dqz[:, :].rearrange("p (a n) -> p a n", n=32), -RC,
                dqs[:, :].unsqueeze(2).broadcast_to([128, 32, 32]),
                op0=ALU.add, op1=ALU.mult)
        wts = [f"dq_ld{i}"] + ([f"dq_st{i-2}"] if i >= 2 else [])
        dq_queue.append((dq_ops, wts, f"dq_done{i}"))

    dq_state = {"pop": 0, "ld": 0}

    def pop_dq(n):
        for _ in range(n):
            if dq_state["pop"] >= ndq:
                return
            i = dq_state["pop"]
            dq_state["pop"] += 1
            while dq_state["ld"] <= min(i + 2, ndq - 1):
                emit_dq_ld(dq_state["ld"])
                dq_state["ld"] += 1
            fn, w, sig = dq_queue[i]
            P.op("vector", fn, w, signal=sig)
            emit_dq_st(i)

    def av_targets(c, kt):
        return range(max(0, kt - 4 * c), 4)

    def ctx_off(m):
        return 65 * m if m < 7 else 512

    nsc = 0
    for b in range(B):
        for c in range(CH):
            i = b * CH + c

            def emit_pj(t, i=i, b=b, c=c):
                qb = psQ[:, 512 * (i % 2):512 * (i % 2) + 512]
                kb = psK[:, 512 * (i % 2):512 * (i % 2) + 512]
                vb = psC[:, 512 * (i % 2):512 * (i % 2) + 512]
                for ft in range(FT):
                    t.matmul(qb, wq16[:, ft, :], xtb[:, i % 2, ft, :],
                             start=(ft == 0), stop=(ft == FT - 1))
                for ft in range(FT):
                    t.matmul(kb, wk16[:, ft, :], xtb[:, i % 2, ft, :],
                             start=(ft == 0), stop=(ft == FT - 1))
                last = None
                for q in range(4):
                    for ft in range(FT):
                        last = t.matmul(vb[:, 128 * q:128 * q + 128],
                                        xtb[:, i % 2, ft, 128 * q:128 * q + 128],
                                        wv16[:, ft, :],
                                        start=(ft == 0), stop=(ft == FT - 1))
                return last
            w = [f"ld_xt{i}"]
            if i == 0:
                w += ["ld_wq", "ld_wk", "ld_wv", "ld_wo", "v_init",
                      "ld_mask", "ld_ident"]
            if i >= 2:
                w += [f"pj_ev{i-2}"]
            if c == 0 and b >= 1:
                w += [f"cx_ev{2 * b * CH - 1}"]
            P.op("tensor", emit_pj, w, signal=f"pj_mm{i}")

            def emit_ev(a, i=i, b=b, c=c):
                qb = psQ[:, 512 * (i % 2):512 * (i % 2) + 512]
                kb = psK[:, 512 * (i % 2):512 * (i % 2) + 512]
                vb = psC[:, 512 * (i % 2):512 * (i % 2) + 512]
                a.activation(qt[:, b % 2, 512 * c:512 * c + 512], qb, AF.Tanh)
                a.activation(ktp0[0:64, b % 2, 512 * c:512 * c + 512], kb[0:64, :], AF.Tanh)
                a.activation(ktp1[64:128, b % 2, 512 * c:512 * c + 512], kb[64:128, :], AF.Tanh)
                a.activation(vtt[:, b % 2, 4 * c:4 * c + 4, 0:64],
                             vb.rearrange("p (a n) -> p a n", n=128)[:, :, 0:64], AF.Tanh)
                return a.activation(vtt[:, b % 2, 4 * c:4 * c + 4, 65:129],
                                    vb.rearrange("p (a n) -> p a n", n=128)[:, :, 64:128],
                                    AF.Tanh)
            P.op("scalar", emit_ev, [f"pj_mm{i}"], signal=f"pj_ev{i}")

        def avbank(q):
            return (psX if q < 2 else psC)[:, 512 * (q % 2):512 * (q % 2) + 512]

        for c in range(CH):
            uidx = b * CH + c

            for h in range(2):
                blk = 2 * uidx + h
                pend = []

                def emit_av(t, ph, pg, pglob, b=b, c=c):
                    last = None
                    for ktl in (2 * pg, 2 * pg + 1):
                        slot = pglob % 6
                        half = ktl % 2
                        for q in av_targets(c, ktl):
                            last = t.matmul(
                                avbank(q)[:, 0:65],
                                expst[:, slot, 512 * half + 128 * q:512 * half + 128 * q + 128],
                                vtt[:, b % 2, ktl, 65 * ph:65 * ph + 65],
                                start=(ktl == 0), stop=(ktl == 4 * c + q))
                    return last

                for g in range(2 * c + 2):
                    glob = nsc

                    def emit_sc(t, h=h, g=g, glob=glob, b=b, c=c):
                        last = None
                        for ktl in (2 * g, 2 * g + 1):
                            bank = (psQ if glob % 2 == 0 else psK)[
                                :, 512 * (ktl % 2):512 * (ktl % 2) + 512]
                            ktp = ktp0 if h == 0 else ktp1
                            diag = ktl >= 4 * c
                            last = t.matmul(bank, ktp[:, b % 2, 128 * ktl:128 * ktl + 128],
                                            qt[:, b % 2, 512 * c:512 * c + 512],
                                            start=True, stop=not diag)
                            if diag:
                                r = ktl - 4 * c
                                last = t.matmul(bank[:, 128 * r:128 * r + 128],
                                                ident[:, :], maskb[:, :],
                                                start=False, stop=True)
                        return last
                    w = []
                    if g == 0 and blk == 0:
                        w += [f"pj_ev{CH - 1}"]
                    elif g == 0 and h == 0:
                        w += [f"pj_ev{(b + 1) * CH - 1}"]
                    if glob >= 2:
                        w += [f"at_ex{glob-2}"]
                    P.op("tensor", emit_sc, w, signal=f"at_sc{glob}")

                    def emit_ex(a, glob=glob):
                        return a.activation(expst[:, glob % 6, :],
                                            (psQ if glob % 2 == 0 else psK)[:, 0:1024],
                                            AF.Exp, scale=0.125)
                    we = [f"at_sc{glob}"]
                    if glob >= 6:
                        we += [f"at_av{glob-6}"]
                    P.op("scalar", emit_ex, we, signal=f"at_ex{glob}")

                    pend.append((h, g, glob))
                    nsc += 1
                    if len(pend) >= 2:
                        ph_, pg_, pglob_ = pend.pop(0)
                        wv = [f"at_ex{pglob_}"]
                        if pg_ == 0 and blk > 0:
                            wv += [f"cx_ev{blk-1}"]
                        P.op("tensor",
                             lambda t, ph=ph_, pg=pg_, pglob=pglob_, fn=emit_av: fn(t, ph, pg, pglob),
                             wv, signal=f"at_av{pglob_}")
                ph_, pg_, pglob_ = pend.pop(0)
                wv = [f"at_ex{pglob_}"]
                if pg_ == 0 and blk > 0:
                    wv += [f"cx_ev{blk-1}"]
                P.op("tensor",
                     lambda t, ph=ph_, pg=pg_, pglob=pglob_, fn=emit_av: fn(t, ph, pg, pglob),
                     wv, signal=f"at_av{pglob_}")

                def emit_cx(v, b=b, c=c, h=h):
                    last = None
                    for q in range(4):
                        j = 4 * c + q
                        bank = avbank(q)
                        vrecip(v, scal[:, SL["tmp"]:SL["tmp"] + 1],
                               bank[:, 64:65])
                        last = v.tensor_scalar(
                            ctxq[:, b % 2, j, 64 * h:64 * h + 64],
                            bank[:, 0:64],
                            scal[:, SL["tmp"]:SL["tmp"] + 1], None, op0=ALU.mult)
                    return last
                wcx = [f"at_av{nsc-1}"]
                if b >= 2:
                    wcx += [f"cxst{b-2}b"]
                P.op("vector", emit_cx, wcx, signal=f"cx_ev{blk}")
                pop_dq(1)

        dma_s(f"cxst{b}a", a2a_i[2 * b, :, :].rearrange("(a p) d -> p a d", p=128),
              ctxq[:, b % 2, 0:S // 256, :], waits=[f"cx_ev{2 * (b + 1) * CH - 1}"],
              ch="cx")
        dma_s(f"cxst{b}b", a2a_i[2 * b + 1, :, :].rearrange("(a p) d -> p a d", p=128),
              ctxq[:, b % 2, S // 256:S // 128, :], waits=[f"cx_ev{2 * (b + 1) * CH - 1}"],
              ch="cx")

    n_groups = nsc

    P.op("gpsimd", lambda g: g.collective_compute(
        "AllToAll", ALU.bypass, replica_groups=[list(range(8))],
        ins=[a2a_i.ap().opt()], outs=[a2a_o.ap().opt()]),
        [f"cxst{B-1}b"], signal="a2a", amount=1, sem="cc")

    for j in range(FT):
        dma_s(f"ctxT{j}", ctxT[:, j, :], a2a_o[j, :, :], waits=["a2a"], tr=True,
              ch="cT")

    pop_dq(len(dq_queue))

    for tt in range(TT):
        w = ([f"ln_c{tt-2}"] if tt >= 2 else [])
        dma_s(f"xm{tt}", xmst[:, tt % 2, :], xre(xm)[:, tt, :], waits=w,
              ch=f"xm{tt % 2}")

    for tt in range(TT):
        def emit_wo(t, tt=tt):
            ps = psQ if tt % 2 == 0 else psK
            last = None
            for ch in range(2):
                bank = ps[:, 512 * ch:512 * ch + 512]
                for j in range(FT):
                    last = t.matmul(bank, ctxT[:, j, 128 * tt:128 * tt + 128],
                                    wo16[:, j, 512 * ch:512 * ch + 512],
                                    start=(j == 0), stop=(j == FT - 1))
            return last
        if tt == 0:
            w = [f"ctxT{FT-1}", f"at_ex{n_groups-1}"]
        elif tt == 1:
            w = [f"ctxT{FT-1}"]
        else:
            w = [f"wo_ev{tt-2}"]
        P.op("tensor", emit_wo, w, signal=f"wo_mm{tt}")

        def emit_woev(a, tt=tt):
            ps = psQ if tt % 2 == 0 else psK
            a.activation(lnsc[:, tt % 2, 0:512], ps[:, 0:512], AF.Tanh)
            return a.activation(lnsc[:, tt % 2, 512:1024], ps[:, 512:1024], AF.Tanh)
        wev = [f"wo_mm{tt}"] + ([f"ln_c{tt-2}"] if tt >= 2 else [])
        P.op("scalar", emit_woev, wev, signal=f"wo_ev{tt}")

        def emit_res(v, tt=tt):
            v.tensor_tensor(y32[:, tt, :], lnsc[:, tt % 2, :], xmst[:, tt % 2, :],
                            op=ALU.add)
            return v.reduce_sum(lnst[:, tt:tt + 1], y32[:, tt, :], axis=AX.X)
        P.op("vector", emit_res, [f"wo_ev{tt}", f"xm{tt}"], signal=f"ln_c{tt}")

    for tt in range(TT):
        def emit_sq(a, tt=tt):
            return a.activation(lnsq[:, :], y32[:, tt, :], AF.Square,
                                accum_out=lnst2[:, tt:tt + 1])
        P.op("scalar", emit_sq, [f"ln_c{tt}"], signal=f"ln_sq{tt}")
    for tt in range(TT):
        def emit_var(v, tt=tt):
            m = lnst[:, tt:tt + 1]
            s2 = lnst2[:, tt:tt + 1]
            v.tensor_scalar(m, m, 1.0 / D, None, op0=ALU.mult)
            v.tensor_tensor(scal[:, SL["tmp"]:SL["tmp"] + 1], m, m, op=ALU.mult)
            v.tensor_scalar(s2, s2, 1.0 / D, None, op0=ALU.mult)
            v.tensor_tensor(s2, s2, scal[:, SL["tmp"]:SL["tmp"] + 1],
                            op=ALU.subtract)
            return v.tensor_scalar(s2, s2, 1e-5, None, op0=ALU.add)
        P.op("vector", emit_var, [f"ln_sq{tt}"], signal=f"ln_v{tt}")
    for tt in range(TT):
        def emit_rs(a, tt=tt):
            return a.activation(lnst2[:, tt:tt + 1], lnst2[:, tt:tt + 1],
                                AF.Sqrt)
        P.op("scalar", emit_rs, [f"ln_v{tt}"], signal=f"ln_rs{tt}")
    for tt in range(TT):
        def emit_y(v, tt=tt):
            vrecip(v, lnst2[:, tt:tt + 1], lnst2[:, tt:tt + 1])
            v.tensor_scalar(y32[:, tt, :], y32[:, tt, :], lnst[:, tt:tt + 1],
                            lnst2[:, tt:tt + 1], op0=ALU.subtract, op1=ALU.mult)
            vreduce(v, scal[:, SL["tmp"]:SL["tmp"] + 1], y32[:, tt, :],
                    axis=AX.X, op=ALU.max, apply_absolute_value=True)
            return v.tensor_tensor(scal[:, SL["m1a"]:SL["m1a"] + 1],
                                   scal[:, SL["m1a"]:SL["m1a"] + 1],
                                   scal[:, SL["tmp"]:SL["tmp"] + 1], op=ALU.max)
        P.op("vector", emit_y, [f"ln_rs{tt}"], signal=f"ln_y{tt}")

    dma_s("y_spill", xre(yd), y32[:, :, :], waits=[f"ln_y{TT-1}"], ch="sp")

    def mbarrier(idx, name, acc_slot, extra_c13, waits):
        dma_g(f"{name}_p1", mxp.ap().rearrange("(p n) -> p n", n=1),
              scal[:, acc_slot:acc_slot + 1], waits=waits, ch="gM")
        dma_g(f"{name}_p2", bcrow[:, :],
              mxp.ap().rearrange("(n p) -> n p", p=128), waits=[f"{name}_p1"],
              ch="gM")

        def vpar(v):
            vreduce(v, scal[0:1, SL["tmp"] + 1:SL["tmp"] + 2],
                    bcrow[:, :], axis=AX.X, op=ALU.max,
                    apply_absolute_value=True)
            return v.nop()
        P.op("vector", vpar, [f"{name}_p2"], signal=f"{name}_par")
        dma_g(f"{name}_o", mxi[idx][:, :], scal[0:1, SL["tmp"] + 1:SL["tmp"] + 2],
              waits=[f"{name}_par"], ch="gM")
        P.op("gpsimd", lambda g: g.collective_compute(
            "AllReduce", ALU.max, replica_groups=[list(range(8))],
            ins=[mxi[idx].ap().opt()], outs=[mxo[idx].ap().opt()]),
            [f"{name}_o"], signal=f"{name}_ar", amount=1, sem="cc")
        dma_g(f"{name}_i", scal[0:1, SL["row"]:SL["row"] + 1], mxo[idx][:, :],
              waits=[f"{name}_ar"], ch="gM")

        def vscale(v):
            r = SL["row"]
            vsync(v, v.tensor_scalar(scal[0:1, r + 1:r + 2], scal[0:1, r:r + 1],
                                     IQA, 1e-12, op0=ALU.mult, op1=ALU.max))
            vrecip(v, scal[0:1, r + 2:r + 3], scal[0:1, r + 1:r + 2])
            if extra_c13:
                return v.tensor_scalar(scal[0:1, r + 3:r + 4], scal[0:1, r + 1:r + 2],
                                       1.0 / 65536.0, None, op0=ALU.mult)
            return v.nop()
        P.op("vector", vscale, [f"{name}_i"], signal=f"{name}_vs")

    def mb_bcast(name, slots):
        def bcmm(t):
            last = None
            for n, (dst, ro) in enumerate(slots):
                last = t.matmul(psX[:, 960 + n:960 + n + 1], bcones[:, :],
                                scal[0:1, SL["row"] + ro:SL["row"] + ro + 1],
                                start=True, stop=True)
            return last
        P.op("tensor", bcmm, [f"{name}_vs"], signal=f"{name}_bm")

        def bccp(v):
            last = None
            for n, (dst, ro) in enumerate(slots):
                last = v.tensor_copy(scal[:, dst:dst + 1],
                                     psX[:, 960 + n:960 + n + 1])
            vsync(v, last)
            return v.nop()
        P.op("vector", bccp, [f"{name}_bm"], signal=f"{name}_bc")

    mbarrier(0, "m1", SL["m1a"], False, [f"ln_y{TT-1}"])
    mb_bcast("m1", [(SL["s1"], 1), (SL["r1"], 2)])

    for tt in range(TT):
        def emit_xq(v, tt=tt):
            v.tensor_scalar(lnsq[:, :], y32[:, tt, :], scal[:, SL["r1"]:SL["r1"] + 1],
                            RC, op0=ALU.mult, op1=ALU.add)
            return v.tensor_scalar(xq16[:, tt, :], lnsq[:, :], -RC,
                                   scal[:, SL["s1"]:SL["s1"] + 1],
                                   op0=ALU.add, op1=ALU.mult)
        P.op("vector", emit_xq, ["m1_bc"] if tt == 0 else [], signal=f"xq{tt}")
    dma_s("xq_spill", xre(xqd), xq16[:, :, :], waits=[f"xq{TT-1}"], ch="sp")

    for ft in range(FT):
        dma_s(f"ytq{ft}", ytq[:, ft, :], xqd[:, 128 * ft:128 * ft + 128],
              waits=["xq_spill"], tr=True, ch="ytq")
    for ft in range(FT):
        dma_s(f"wtgG{ft}", wtg[:, ft, :], wg16d[:, 128 * ft:128 * ft + 128],
              waits=[f"dq_st{IT-2}", f"dq_st{IT-1}"], tr=True, ch="wgG")

    def bank4(n):
        return (psQ if (n % 4) < 2 else psK)[:, 512 * (n % 2):512 * (n % 2) + 512]

    nmm = 0
    for wsel in range(2):
        for it in range(IT):
            for tc in range(TC):
                def emit_ug(t, it=it, tc=tc, nmm=nmm):
                    bk = bank4(nmm)
                    last = None
                    for ft in range(FT):
                        last = t.matmul(bk, wtg[:, ft, 128 * it:128 * it + 128],
                                        ytq[:, ft, 512 * tc:512 * tc + 512],
                                        start=(ft == 0), stop=(ft == FT - 1))
                    return last
                w = []
                if nmm == 0:
                    w = [f"ytq{FT-1}", f"wtgG{FT-1}", f"wo_ev{TT-1}"]
                if wsel == 1 and it == 0 and tc == 0:
                    w += [f"wtgU{FT-1}"]
                if nmm >= 4:
                    w += [f"ug_ev{nmm-4}"]
                P.op("tensor", emit_ug, w, signal=f"ug_mm{nmm}")

                def emit_ugev(a, wsel=wsel, it=it, tc=tc, nmm=nmm):
                    stg = stg1 if wsel == 0 else stg2
                    return a.activation(stg[:, it % 2, 512 * tc:512 * tc + 512],
                                        bank4(nmm), AF.Copy)
                wev = [f"ug_mm{nmm}"]
                if nmm >= 2 * TC:
                    wev += [f"ug_mx{nmm - 2 * TC}", f"rawsp{(nmm - 2 * TC) // (IT * TC)}_{((nmm - 2 * TC) // TC) % IT}"]
                P.op("scalar", emit_ugev, wev, signal=f"ug_ev{nmm}")

                def emit_ugmx(v, wsel=wsel, it=it, tc=tc):
                    stg = stg1 if wsel == 0 else stg2
                    acc = SL["m3a"] if wsel == 0 else SL["m2a"]
                    vreduce(v, scal[:, SL["tmp"]:SL["tmp"] + 1],
                            stg[:, it % 2, 512 * tc:512 * tc + 512],
                            axis=AX.X, op=ALU.max, apply_absolute_value=True)
                    return v.tensor_tensor(scal[:, acc:acc + 1], scal[:, acc:acc + 1],
                                           scal[:, SL["tmp"]:SL["tmp"] + 1], op=ALU.max)
                P.op("vector", emit_ugmx, [f"ug_ev{nmm}"], signal=f"ug_mx{nmm}")
                nmm += 1
            stgd = graw if wsel == 0 else uraw
            stgs = stg1 if wsel == 0 else stg2
            dma_s(f"rawsp{wsel}_{it}", xre(stgd)[:, it, :], stgs[:, it % 2, :],
                  waits=[f"ug_ev{nmm-1}"], ch="rwG" if wsel == 0 else "rwU")
        if wsel == 0:
            for ft in range(FT):
                dma_s(f"wtgU{ft}", wtg[:, ft, :], wu16d[:, 128 * ft:128 * ft + 128],
                      waits=[f"ug_mm{IT * TC - 1}", f"dq_st{2*IT-2}", f"dq_st{2*IT-1}"],
                      tr=True, ch="wgU")

    mbarrier(1, "m3", SL["m3a"], True, [f"ug_mx{IT * TC - 1}"])
    mb_bcast("m3", [(SL["s3"], 1), (SL["r3"], 2), (SL["c13"], 3)])
    mbarrier(2, "m2", SL["m2a"], False, [f"ug_mx{2 * IT * TC - 1}"])
    mb_bcast("m2", [(SL["s2"], 1), (SL["r2"], 2)])

    def emit_s1a(v, it):
        p = it % 2
        v.tensor_scalar(fA[:, p, :], stg1[:, p, :], scal[:, SL["r3"]:SL["r3"] + 1],
                        RC, op0=ALU.mult, op1=ALU.add)
        return v.tensor_scalar(fA[:, p, :], fA[:, p, :], -RC, None, op0=ALU.add)

    def emit_sig(a, it):
        p = it % 2
        return a.activation(fB[:, p, :], fA[:, p, :], AF.Sigmoid,
                            scale=scal[:, SL["s3"]:SL["s3"] + 1])

    def emit_s1b(v, it):
        p = it % 2
        v.tensor_scalar(fB[:, p, :], fB[:, p, :], 65536.0, RC, op0=ALU.mult, op1=ALU.add)
        v.tensor_scalar(fB[:, p, :], fB[:, p, :], -RC, 65535.0, op0=ALU.add, op1=ALU.min)
        v.scalar_tensor_tensor(fC[:, :], fA[:, p, :], scal[:, SL["c13"]:SL["c13"] + 1],
                               fB[:, p, :], op0=ALU.mult, op1=ALU.mult)
        vreduce(v, scal[:, SL["tmp"]:SL["tmp"] + 1], fC[:, :],
                axis=AX.X, op=ALU.max, apply_absolute_value=True)
        v.tensor_tensor(scal[:, SL["m4a"]:SL["m4a"] + 1],
                        scal[:, SL["m4a"]:SL["m4a"] + 1],
                        scal[:, SL["tmp"]:SL["tmp"] + 1], op=ALU.max)
        return v.tensor_copy(stg3[:, p, :], fC[:, :])

    def emit_s15(v, it):
        p = it % 2
        v.tensor_scalar(fD[:, :], stg2[:, p, :], scal[:, SL["r2"]:SL["r2"] + 1],
                        RC, op0=ALU.mult, op1=ALU.add)
        return v.tensor_scalar(stg2[:, p, :], fD[:, :], -RC,
                               scal[:, SL["s2"]:SL["s2"] + 1],
                               op0=ALU.add, op1=ALU.mult)

    def s1_dma(it):
        w = [f"rawsp0_{IT-1}"] + ([f"s1a{it-2}"] if it >= 2 else [])
        dma_s(f"s1g{it}", stg1[:, it % 2, :], xre(graw)[:, it, :], waits=w,
              ch=f"fI{it % 2}")

    def s15_dma(it):
        w = [f"rawsp1_{IT-1}"] + ([f"s15v{it-2}", f"s15o{it-2}"] if it >= 2 else [])
        dma_s(f"s15i{it}", stg2[:, it % 2, :], xre(uraw)[:, it, :], waits=w,
              ch=f"fJ{it % 2}")

    for it in range(IT):
        if it >= 2:
            dma_s(f"s1p{it-2}", xre(prodd)[:, it - 2, :],
                  stg3[:, (it - 2) % 2, :], waits=[f"s1v{it-2}"], ch=f"fO{(it-2) % 2}")
            dma_s(f"s15o{it-2}", xre(uqd)[:, it - 2, :], stg2[:, (it - 2) % 2, :],
                  waits=[f"s15v{it-2}"], ch=f"fO{(it-2) % 2}")
        s1_dma(it)
        s15_dma(it)
        w = [f"s1g{it}", "m3_bc"] + ([f"s1sg{it-2}"] if it >= 2 else [])
        P.op("vector", lambda v, it=it: emit_s1a(v, it), w, signal=f"s1a{it}")
        ws = [f"s1a{it}"] + ([f"s1v{it-2}"] if it >= 2 else [])
        P.op("scalar", lambda a, it=it: emit_sig(a, it), ws, signal=f"s1sg{it}")
        if it >= 1:
            pit = it - 1
            w = [f"s1sg{pit}"] + ([f"s1p{pit-2}"] if pit >= 2 else [])
            P.op("vector", lambda v, it=pit: emit_s1b(v, it), w, signal=f"s1v{pit}")
            w = [f"s15i{pit}", "m2_bc"] + ([f"s15o{pit-2}"] if pit >= 2 else [])
            P.op("vector", lambda v, it=pit: emit_s15(v, it), w, signal=f"s15v{pit}")
    pit = IT - 1
    P.op("vector", lambda v, it=pit: emit_s1b(v, it),
         [f"s1sg{pit}", f"s1p{pit-2}"], signal=f"s1v{pit}")
    P.op("vector", lambda v, it=pit: emit_s15(v, it),
         [f"s15i{pit}", "m2_bc", f"s15o{pit-2}"], signal=f"s15v{pit}")
    for it in (IT - 2, IT - 1):
        dma_s(f"s1p{it}", xre(prodd)[:, it, :], stg3[:, it % 2, :],
              waits=[f"s1v{it}"], ch=f"fO{it % 2}")
        dma_s(f"s15o{it}", xre(uqd)[:, it, :], stg2[:, it % 2, :],
              waits=[f"s15v{it}"], ch=f"fO{it % 2}")

    mbarrier(3, "m4", SL["m4a"], False, [f"s1v{IT-1}"])
    mb_bcast("m4", [(SL["s4"], 1), (SL["r4"], 2)])

    for it in range(IT):
        p = it % 2
        w = [f"s1p{IT-2}", f"s1p{IT-1}"]
        if it >= 2:
            w += [f"s2v{it-2}"]
        else:
            w += [f"s1v{IT-1}"]
        dma_s(f"s2i{it}", stg3[:, p, :], xre(prodd)[:, it, :], waits=w,
              ch=f"fI{p}")
        w2 = [f"s15o{IT-2}", f"s15o{IT-1}"]
        if it >= 2:
            w2 += [f"s2v{it-2}"]
        else:
            w2 += [f"s15v{IT-1}"]
        dma_s(f"s2u{it}", stg2[:, p, :], xre(uqd)[:, it, :], waits=w2,
              ch=f"fJ{p}")

        def emit_s2(v, it=it, p=p):
            v.tensor_scalar(fA[:, p, :], stg3[:, p, :], scal[:, SL["r4"]:SL["r4"] + 1],
                            RC, op0=ALU.mult, op1=ALU.add)
            v.tensor_scalar(fA[:, p, :], fA[:, p, :], -RC, scal[:, SL["s4"]:SL["s4"] + 1],
                            op0=ALU.add, op1=ALU.mult)
            v.tensor_tensor(fB[:, p, :], fA[:, p, :], stg2[:, p, :], op=ALU.mult)
            vreduce(v, scal[:, SL["tmp"]:SL["tmp"] + 1], fB[:, p, :],
                    axis=AX.X, op=ALU.max, apply_absolute_value=True)
            v.tensor_tensor(scal[:, SL["m5a"]:SL["m5a"] + 1],
                            scal[:, SL["m5a"]:SL["m5a"] + 1],
                            scal[:, SL["tmp"]:SL["tmp"] + 1], op=ALU.max)
            return v.tensor_copy(stg1[:, p, :], fB[:, p, :])
        w3 = [f"s2i{it}", f"s2u{it}", "m4_bc", f"s15v{IT-1}"]
        if it >= 2:
            w3 += [f"s2o{it-2}"]
        P.op("vector", emit_s2, w3, signal=f"s2v{it}")
        dma_s(f"s2o{it}", xre(qsd)[:, it, :], stg1[:, p, :], waits=[f"s2v{it}"],
              ch=f"fO{p}")

    mbarrier(4, "m5", SL["m5a"], False, [f"s2v{IT-1}"])
    mb_bcast("m5", [(SL["s5"], 1), (SL["r5"], 2)])

    def bank8(idx):
        tensors = [psQ, psQ, psK, psK, psC, psC, psX, psX]
        return tensors[idx][:, 512 * (idx % 2):512 * (idx % 2) + 512]

    for tg in range(TG):
        for it in range(IT):
            gi = tg * IT + it
            w = [f"s2o{IT-2}", f"s2o{IT-1}"]
            if gi >= 2:
                w += [f"s3v{gi-2}"]
            else:
                w += [f"s2v{IT-1}"]
            dma_s(f"s3i{gi}", stg1[:, it % 2, 0:512],
                  xre(qsd)[:, it, 512 * tg:512 * tg + 512], waits=w,
                  ch=f"fI{gi % 2}")
            ww = [f"dq_st{ndq-2}", f"dq_st{ndq-1}"]
            if gi >= 2:
                ww += [f"dn_it{(gi-2)//IT}_{(gi-2)%IT}"]
            dma_s(f"s3w{gi}", wtd[:, it % 2, :], wd16d[:, 128 * it:128 * it + 128],
                  waits=ww, tr=True, ch=f"fJ{gi % 2}")

            def emit_s3(v, it=it, gi=gi, tg=tg):
                v.tensor_scalar(fA[:, gi % 2, 0:512], stg1[:, it % 2, 0:512],
                                scal[:, SL["r5"]:SL["r5"] + 1], RC,
                                op0=ALU.mult, op1=ALU.add)
                return v.tensor_scalar(ot[:, it, :], fA[:, gi % 2, 0:512], -RC,
                                       scal[:, SL["s5"]:SL["s5"] + 1],
                                       op0=ALU.add, op1=ALU.mult)
            w2 = [f"s3i{gi}", "m5_bc"]
            if tg >= 1:
                w2 += [f"dn_it{tg-1}_{IT-1}"]
            P.op("vector", emit_s3, w2, signal=f"s3v{gi}")

        for it in range(IT):
            gi = tg * IT + it

            def emit_dn(t, tg=tg, it=it, gi=gi):
                last = None
                for q in range(4):
                    for ch in range(2):
                        last = t.matmul(bank8(2 * q + ch),
                                        ot[:, it, 128 * q:128 * q + 128],
                                        wtd[:, it % 2, 512 * ch:512 * ch + 512],
                                        start=(it == 0), stop=(it == IT - 1))
                return last
            w = [f"s3v{gi}", f"s3w{gi}"]
            if it == 0:
                if tg == 0:
                    w += [f"ug_ev{2 * IT * TC - 1}", f"cx_ev{2 * B * CH - 1}",
                          f"at_ex{n_groups-1}"]
                else:
                    w += [f"dn_ev{4*(tg-1)+q}" for q in range(4)]
            P.op("tensor", emit_dn, w, signal=f"dn_it{tg}_{it}")

        for q in range(4):
            tt = 4 * tg + q

            def emit_dnev(a, q=q, tt=tt):
                a.activation(dnsc[:, tt % 4, 0:512], bank8(2 * q), AF.Copy)
                return a.activation(dnsc[:, tt % 4, 512:1024], bank8(2 * q + 1), AF.Copy)
            w = [f"dn_it{tg}_{IT-1}"]
            if tt >= 4:
                w += [f"o_y{tt-4}"]
            P.op("scalar", emit_dnev, w, signal=f"dn_ev{tt}")

            w = ["xq_spill"] + ([f"o_res{tt-2}"] if tt >= 2 else [])
            dma_s(f"yr{tt}", yrl[:, tt % 2, :], xre(yd)[:, tt, :], waits=w,
                  ch=f"yr{tt % 2}")
        for q in range(4):
            tt = 4 * tg + q

            def emit_ores(v, tt=tt):
                v.tensor_tensor(dnsc[:, tt % 4, :], dnsc[:, tt % 4, :],
                                yrl[:, tt % 2, :], op=ALU.add)
                return v.reduce_sum(lnst[:, 2 * (tt % 4):2 * (tt % 4) + 1],
                                    dnsc[:, tt % 4, :], axis=AX.X)
            P.op("vector", emit_ores, [f"dn_ev{tt}", f"yr{tt}"], signal=f"o_res{tt}")
        for q in range(4):
            tt = 4 * tg + q

            def emit_osq(a, tt=tt):
                return a.activation(lnsq[:, :], dnsc[:, tt % 4, :], AF.Square,
                                    accum_out=lnst[:, 2 * (tt % 4) + 1:2 * (tt % 4) + 2])
            P.op("scalar", emit_osq, [f"o_res{tt}"], signal=f"o_sq{tt}")
        for q in range(4):
            tt = 4 * tg + q

            def emit_ovar(v, tt=tt):
                st0 = 2 * (tt % 4)
                m = lnst[:, st0:st0 + 1]
                s2 = lnst[:, st0 + 1:st0 + 2]
                v.tensor_scalar(m, m, 1.0 / D, None, op0=ALU.mult)
                v.tensor_tensor(scal[:, SL["tmp"]:SL["tmp"] + 1], m, m, op=ALU.mult)
                v.tensor_scalar(s2, s2, 1.0 / D, None, op0=ALU.mult)
                v.tensor_tensor(s2, s2, scal[:, SL["tmp"]:SL["tmp"] + 1],
                                op=ALU.subtract)
                return v.tensor_scalar(s2, s2, 1e-5, None, op0=ALU.add)
            P.op("vector", emit_ovar, [f"o_sq{tt}"], signal=f"o_var{tt}")
        for q in range(4):
            tt = 4 * tg + q

            def emit_ors(a, tt=tt):
                st0 = 2 * (tt % 4)
                return a.activation(lnst[:, st0 + 1:st0 + 2], lnst[:, st0 + 1:st0 + 2],
                                    AF.Sqrt)
            P.op("scalar", emit_ors, [f"o_var{tt}"], signal=f"o_rs{tt}")
        for q in range(4):
            tt = 4 * tg + q

            def emit_oy(v, tt=tt):
                st0 = 2 * (tt % 4)
                vrecip(v, lnst[:, st0 + 1:st0 + 2], lnst[:, st0 + 1:st0 + 2])
                return v.tensor_scalar(outb[:, tt % 2, :], dnsc[:, tt % 4, :],
                                       lnst[:, st0:st0 + 1], lnst[:, st0 + 1:st0 + 2],
                                       op0=ALU.subtract, op1=ALU.mult)
            w = [f"o_rs{tt}"] + ([f"ost{tt-2}"] if tt >= 2 else [])
            P.op("vector", emit_oy, w, signal=f"o_y{tt}")
            dma_s(f"ost{tt}", xre(out)[:, tt, :], outb[:, tt % 2, :], waits=[f"o_y{tt}"],
                  ch=f"os{tt % 2}")

    P.op("sync", lambda s: s.nop(), [f"ost{TT-2}", f"ost{TT-1}"], signal=None)

    chans = ["vself", "ini", "cx", "cT", "sp", "ytq", "wgG", "wgU", "rwG", "rwU",
             "xm0", "xm1", "dqL0", "dqL1", "dqS0", "dqS1", "fI0", "fI1",
             "fJ0", "fJ1", "fO0", "fO1", "yr0", "yr1", "os0", "os1",
             "gW", "gM", "g0", "g1", "cc",
             "c_tensor", "c_scalar", "c_vector", "c_gpsimd", "c_sync"]
    import contextlib
    with contextlib.ExitStack() as stack:
        sems = {k: stack.enter_context(nc.semaphore(k)) for k in chans}
        vself["sem"] = sems["vself"]
        block = stack.enter_context(nc.Block())
        engines = {}

        def reg(name):
            def f(e):
                engines[name] = e
                run(name)
            return f
        run = P.build_and_emit(sems, engines)
        block.sync(reg("sync"))
        block.gpsimd(reg("gpsimd"))
        block.tensor(reg("tensor"))
        block.scalar(reg("scalar"))
        block.vector(reg("vector"))
    return nc


def _in_maps(cfg, inputs):
    st = np.asarray(inputs["L0_attn_states"], dtype=np.float32)
    B, S, D = st.shape
    flat = np.ascontiguousarray(st.reshape(B * S, D))
    xt = np.ascontiguousarray(flat.T)
    wq = np.asarray(inputs["L0_attn_Wq"], np.float32)
    wk = np.asarray(inputs["L0_attn_Wk"], np.float32)
    wv = np.asarray(inputs["L0_attn_Wv"], np.float32)
    wo = np.ascontiguousarray(np.asarray(inputs["L0_attn_Wo"], np.float32))
    wgn = np.ascontiguousarray(np.asarray(inputs["L0_ffn_w_gate"], np.float32))
    wun = np.ascontiguousarray(np.asarray(inputs["L0_ffn_w_up"], np.float32))
    wdn = np.ascontiguousarray(np.asarray(inputs["L0_ffn_w_down"], np.float32))
    tri = np.triu(np.ones((128, 128), np.float32))
    maskc = ((tri - 1.0) * 30000.0).astype(np.float16)
    identc = np.eye(128, dtype=np.float16)
    TM = cfg.TM
    maps = []
    for c in range(8):
        maps.append(dict(
            xt=xt,
            xm=np.ascontiguousarray(flat[TM * c:TM * (c + 1)]),
            wq=np.ascontiguousarray(wq[:, 128 * c:128 * c + 128]),
            wk=np.ascontiguousarray(wk[:, 128 * c:128 * c + 128]),
            wv=np.ascontiguousarray(wv[:, 128 * c:128 * c + 128]),
            wo=wo, wgn=wgn, wun=wun, wdn=wdn,
            maskc=maskc, identc=identc,
        ))
    return maps


def kernel(**inputs):
    st = np.asarray(inputs["L0_attn_states"])
    B, S, D = st.shape
    cfg = CFG(S=S)
    nc = build_nc(cfg)
    maps = _in_maps(cfg, inputs)
    res = run_bass_kernel_spmd(nc, maps, core_ids=list(range(8)))
    outp = np.concatenate([np.asarray(r["out"]) for r in res.results], axis=0)
    return outp.reshape(B, S, D).astype(np.float32)


# version 04
# speedup vs baseline: 1.0853x; 1.0065x over previous
import os
os.environ.setdefault("TRN_ALLOW_ICI_ALL_TO_ALL", "1")

import numpy as np
import concourse.bass as bass
import concourse.mybir as mybir
from concourse.bass_utils import run_bass_kernel_spmd

F32 = mybir.dt.float32
F16 = mybir.dt.float16
AF = mybir.ActivationFunctionType
ALU = mybir.AluOpType
AX = mybir.AxisListType

RC = 12582912.0
IQA = 1.0 / 32767.0
IQW = 1.0 / 7.0


class CFG:
    def __init__(self, S=4096, debug=False):
        self.debug = debug
        self.B = 4
        self.S = S
        self.D = 1024
        self.I = 3072
        self.TM = self.B * S // 8
        self.CH = S // 512
        self.TT = self.TM // 128
        self.TG = self.TM // 512
        self.IT = self.I // 128
        self.FT = self.D // 128
        self.TC = self.TM // 512


class Prog:
    ENGINES = ("sync", "gpsimd", "tensor", "scalar", "vector")

    def __init__(self):
        self.ops = []

    def op(self, engine, emit, waits=(), signal=None, amount=1, sem=None):
        self.ops.append((engine, emit, tuple(waits), signal, amount, sem))

    def build_and_emit(self, sems, engines):
        counts = {k: 0 for k in sems}
        table = {}
        for engine, emit, waits, signal, amount, semk in self.ops:
            if signal is not None:
                k = semk if semk is not None else "c_" + engine
                counts[k] += amount
                assert signal not in table, f"dup event {signal}"
                table[signal] = (k, counts[k])
        for engine, emit, waits, signal, amount, semk in self.ops:
            for w in waits:
                assert w in table, f"missing event {w}"
        watermark = {e: {k: 0 for k in sems} for e in self.ENGINES}

        def run(engine):
            eng_obj = engines[engine]
            for e2, emit, waits, signal, amount, semk in self.ops:
                if e2 != engine:
                    continue
                need = {}
                for w in waits:
                    k, v = table[w]
                    need[k] = max(need.get(k, 0), v)
                for k, v in need.items():
                    if watermark[engine][k] < v:
                        eng_obj.wait_ge(sems[k], v)
                        watermark[engine][k] = v
                inst = emit(eng_obj)
                if signal is not None:
                    k = semk if semk is not None else "c_" + engine
                    assert inst is not None, f"event {signal} has no instruction"
                    inst.then_inc(sems[k], amount)
        return run


def mem_map(cfg):
    M, o = {}, 17408

    def al(name, nbytes, at=None):
        nonlocal o
        if at is not None:
            M[name] = at
            return at
        M[name] = o
        o += (nbytes + 31) // 32 * 32
        return M[name]

    al("ident", 256); al("maskb", 256)
    al("wq16", cfg.FT * 256); al("wk16", cfg.FT * 256); al("wv16", cfg.FT * 256)
    al("wo16", cfg.FT * 2048)
    al("scal", 256)
    al("dqs", 128); al("dqr", 128)
    al("bcrow", 512); al("bcones", 512)
    al("lnsq", 4096)
    al("lnst", 64)
    al("lnst2", 64)
    al("dqz", 4096)
    al("dqw32", 2 * 4096)
    al("dq16", 2 * 2048)
    P1 = o
    al("xtb", 2 * cfg.FT * 1024)
    al("qt", 2 * cfg.S * 2)
    al("ktp0", 2 * cfg.S * 2)
    al("ktp1", 2 * cfg.S * 2)
    al("vtt", 2 * (cfg.S // 128) * 264)
    al("expst", 6 * 2048)
    al("ctxq", 2 * (cfg.S // 128) * 256)
    p1_end = o
    al("ctxT", 0, at=P1)
    al("xq16", 0, at=P1)
    ph2 = P1 + max(cfg.FT * cfg.TM * 2, cfg.TT * 2048)
    al("y32", cfg.TT * 4096, at=ph2); ph2 += cfg.TT * 4096
    al("xmst", 0, at=ph2); ph2 += 2 * 4096
    al("lnsc", 0, at=ph2); ph2 += 2 * 4096
    al("ytq", 0, at=P1)
    al("ot", 0, at=P1)
    base3 = P1 + max(cfg.FT * cfg.TM * 2, cfg.IT * 1024)
    al("wtg", 0, at=base3); base3 += cfg.FT * 6144
    al("wtd", 2 * 2048, at=base3); base3 += 2 * 2048
    al("stg1", 0, at=base3); base3 += 2 * cfg.TM * 2
    al("fA", 0, at=base3); base3 += 2 * cfg.TM * 4
    dead0 = base3
    al("stg2", 0, at=base3); base3 += 2 * cfg.TM * 2
    al("stg3", 0, at=base3); base3 += 2 * cfg.TM * 2
    al("fB", 0, at=base3); base3 += 2 * cfg.TM * 4
    al("fC", 0, at=base3); base3 += cfg.TM * 4
    al("fD", 0, at=M["fC"])
    o3 = dead0
    al("yrl", 0, at=o3); o3 += 2 * 4096
    al("outb", 0, at=o3); o3 += 2 * 4096
    al("dnsc", 0, at=o3); o3 += 4 * 4096
    M["_END"] = max(p1_end, ph2, base3, o3)
    return M


def build_nc(cfg):
    nc = bass.Bass("TRN2", target_bir_lowering=False,
                   detect_race_conditions=False)
    B, S, D, I = cfg.B, cfg.S, cfg.D, cfg.I
    TM, CH, TT, TG, IT, FT, TC = (cfg.TM, cfg.CH, cfg.TT, cfg.TG, cfg.IT,
                                  cfg.FT, cfg.TC)
    M = mem_map(cfg)
    assert M["_END"] <= 229344, M["_END"]

    xt = nc.dram_tensor("xt", [D, B * S], F32, kind="ExternalInput").ap()
    xm = nc.dram_tensor("xm", [TM, D], F32, kind="ExternalInput").ap()
    wq = nc.dram_tensor("wq", [D, 128], F32, kind="ExternalInput").ap()
    wk = nc.dram_tensor("wk", [D, 128], F32, kind="ExternalInput").ap()
    wv = nc.dram_tensor("wv", [D, 128], F32, kind="ExternalInput").ap()
    wo = nc.dram_tensor("wo", [D, D], F32, kind="ExternalInput").ap()
    wgn = nc.dram_tensor("wgn", [I, D], F32, kind="ExternalInput").ap()
    wun = nc.dram_tensor("wun", [I, D], F32, kind="ExternalInput").ap()
    wdn = nc.dram_tensor("wdn", [D, I], F32, kind="ExternalInput").ap()
    maskc = nc.dram_tensor("maskc", [128, 128], F16, kind="ExternalInput").ap()
    identc = nc.dram_tensor("identc", [128, 128], F16, kind="ExternalInput").ap()
    out = nc.dram_tensor("out", [TM, D], F32, kind="ExternalOutput").ap()

    dbg = "ExternalOutput" if cfg.debug else "Internal"
    a2a_i = nc.dram_tensor("a2a_i", [8, TM, 128], F16)
    a2a_o = nc.dram_tensor("a2a_o", [8, TM, 128], F16)
    mxi = [nc.dram_tensor(f"mxi{k}", [1, 1], F32) for k in range(5)]
    mxo = [nc.dram_tensor(f"mxo{k}", [1, 1], F32) for k in range(5)]
    mxp = nc.dram_tensor("mxp", [128], F32)
    wg16d = nc.dram_tensor("wg16d", [I, D], F16, kind=dbg)
    wu16d = nc.dram_tensor("wu16d", [I, D], F16, kind=dbg)
    wd16d = nc.dram_tensor("wd16d", [D, I], F16, kind=dbg)
    graw = nc.dram_tensor("graw", [I, TM], F16, kind=dbg)
    uraw = nc.dram_tensor("uraw", [I, TM], F16, kind=dbg)
    uqd = nc.dram_tensor("uqd", [I, TM], F16, kind=dbg)
    prodd = nc.dram_tensor("prodd", [I, TM], F16, kind=dbg)
    qsd = nc.dram_tensor("qsd", [I, TM], F16, kind=dbg)
    xqd = nc.dram_tensor("xqd", [TM, D], F16, kind=dbg)
    yd = nc.dram_tensor("yd", [TM, D], F32, kind=dbg)

    def t16(name, *shape, key=None):
        return nc.alloc_sbuf_tensor_at(name, [128, *shape], F16, offset=M[key or name])

    def t32(name, *shape, key=None):
        return nc.alloc_sbuf_tensor_at(name, [128, *shape], F32, offset=M[key or name])

    ident = t16("ident", 128)
    maskb = t16("maskb", 128)
    wq16 = t16("wq16", FT, 128)
    wk16 = t16("wk16", FT, 128)
    wv16 = t16("wv16", FT, 128)
    wo16 = t16("wo16", FT, 1024)
    scal = t32("scal", 64)
    dqs = t32("dqs", 32)
    dqr = t32("dqr", 32)
    bcrow = nc.alloc_sbuf_tensor_at("bcrow", [1, 128], F32, offset=M["bcrow"])
    bcones = nc.alloc_sbuf_tensor_at("bcones", [1, 128], F32, offset=M["bcones"])
    xtb = t16("xtb", 2, FT, 512)
    qt = t16("qt", 2, S)
    ktp0 = t16("ktp0", 2, S)
    ktp1 = t16("ktp1", 2, S)
    vtt = t16("vtt", 2, S // 128, 132)
    expst = t16("expst", 6, 1024)
    ctxq = t16("ctxq", 2, S // 128, 128)
    dqz = t32("dqz", 1024)
    dqw32 = t32("dqw32", 2, 1024)
    dq16 = t16("dq16", 2, 1024)
    ctxT = t16("ctxT", FT, TM)
    xq16 = t16("xq16", TT, 1024)
    y32 = t32("y32", TT, 1024)
    xmst = t32("xmst", 2, 1024)
    lnsc = t32("lnsc", 2, 1024)
    lnsq = t32("lnsq", 1024)
    lnst = t32("lnst", 16)
    lnst2 = t32("lnst2", 16)
    ytq = t16("ytq", FT, TM)
    ot = t16("ot", IT, 512)
    wtg = t16("wtg", FT, 3072)
    wtd = t16("wtd", 2, 1024)
    stg1 = t16("stg1", 2, TM)
    stg2 = t16("stg2", 2, TM)
    stg3 = t16("stg3", 2, TM)
    fA = t32("fA", 2, TM)
    fB = t32("fB", 2, TM)
    fC = t32("fC", TM)
    fD = t32("fD", TM)
    yrl = t32("yrl", 2, 1024)
    outb = t32("outb", 2, 1024)
    dnsc = t32("dnsc", 4, 1024)

    psQ = nc.alloc_psum_tensor("psQ", [128, 1024], F32)
    psK = nc.alloc_psum_tensor("psK", [128, 1024], F32)
    psC = nc.alloc_psum_tensor("psC", [128, 1024], F32)
    psX = nc.alloc_psum_tensor("psX", [128, 1024], F32)

    SL = dict(r1=0, s1=1, r3=2, s3=3, c13=4, r2=5, s2=6, r4=7, s4=8, r5=9,
              s5=10, row=16, m1a=24, m3a=25, m2a=26, m4a=27, m5a=28, tmp=29)

    P = Prog()
    vself = {"n": 0, "sem": None}

    def vsync(v, inst):
        vself["n"] += 1
        inst.then_inc(vself["sem"], 1)
        v.wait_ge(vself["sem"], vself["n"])

    def vreduce(v, out, in_, **kw):
        inst = v.tensor_reduce(out, in_, **kw)
        vsync(v, inst)
        return inst

    def vrecip(v, out, in_):
        inst = v.reciprocal(out, in_)
        vsync(v, inst)
        return inst

    def dma_g(name, o_, i_, waits=(), ch="gW"):
        P.op("gpsimd", lambda g, o_=o_, i_=i_: g.dma_start(out=o_, in_=i_),
             waits, signal=name, amount=16, sem=ch)

    def dma_s(name, o_, i_, waits=(), tr=False, ch="ini"):
        if tr:
            P.op("sync", lambda s, o_=o_, i_=i_: s.dma_start(o_, i_, transpose=True),
                 waits, signal=name, amount=16, sem=ch)
        else:
            P.op("sync", lambda s, o_=o_, i_=i_: s.dma_start(out=o_, in_=i_),
                 waits, signal=name, amount=16, sem=ch)

    def xre(dram_ap, p=128):
        return dram_ap.rearrange("(a p) n -> p a n", p=p)

    dma_s("ld_mask", maskb[:, :], maskc)
    dma_s("ld_ident", ident[:, :], identc)
    dma_g("ld_wq", wq16[:, :, :], xre(wq))
    dma_g("ld_wk", wk16[:, :, :], xre(wk))
    dma_g("ld_wv", wv16[:, :, :], xre(wv))
    dma_g("ld_wo", wo16[:, :, :], xre(wo))

    def v_init(v):
        v.memset(bcones[:, :], 1.0)
        v.memset(ktp0[64:128, :, :], 0.0)
        v.memset(ktp1[0:64, :, :], 0.0)
        v.memset(vtt[:, :, :, 64:65], 1.0)
        v.memset(vtt[:, :, :, 129:130], 1.0)
        return v.memset(scal[:, SL["m1a"]:SL["m5a"] + 1], 0.0)
    P.op("vector", v_init, (), signal="v_init")

    nchunks = B * CH
    for i in range(nchunks):
        b, c = divmod(i, CH)
        w = [f"pj_mm{i-2}"] if i >= 2 else []
        dma_g(f"ld_xt{i}", xtb[:, i % 2, :, :],
              xre(xt)[:, :, b * S + 512 * c: b * S + 512 * c + 512], waits=w,
              ch=f"g{i % 2}")

    ndq = IT + IT + 3 * (D // 128)

    def dq_src(i):
        if i < IT:
            return xre(wgn)[:, i, :], xre(wg16d)[:, i, :]
        if i < 2 * IT:
            return xre(wun)[:, i - IT, :], xre(wu16d)[:, i - IT, :]
        r, cc = divmod(i - 2 * IT, 3)
        return (xre(wdn)[:, r, 1024 * cc:1024 * cc + 1024],
                xre(wd16d)[:, r, 1024 * cc:1024 * cc + 1024])

    def emit_dq_ld(i):
        w = [f"dq_done{i-2}"] if i >= 2 else []
        dma_s(f"dq_ld{i}", dqw32[:, i % 2, :], dq_src(i)[0], waits=w,
              ch=f"dqL{i % 2}")

    def emit_dq_st(i):
        dma_s(f"dq_st{i}", dq_src(i)[1], dq16[:, i % 2, :], waits=[f"dq_done{i}"],
              ch=f"dqS{i % 2}")

    dq_queue = []
    for i in range(ndq):
        def dq_ops(v, i=i):
            wt = dqw32[:, i % 2, :]
            for hh in range(2):
                vreduce(v, dqs[:, 16 * hh:16 * hh + 16].unsqueeze(2),
                        wt[:, 512 * hh:512 * hh + 512].rearrange("p (a n) -> p a n", n=32),
                        axis=AX.X, op=ALU.max, apply_absolute_value=True)
            vsync(v, v.tensor_scalar(dqs[:, :], dqs[:, :], IQW, 1e-12,
                                     op0=ALU.mult, op1=ALU.max))
            vrecip(v, dqr[:, :], dqs[:, :])
            v.tensor_tensor(dqz[:, :].rearrange("p (a n) -> p a n", n=32),
                            wt.rearrange("p (a n) -> p a n", n=32),
                            dqr[:, :].unsqueeze(2).broadcast_to([128, 32, 32]),
                            op=ALU.mult)
            v.tensor_scalar(dqz[:, :], dqz[:, :], RC, None, op0=ALU.add)
            return v.scalar_tensor_tensor(
                dq16[:, i % 2, :].rearrange("p (a n) -> p a n", n=32),
                dqz[:, :].rearrange("p (a n) -> p a n", n=32), -RC,
                dqs[:, :].unsqueeze(2).broadcast_to([128, 32, 32]),
                op0=ALU.add, op1=ALU.mult)
        wts = [f"dq_ld{i}"] + ([f"dq_st{i-2}"] if i >= 2 else [])
        dq_queue.append((dq_ops, wts, f"dq_done{i}"))

    dq_state = {"pop": 0, "ld": 0}

    def pop_dq(n):
        for _ in range(n):
            if dq_state["pop"] >= ndq:
                return
            i = dq_state["pop"]
            dq_state["pop"] += 1
            while dq_state["ld"] <= min(i + 2, ndq - 1):
                emit_dq_ld(dq_state["ld"])
                dq_state["ld"] += 1
            fn, w, sig = dq_queue[i]
            P.op("vector", fn, w, signal=sig)
            emit_dq_st(i)

    def av_targets(c, kt):
        return range(max(0, kt - 4 * c), 4)

    def ctx_off(m):
        return 65 * m if m < 7 else 512

    nsc = 0
    for b in range(B):
        for c in range(CH):
            i = b * CH + c

            def emit_pj(t, i=i, b=b, c=c):
                qb = psQ[:, 512 * (i % 2):512 * (i % 2) + 512]
                kb = psK[:, 512 * (i % 2):512 * (i % 2) + 512]
                vb = psC[:, 512 * (i % 2):512 * (i % 2) + 512]
                for ft in range(FT):
                    t.matmul(qb, wq16[:, ft, :], xtb[:, i % 2, ft, :],
                             start=(ft == 0), stop=(ft == FT - 1))
                for ft in range(FT):
                    t.matmul(kb, wk16[:, ft, :], xtb[:, i % 2, ft, :],
                             start=(ft == 0), stop=(ft == FT - 1))
                last = None
                for q in range(4):
                    for ft in range(FT):
                        last = t.matmul(vb[:, 128 * q:128 * q + 128],
                                        xtb[:, i % 2, ft, 128 * q:128 * q + 128],
                                        wv16[:, ft, :],
                                        start=(ft == 0), stop=(ft == FT - 1))
                return last
            w = [f"ld_xt{i}"]
            if i == 0:
                w += ["ld_wq", "ld_wk", "ld_wv", "ld_wo", "v_init",
                      "ld_mask", "ld_ident"]
            if i >= 2:
                w += [f"pj_ev{i-2}"]
            if c == 0 and b >= 1:
                w += [f"cx_ev{2 * b * CH - 1}"]
            P.op("tensor", emit_pj, w, signal=f"pj_mm{i}")

            def emit_ev(a, i=i, b=b, c=c):
                qb = psQ[:, 512 * (i % 2):512 * (i % 2) + 512]
                kb = psK[:, 512 * (i % 2):512 * (i % 2) + 512]
                vb = psC[:, 512 * (i % 2):512 * (i % 2) + 512]
                a.activation(qt[:, b % 2, 512 * c:512 * c + 512], qb, AF.Tanh)
                a.activation(ktp0[0:64, b % 2, 512 * c:512 * c + 512], kb[0:64, :], AF.Tanh)
                a.activation(ktp1[64:128, b % 2, 512 * c:512 * c + 512], kb[64:128, :], AF.Tanh)
                a.activation(vtt[:, b % 2, 4 * c:4 * c + 4, 0:64],
                             vb.rearrange("p (a n) -> p a n", n=128)[:, :, 0:64], AF.Tanh)
                return a.activation(vtt[:, b % 2, 4 * c:4 * c + 4, 65:129],
                                    vb.rearrange("p (a n) -> p a n", n=128)[:, :, 64:128],
                                    AF.Tanh)
            P.op("scalar", emit_ev, [f"pj_mm{i}"], signal=f"pj_ev{i}")

        def avbank(q):
            return (psX if q < 2 else psC)[:, 512 * (q % 2):512 * (q % 2) + 512]

        for c in range(CH):
            uidx = b * CH + c

            for h in range(2):
                blk = 2 * uidx + h
                pend = []

                def emit_av(t, ph, pg, pglob, b=b, c=c):
                    last = None
                    for ktl in (2 * pg, 2 * pg + 1):
                        slot = pglob % 6
                        half = ktl % 2
                        for q in av_targets(c, ktl):
                            last = t.matmul(
                                avbank(q)[:, 0:65],
                                expst[:, slot, 512 * half + 128 * q:512 * half + 128 * q + 128],
                                vtt[:, b % 2, ktl, 65 * ph:65 * ph + 65],
                                start=(ktl == 0), stop=(ktl == 4 * c + q))
                    return last

                for g in range(2 * c + 2):
                    glob = nsc

                    def emit_sc(t, h=h, g=g, glob=glob, b=b, c=c):
                        last = None
                        for ktl in (2 * g, 2 * g + 1):
                            bank = (psQ if glob % 2 == 0 else psK)[
                                :, 512 * (ktl % 2):512 * (ktl % 2) + 512]
                            ktp = ktp0 if h == 0 else ktp1
                            diag = ktl >= 4 * c
                            last = t.matmul(bank, ktp[:, b % 2, 128 * ktl:128 * ktl + 128],
                                            qt[:, b % 2, 512 * c:512 * c + 512],
                                            start=True, stop=not diag)
                            if diag:
                                r = ktl - 4 * c
                                last = t.matmul(bank[:, 128 * r:128 * r + 128],
                                                ident[:, :], maskb[:, :],
                                                start=False, stop=True)
                        return last
                    w = []
                    if g == 0 and blk == 0:
                        w += [f"pj_ev{CH - 1}"]
                    elif g == 0 and h == 0:
                        w += [f"pj_ev{(b + 1) * CH - 1}"]
                    if glob >= 2:
                        w += [f"at_ex{glob-2}"]
                    P.op("tensor", emit_sc, w, signal=f"at_sc{glob}")

                    def emit_ex(a, glob=glob):
                        return a.activation(expst[:, glob % 6, :],
                                            (psQ if glob % 2 == 0 else psK)[:, 0:1024],
                                            AF.Exp, scale=0.125)
                    we = [f"at_sc{glob}"]
                    if glob >= 6:
                        we += [f"at_av{glob-6}"]
                    P.op("scalar", emit_ex, we, signal=f"at_ex{glob}")

                    pend.append((h, g, glob))
                    nsc += 1
                    if len(pend) >= 2:
                        ph_, pg_, pglob_ = pend.pop(0)
                        wv = [f"at_ex{pglob_}"]
                        if pg_ == 0 and blk > 0:
                            wv += [f"cx_ev{blk-1}"]
                        P.op("tensor",
                             lambda t, ph=ph_, pg=pg_, pglob=pglob_, fn=emit_av: fn(t, ph, pg, pglob),
                             wv, signal=f"at_av{pglob_}")
                ph_, pg_, pglob_ = pend.pop(0)
                wv = [f"at_ex{pglob_}"]
                if pg_ == 0 and blk > 0:
                    wv += [f"cx_ev{blk-1}"]
                P.op("tensor",
                     lambda t, ph=ph_, pg=pg_, pglob=pglob_, fn=emit_av: fn(t, ph, pg, pglob),
                     wv, signal=f"at_av{pglob_}")

                def emit_cx(v, b=b, c=c, h=h):
                    last = None
                    for q in range(4):
                        j = 4 * c + q
                        bank = avbank(q)
                        vrecip(v, scal[:, SL["tmp"]:SL["tmp"] + 1],
                               bank[:, 64:65])
                        last = v.tensor_scalar(
                            ctxq[:, b % 2, j, 64 * h:64 * h + 64],
                            bank[:, 0:64],
                            scal[:, SL["tmp"]:SL["tmp"] + 1], None, op0=ALU.mult)
                    return last
                wcx = [f"at_av{nsc-1}"]
                if b >= 2:
                    wcx += [f"cxst{b-2}b"]
                P.op("vector", emit_cx, wcx, signal=f"cx_ev{blk}")
                pop_dq(1)

        dma_s(f"cxst{b}a", a2a_i[2 * b, :, :].rearrange("(a p) d -> p a d", p=128),
              ctxq[:, b % 2, 0:S // 256, :], waits=[f"cx_ev{2 * (b + 1) * CH - 1}"],
              ch="cx")
        dma_s(f"cxst{b}b", a2a_i[2 * b + 1, :, :].rearrange("(a p) d -> p a d", p=128),
              ctxq[:, b % 2, S // 256:S // 128, :], waits=[f"cx_ev{2 * (b + 1) * CH - 1}"],
              ch="cx")

    n_groups = nsc

    P.op("gpsimd", lambda g: g.collective_compute(
        "AllToAll", ALU.bypass, replica_groups=[list(range(8))],
        ins=[a2a_i.ap().opt()], outs=[a2a_o.ap().opt()]),
        [f"cxst{B-1}b"], signal="a2a", amount=1, sem="cc")

    for j in range(FT):
        dma_s(f"ctxT{j}", ctxT[:, j, :], a2a_o[j, :, :], waits=["a2a"], tr=True,
              ch="cT")

    pop_dq(len(dq_queue))

    for tt in range(TT):
        w = ([f"ln_c{tt-2}"] if tt >= 2 else [])
        dma_s(f"xm{tt}", xmst[:, tt % 2, :], xre(xm)[:, tt, :], waits=w,
              ch=f"xm{tt % 2}")

    for tt in range(TT):
        def emit_wo(t, tt=tt):
            ps = psQ if tt % 2 == 0 else psK
            last = None
            for ch in range(2):
                bank = ps[:, 512 * ch:512 * ch + 512]
                for j in range(FT):
                    last = t.matmul(bank, ctxT[:, j, 128 * tt:128 * tt + 128],
                                    wo16[:, j, 512 * ch:512 * ch + 512],
                                    start=(j == 0), stop=(j == FT - 1))
            return last
        if tt == 0:
            w = [f"ctxT{FT-1}", f"at_ex{n_groups-1}"]
        elif tt == 1:
            w = [f"ctxT{FT-1}"]
        else:
            w = [f"wo_ev{tt-2}"]
        P.op("tensor", emit_wo, w, signal=f"wo_mm{tt}")

        def emit_woev(a, tt=tt):
            ps = psQ if tt % 2 == 0 else psK
            a.activation(lnsc[:, tt % 2, 0:512], ps[:, 0:512], AF.Tanh)
            return a.activation(lnsc[:, tt % 2, 512:1024], ps[:, 512:1024], AF.Tanh)
        wev = [f"wo_mm{tt}"] + ([f"ln_c{tt-2}"] if tt >= 2 else [])
        P.op("scalar", emit_woev, wev, signal=f"wo_ev{tt}")

        def emit_res(v, tt=tt):
            v.tensor_tensor(y32[:, tt, :], lnsc[:, tt % 2, :], xmst[:, tt % 2, :],
                            op=ALU.add)
            return v.reduce_sum(lnst[:, tt:tt + 1], y32[:, tt, :], axis=AX.X)
        P.op("vector", emit_res, [f"wo_ev{tt}", f"xm{tt}"], signal=f"ln_c{tt}")

    for tt in range(TT):
        def emit_sq(a, tt=tt):
            return a.activation(lnsq[:, :], y32[:, tt, :], AF.Square,
                                accum_out=lnst2[:, tt:tt + 1])
        P.op("scalar", emit_sq, [f"ln_c{tt}"], signal=f"ln_sq{tt}")
    for tt in range(TT):
        def emit_var(v, tt=tt):
            m = lnst[:, tt:tt + 1]
            s2 = lnst2[:, tt:tt + 1]
            v.tensor_scalar(m, m, 1.0 / D, None, op0=ALU.mult)
            v.tensor_tensor(scal[:, SL["tmp"]:SL["tmp"] + 1], m, m, op=ALU.mult)
            v.tensor_scalar(s2, s2, 1.0 / D, None, op0=ALU.mult)
            v.tensor_tensor(s2, s2, scal[:, SL["tmp"]:SL["tmp"] + 1],
                            op=ALU.subtract)
            return v.tensor_scalar(s2, s2, 1e-5, None, op0=ALU.add)
        P.op("vector", emit_var, [f"ln_sq{tt}"], signal=f"ln_v{tt}")
    for tt in range(TT):
        def emit_rs(a, tt=tt):
            return a.activation(lnst2[:, tt:tt + 1], lnst2[:, tt:tt + 1],
                                AF.Sqrt)
        P.op("scalar", emit_rs, [f"ln_v{tt}"], signal=f"ln_rs{tt}")
    for tt in range(TT):
        def emit_y(v, tt=tt):
            vrecip(v, lnst2[:, tt:tt + 1], lnst2[:, tt:tt + 1])
            v.tensor_scalar(y32[:, tt, :], y32[:, tt, :], lnst[:, tt:tt + 1],
                            lnst2[:, tt:tt + 1], op0=ALU.subtract, op1=ALU.mult)
            vreduce(v, scal[:, SL["tmp"]:SL["tmp"] + 1], y32[:, tt, :],
                    axis=AX.X, op=ALU.max, apply_absolute_value=True)
            return v.tensor_tensor(scal[:, SL["m1a"]:SL["m1a"] + 1],
                                   scal[:, SL["m1a"]:SL["m1a"] + 1],
                                   scal[:, SL["tmp"]:SL["tmp"] + 1], op=ALU.max)
        P.op("vector", emit_y, [f"ln_rs{tt}"], signal=f"ln_y{tt}")

    dma_s("y_spill", xre(yd), y32[:, :, :], waits=[f"ln_y{TT-1}"], ch="sp")

    def mbarrier(idx, name, acc_slot, extra_c13, waits):
        dma_g(f"{name}_p1", mxp.ap().rearrange("(p n) -> p n", n=1),
              scal[:, acc_slot:acc_slot + 1], waits=waits, ch="gM")
        dma_g(f"{name}_p2", bcrow[:, :],
              mxp.ap().rearrange("(n p) -> n p", p=128), waits=[f"{name}_p1"],
              ch="gM")

        def vpar(v):
            vreduce(v, scal[0:1, SL["tmp"] + 1:SL["tmp"] + 2],
                    bcrow[:, :], axis=AX.X, op=ALU.max,
                    apply_absolute_value=True)
            return v.nop()
        P.op("vector", vpar, [f"{name}_p2"], signal=f"{name}_par")
        dma_g(f"{name}_o", mxi[idx][:, :], scal[0:1, SL["tmp"] + 1:SL["tmp"] + 2],
              waits=[f"{name}_par"], ch="gM")
        P.op("gpsimd", lambda g: g.collective_compute(
            "AllReduce", ALU.max, replica_groups=[list(range(8))],
            ins=[mxi[idx].ap().opt()], outs=[mxo[idx].ap().opt()]),
            [f"{name}_o"], signal=f"{name}_ar", amount=1, sem="cc")
        dma_g(f"{name}_i", scal[0:1, SL["row"]:SL["row"] + 1], mxo[idx][:, :],
              waits=[f"{name}_ar"], ch="gM")

        def vscale(v):
            r = SL["row"]
            vsync(v, v.tensor_scalar(scal[0:1, r + 1:r + 2], scal[0:1, r:r + 1],
                                     IQA, 1e-12, op0=ALU.mult, op1=ALU.max))
            vrecip(v, scal[0:1, r + 2:r + 3], scal[0:1, r + 1:r + 2])
            if extra_c13:
                return v.tensor_scalar(scal[0:1, r + 3:r + 4], scal[0:1, r + 1:r + 2],
                                       1.0 / 65536.0, None, op0=ALU.mult)
            return v.nop()
        P.op("vector", vscale, [f"{name}_i"], signal=f"{name}_vs")

    def mb_bcast(name, slots):
        def bcmm(t):
            last = None
            for n, (dst, ro) in enumerate(slots):
                last = t.matmul(psX[:, 960 + n:960 + n + 1], bcones[:, :],
                                scal[0:1, SL["row"] + ro:SL["row"] + ro + 1],
                                start=True, stop=True)
            return last
        P.op("tensor", bcmm, [f"{name}_vs"], signal=f"{name}_bm")

        def bccp(v):
            last = None
            for n, (dst, ro) in enumerate(slots):
                last = v.tensor_copy(scal[:, dst:dst + 1],
                                     psX[:, 960 + n:960 + n + 1])
            vsync(v, last)
            return v.nop()
        P.op("vector", bccp, [f"{name}_bm"], signal=f"{name}_bc")

    mbarrier(0, "m1", SL["m1a"], False, [f"ln_y{TT-1}"])
    mb_bcast("m1", [(SL["s1"], 1), (SL["r1"], 2)])

    for tt in range(TT):
        def emit_xq(v, tt=tt):
            v.tensor_scalar(lnsq[:, :], y32[:, tt, :], scal[:, SL["r1"]:SL["r1"] + 1],
                            RC, op0=ALU.mult, op1=ALU.add)
            return v.tensor_scalar(xq16[:, tt, :], lnsq[:, :], -RC,
                                   scal[:, SL["s1"]:SL["s1"] + 1],
                                   op0=ALU.add, op1=ALU.mult)
        P.op("vector", emit_xq, ["m1_bc"] if tt == 0 else [], signal=f"xq{tt}")
    dma_s("xq_spill", xre(xqd), xq16[:, :, :], waits=[f"xq{TT-1}"], ch="sp")

    for ft in range(FT):
        dma_s(f"ytq{ft}", ytq[:, ft, :], xqd[:, 128 * ft:128 * ft + 128],
              waits=["xq_spill"], tr=True, ch="ytq")
    for ft in range(FT):
        dma_s(f"wtgG{ft}", wtg[:, ft, :], wg16d[:, 128 * ft:128 * ft + 128],
              waits=[f"dq_st{IT-2}", f"dq_st{IT-1}"], tr=True, ch="wgG")

    def bank4(n):
        return (psQ if (n % 4) < 2 else psK)[:, 512 * (n % 2):512 * (n % 2) + 512]

    nmm = 0
    for wsel in range(2):
        for it in range(IT):
            for tc in range(TC):
                def emit_ug(t, it=it, tc=tc, nmm=nmm):
                    bk = bank4(nmm)
                    last = None
                    for ft in range(FT):
                        last = t.matmul(bk, wtg[:, ft, 128 * it:128 * it + 128],
                                        ytq[:, ft, 512 * tc:512 * tc + 512],
                                        start=(ft == 0), stop=(ft == FT - 1))
                    return last
                w = []
                if nmm == 0:
                    w = [f"ytq{FT-1}", f"wtgG{FT-1}", f"wo_ev{TT-1}"]
                if wsel == 1 and it == 0 and tc == 0:
                    w += [f"wtgU{FT-1}"]
                if nmm >= 4:
                    w += [f"ug_ev{nmm-4}"]
                P.op("tensor", emit_ug, w, signal=f"ug_mm{nmm}")

                def emit_ugev(a, wsel=wsel, it=it, tc=tc, nmm=nmm):
                    stg = stg1 if wsel == 0 else stg2
                    return a.activation(stg[:, it % 2, 512 * tc:512 * tc + 512],
                                        bank4(nmm), AF.Copy)
                wev = [f"ug_mm{nmm}"]
                if nmm >= 2 * TC:
                    wev += [f"ug_mx{nmm - 2 * TC}", f"rawsp{(nmm - 2 * TC) // (IT * TC)}_{((nmm - 2 * TC) // TC) % IT}"]
                P.op("scalar", emit_ugev, wev, signal=f"ug_ev{nmm}")

                def emit_ugmx(v, wsel=wsel, it=it, tc=tc):
                    stg = stg1 if wsel == 0 else stg2
                    acc = SL["m3a"] if wsel == 0 else SL["m2a"]
                    vreduce(v, scal[:, SL["tmp"]:SL["tmp"] + 1],
                            stg[:, it % 2, 512 * tc:512 * tc + 512],
                            axis=AX.X, op=ALU.max, apply_absolute_value=True)
                    return v.tensor_tensor(scal[:, acc:acc + 1], scal[:, acc:acc + 1],
                                           scal[:, SL["tmp"]:SL["tmp"] + 1], op=ALU.max)
                P.op("vector", emit_ugmx, [f"ug_ev{nmm}"], signal=f"ug_mx{nmm}")
                nmm += 1
            stgd = graw if wsel == 0 else uraw
            stgs = stg1 if wsel == 0 else stg2
            dma_s(f"rawsp{wsel}_{it}", xre(stgd)[:, it, :], stgs[:, it % 2, :],
                  waits=[f"ug_ev{nmm-1}"], ch="rwG" if wsel == 0 else "rwU")
        if wsel == 0:
            for ft in range(FT):
                dma_s(f"wtgU{ft}", wtg[:, ft, :], wu16d[:, 128 * ft:128 * ft + 128],
                      waits=[f"ug_mm{IT * TC - 1}", f"dq_st{2*IT-2}", f"dq_st{2*IT-1}"],
                      tr=True, ch="wgU")

    mbarrier(1, "m3", SL["m3a"], True, [f"ug_mx{IT * TC - 1}"])
    mb_bcast("m3", [(SL["s3"], 1), (SL["r3"], 2), (SL["c13"], 3)])
    mbarrier(2, "m2", SL["m2a"], False, [f"ug_mx{2 * IT * TC - 1}"])
    mb_bcast("m2", [(SL["s2"], 1), (SL["r2"], 2)])

    def emit_s1a(v, it):
        p = it % 2
        v.tensor_scalar(fA[:, p, :], stg1[:, p, :], scal[:, SL["r3"]:SL["r3"] + 1],
                        RC, op0=ALU.mult, op1=ALU.add)
        return v.tensor_scalar(fA[:, p, :], fA[:, p, :], -RC, None, op0=ALU.add)

    def emit_sig(a, it):
        p = it % 2
        return a.activation(fB[:, p, :], fA[:, p, :], AF.Sigmoid,
                            scale=scal[:, SL["s3"]:SL["s3"] + 1])

    def emit_s1b(v, it):
        p = it % 2
        v.tensor_scalar(fB[:, p, :], fB[:, p, :], 65536.0, RC, op0=ALU.mult, op1=ALU.add)
        v.tensor_scalar(fB[:, p, :], fB[:, p, :], -RC, 65535.0, op0=ALU.add, op1=ALU.min)
        v.scalar_tensor_tensor(fC[:, :], fA[:, p, :], scal[:, SL["c13"]:SL["c13"] + 1],
                               fB[:, p, :], op0=ALU.mult, op1=ALU.mult)
        vreduce(v, scal[:, SL["tmp"]:SL["tmp"] + 1], fC[:, :],
                axis=AX.X, op=ALU.max, apply_absolute_value=True)
        v.tensor_tensor(scal[:, SL["m4a"]:SL["m4a"] + 1],
                        scal[:, SL["m4a"]:SL["m4a"] + 1],
                        scal[:, SL["tmp"]:SL["tmp"] + 1], op=ALU.max)
        return v.tensor_copy(stg3[:, p, :], fC[:, :])

    def s1_dma(it):
        w = [f"rawsp0_{IT-1}"] + ([f"s1a{it-2}"] if it >= 2 else [])
        dma_s(f"s1g{it}", stg1[:, it % 2, :], xre(graw)[:, it, :], waits=w,
              ch=f"fI{it % 2}")

    for it in range(IT):
        if it >= 2:
            dma_s(f"s1p{it-2}", xre(prodd)[:, it - 2, :],
                  stg3[:, (it - 2) % 2, :], waits=[f"s1v{it-2}"], ch=f"fO{(it-2) % 2}")
        s1_dma(it)
        w = [f"s1g{it}", "m3_bc"] + ([f"s1sg{it-2}"] if it >= 2 else [])
        P.op("vector", lambda v, it=it: emit_s1a(v, it), w, signal=f"s1a{it}")
        ws = [f"s1a{it}"] + ([f"s1v{it-2}"] if it >= 2 else [])
        P.op("scalar", lambda a, it=it: emit_sig(a, it), ws, signal=f"s1sg{it}")
        if it >= 1:
            pit = it - 1
            w = [f"s1sg{pit}"] + ([f"s1p{pit-2}"] if pit >= 2 else [])
            P.op("vector", lambda v, it=pit: emit_s1b(v, it), w, signal=f"s1v{pit}")
    pit = IT - 1
    P.op("vector", lambda v, it=pit: emit_s1b(v, it),
         [f"s1sg{pit}", f"s1p{pit-2}"], signal=f"s1v{pit}")
    for it in (IT - 2, IT - 1):
        dma_s(f"s1p{it}", xre(prodd)[:, it, :], stg3[:, it % 2, :],
              waits=[f"s1v{it}"], ch=f"fO{it % 2}")

    mbarrier(3, "m4", SL["m4a"], False, [f"s1v{IT-1}"])
    mb_bcast("m4", [(SL["s4"], 1), (SL["r4"], 2)])

    for it in range(IT):
        p = it % 2
        w = [f"s1p{IT-2}", f"s1p{IT-1}"]
        if it >= 2:
            w += [f"s2v{it-2}"]
        else:
            w += [f"s1v{IT-1}"]
        dma_s(f"s2i{it}", stg3[:, p, :], xre(prodd)[:, it, :], waits=w,
              ch=f"fI{p}")
        w2 = [f"rawsp1_{IT-1}"]
        if it >= 2:
            w2 += [f"s2v{it-2}"]
        dma_s(f"s2u{it}", stg2[:, p, :], xre(uraw)[:, it, :], waits=w2,
              ch=f"fJ{p}")

        def emit_s2(v, it=it, p=p):
            v.tensor_scalar(fD[:, :], stg2[:, p, :], scal[:, SL["r2"]:SL["r2"] + 1],
                            RC, op0=ALU.mult, op1=ALU.add)
            v.tensor_scalar(stg2[:, p, :], fD[:, :], -RC,
                            scal[:, SL["s2"]:SL["s2"] + 1],
                            op0=ALU.add, op1=ALU.mult)
            v.tensor_scalar(fA[:, p, :], stg3[:, p, :], scal[:, SL["r4"]:SL["r4"] + 1],
                            RC, op0=ALU.mult, op1=ALU.add)
            v.tensor_scalar(fA[:, p, :], fA[:, p, :], -RC, scal[:, SL["s4"]:SL["s4"] + 1],
                            op0=ALU.add, op1=ALU.mult)
            v.tensor_tensor(fB[:, p, :], fA[:, p, :], stg2[:, p, :], op=ALU.mult)
            vreduce(v, scal[:, SL["tmp"]:SL["tmp"] + 1], fB[:, p, :],
                    axis=AX.X, op=ALU.max, apply_absolute_value=True)
            v.tensor_tensor(scal[:, SL["m5a"]:SL["m5a"] + 1],
                            scal[:, SL["m5a"]:SL["m5a"] + 1],
                            scal[:, SL["tmp"]:SL["tmp"] + 1], op=ALU.max)
            return v.tensor_copy(stg1[:, p, :], fB[:, p, :])
        w3 = [f"s2i{it}", f"s2u{it}", "m4_bc", "m2_bc"]
        if it >= 2:
            w3 += [f"s2o{it-2}"]
        P.op("vector", emit_s2, w3, signal=f"s2v{it}")
        dma_s(f"s2o{it}", xre(qsd)[:, it, :], stg1[:, p, :], waits=[f"s2v{it}"],
              ch=f"fO{p}")

    mbarrier(4, "m5", SL["m5a"], False, [f"s2v{IT-1}"])
    mb_bcast("m5", [(SL["s5"], 1), (SL["r5"], 2)])

    def bank8(idx):
        tensors = [psQ, psQ, psK, psK, psC, psC, psX, psX]
        return tensors[idx][:, 512 * (idx % 2):512 * (idx % 2) + 512]

    for tg in range(TG):
        for it in range(IT):
            gi = tg * IT + it
            w = [f"s2o{IT-2}", f"s2o{IT-1}"]
            if gi >= 2:
                w += [f"s3v{gi-2}"]
            else:
                w += [f"s2v{IT-1}"]
            dma_s(f"s3i{gi}", stg1[:, it % 2, 0:512],
                  xre(qsd)[:, it, 512 * tg:512 * tg + 512], waits=w,
                  ch=f"fI{gi % 2}")
            ww = [f"dq_st{ndq-2}", f"dq_st{ndq-1}"]
            if gi >= 2:
                ww += [f"dn_it{(gi-2)//IT}_{(gi-2)%IT}"]
            dma_s(f"s3w{gi}", wtd[:, it % 2, :], wd16d[:, 128 * it:128 * it + 128],
                  waits=ww, tr=True, ch=f"fJ{gi % 2}")

            def emit_s3(v, it=it, gi=gi, tg=tg):
                v.tensor_scalar(fA[:, gi % 2, 0:512], stg1[:, it % 2, 0:512],
                                scal[:, SL["r5"]:SL["r5"] + 1], RC,
                                op0=ALU.mult, op1=ALU.add)
                return v.tensor_scalar(ot[:, it, :], fA[:, gi % 2, 0:512], -RC,
                                       scal[:, SL["s5"]:SL["s5"] + 1],
                                       op0=ALU.add, op1=ALU.mult)
            w2 = [f"s3i{gi}", "m5_bc"]
            if tg >= 1:
                w2 += [f"dn_it{tg-1}_{IT-1}"]
            P.op("vector", emit_s3, w2, signal=f"s3v{gi}")

        for it in range(IT):
            gi = tg * IT + it

            def emit_dn(t, tg=tg, it=it, gi=gi):
                last = None
                for q in range(4):
                    for ch in range(2):
                        last = t.matmul(bank8(2 * q + ch),
                                        ot[:, it, 128 * q:128 * q + 128],
                                        wtd[:, it % 2, 512 * ch:512 * ch + 512],
                                        start=(it == 0), stop=(it == IT - 1))
                return last
            w = [f"s3v{gi}", f"s3w{gi}"]
            if it == 0:
                if tg == 0:
                    w += [f"ug_ev{2 * IT * TC - 1}", f"cx_ev{2 * B * CH - 1}",
                          f"at_ex{n_groups-1}"]
                else:
                    w += [f"dn_ev{4*(tg-1)+q}" for q in range(4)]
            P.op("tensor", emit_dn, w, signal=f"dn_it{tg}_{it}")

        for q in range(4):
            tt = 4 * tg + q

            def emit_dnev(a, q=q, tt=tt):
                a.activation(dnsc[:, tt % 4, 0:512], bank8(2 * q), AF.Copy)
                return a.activation(dnsc[:, tt % 4, 512:1024], bank8(2 * q + 1), AF.Copy)
            w = [f"dn_it{tg}_{IT-1}"]
            if tt >= 4:
                w += [f"o_y{tt-4}"]
            P.op("scalar", emit_dnev, w, signal=f"dn_ev{tt}")

            w = ["xq_spill"] + ([f"o_res{tt-2}"] if tt >= 2 else [])
            dma_s(f"yr{tt}", yrl[:, tt % 2, :], xre(yd)[:, tt, :], waits=w,
                  ch=f"yr{tt % 2}")
        for q in range(4):
            tt = 4 * tg + q

            def emit_ores(v, tt=tt):
                v.tensor_tensor(dnsc[:, tt % 4, :], dnsc[:, tt % 4, :],
                                yrl[:, tt % 2, :], op=ALU.add)
                return v.reduce_sum(lnst[:, 2 * (tt % 4):2 * (tt % 4) + 1],
                                    dnsc[:, tt % 4, :], axis=AX.X)
            P.op("vector", emit_ores, [f"dn_ev{tt}", f"yr{tt}"], signal=f"o_res{tt}")
        for q in range(4):
            tt = 4 * tg + q

            def emit_osq(a, tt=tt):
                return a.activation(lnsq[:, :], dnsc[:, tt % 4, :], AF.Square,
                                    accum_out=lnst[:, 2 * (tt % 4) + 1:2 * (tt % 4) + 2])
            P.op("scalar", emit_osq, [f"o_res{tt}"], signal=f"o_sq{tt}")
        for q in range(4):
            tt = 4 * tg + q

            def emit_ovar(v, tt=tt):
                st0 = 2 * (tt % 4)
                m = lnst[:, st0:st0 + 1]
                s2 = lnst[:, st0 + 1:st0 + 2]
                v.tensor_scalar(m, m, 1.0 / D, None, op0=ALU.mult)
                v.tensor_tensor(scal[:, SL["tmp"]:SL["tmp"] + 1], m, m, op=ALU.mult)
                v.tensor_scalar(s2, s2, 1.0 / D, None, op0=ALU.mult)
                v.tensor_tensor(s2, s2, scal[:, SL["tmp"]:SL["tmp"] + 1],
                                op=ALU.subtract)
                return v.tensor_scalar(s2, s2, 1e-5, None, op0=ALU.add)
            P.op("vector", emit_ovar, [f"o_sq{tt}"], signal=f"o_var{tt}")
        for q in range(4):
            tt = 4 * tg + q

            def emit_ors(a, tt=tt):
                st0 = 2 * (tt % 4)
                return a.activation(lnst[:, st0 + 1:st0 + 2], lnst[:, st0 + 1:st0 + 2],
                                    AF.Sqrt)
            P.op("scalar", emit_ors, [f"o_var{tt}"], signal=f"o_rs{tt}")
        for q in range(4):
            tt = 4 * tg + q

            def emit_oy(v, tt=tt):
                st0 = 2 * (tt % 4)
                vrecip(v, lnst[:, st0 + 1:st0 + 2], lnst[:, st0 + 1:st0 + 2])
                return v.tensor_scalar(outb[:, tt % 2, :], dnsc[:, tt % 4, :],
                                       lnst[:, st0:st0 + 1], lnst[:, st0 + 1:st0 + 2],
                                       op0=ALU.subtract, op1=ALU.mult)
            w = [f"o_rs{tt}"] + ([f"ost{tt-2}"] if tt >= 2 else [])
            P.op("vector", emit_oy, w, signal=f"o_y{tt}")
            dma_s(f"ost{tt}", xre(out)[:, tt, :], outb[:, tt % 2, :], waits=[f"o_y{tt}"],
                  ch=f"os{tt % 2}")

    P.op("sync", lambda s: s.nop(), [f"ost{TT-2}", f"ost{TT-1}"], signal=None)

    chans = ["vself", "ini", "cx", "cT", "sp", "ytq", "wgG", "wgU", "rwG", "rwU",
             "xm0", "xm1", "dqL0", "dqL1", "dqS0", "dqS1", "fI0", "fI1",
             "fJ0", "fJ1", "fO0", "fO1", "yr0", "yr1", "os0", "os1",
             "gW", "gM", "g0", "g1", "cc",
             "c_tensor", "c_scalar", "c_vector", "c_gpsimd", "c_sync"]
    import contextlib
    with contextlib.ExitStack() as stack:
        sems = {k: stack.enter_context(nc.semaphore(k)) for k in chans}
        vself["sem"] = sems["vself"]
        block = stack.enter_context(nc.Block())
        engines = {}

        def reg(name):
            def f(e):
                engines[name] = e
                run(name)
            return f
        run = P.build_and_emit(sems, engines)
        block.sync(reg("sync"))
        block.gpsimd(reg("gpsimd"))
        block.tensor(reg("tensor"))
        block.scalar(reg("scalar"))
        block.vector(reg("vector"))
    return nc


def _in_maps(cfg, inputs):
    st = np.asarray(inputs["L0_attn_states"], dtype=np.float32)
    B, S, D = st.shape
    flat = np.ascontiguousarray(st.reshape(B * S, D))
    xt = np.ascontiguousarray(flat.T)
    wq = np.asarray(inputs["L0_attn_Wq"], np.float32)
    wk = np.asarray(inputs["L0_attn_Wk"], np.float32)
    wv = np.asarray(inputs["L0_attn_Wv"], np.float32)
    wo = np.ascontiguousarray(np.asarray(inputs["L0_attn_Wo"], np.float32))
    wgn = np.ascontiguousarray(np.asarray(inputs["L0_ffn_w_gate"], np.float32))
    wun = np.ascontiguousarray(np.asarray(inputs["L0_ffn_w_up"], np.float32))
    wdn = np.ascontiguousarray(np.asarray(inputs["L0_ffn_w_down"], np.float32))
    tri = np.triu(np.ones((128, 128), np.float32))
    maskc = ((tri - 1.0) * 30000.0).astype(np.float16)
    identc = np.eye(128, dtype=np.float16)
    TM = cfg.TM
    maps = []
    for c in range(8):
        maps.append(dict(
            xt=xt,
            xm=np.ascontiguousarray(flat[TM * c:TM * (c + 1)]),
            wq=np.ascontiguousarray(wq[:, 128 * c:128 * c + 128]),
            wk=np.ascontiguousarray(wk[:, 128 * c:128 * c + 128]),
            wv=np.ascontiguousarray(wv[:, 128 * c:128 * c + 128]),
            wo=wo, wgn=wgn, wun=wun, wdn=wdn,
            maskc=maskc, identc=identc,
        ))
    return maps


def kernel(**inputs):
    st = np.asarray(inputs["L0_attn_states"])
    B, S, D = st.shape
    cfg = CFG(S=S)
    nc = build_nc(cfg)
    maps = _in_maps(cfg, inputs)
    res = run_bass_kernel_spmd(nc, maps, core_ids=list(range(8)))
    outp = np.concatenate([np.asarray(r["out"]) for r in res.results], axis=0)
    return outp.reshape(B, S, D).astype(np.float32)


# version 05
# speedup vs baseline: 1.0970x; 1.0109x over previous
import os
os.environ.setdefault("TRN_ALLOW_ICI_ALL_TO_ALL", "1")

import numpy as np
import concourse.bass as bass
import concourse.mybir as mybir
from concourse.bass_utils import run_bass_kernel_spmd

F32 = mybir.dt.float32
F16 = mybir.dt.float16
AF = mybir.ActivationFunctionType
ALU = mybir.AluOpType
AX = mybir.AxisListType

RC = 12582912.0
IQA = 1.0 / 32767.0
IQW = 1.0 / 7.0


class CFG:
    def __init__(self, S=4096, debug=False):
        self.debug = debug
        self.B = 4
        self.S = S
        self.D = 1024
        self.I = 3072
        self.TM = self.B * S // 8
        self.CH = S // 512
        self.TT = self.TM // 128
        self.TG = self.TM // 512
        self.IT = self.I // 128
        self.FT = self.D // 128
        self.TC = self.TM // 512


class Prog:
    ENGINES = ("sync", "gpsimd", "tensor", "scalar", "vector")

    def __init__(self):
        self.ops = []

    def op(self, engine, emit, waits=(), signal=None, amount=1, sem=None):
        self.ops.append((engine, emit, tuple(waits), signal, amount, sem))

    def build_and_emit(self, sems, engines):
        counts = {k: 0 for k in sems}
        table = {}
        for engine, emit, waits, signal, amount, semk in self.ops:
            if signal is not None:
                k = semk if semk is not None else "c_" + engine
                counts[k] += amount
                assert signal not in table, f"dup event {signal}"
                table[signal] = (k, counts[k])
        for engine, emit, waits, signal, amount, semk in self.ops:
            for w in waits:
                assert w in table, f"missing event {w}"
        watermark = {e: {k: 0 for k in sems} for e in self.ENGINES}

        def run(engine):
            eng_obj = engines[engine]
            for e2, emit, waits, signal, amount, semk in self.ops:
                if e2 != engine:
                    continue
                need = {}
                for w in waits:
                    k, v = table[w]
                    need[k] = max(need.get(k, 0), v)
                for k, v in need.items():
                    if watermark[engine][k] < v:
                        eng_obj.wait_ge(sems[k], v)
                        watermark[engine][k] = v
                inst = emit(eng_obj)
                if signal is not None:
                    k = semk if semk is not None else "c_" + engine
                    assert inst is not None, f"event {signal} has no instruction"
                    inst.then_inc(sems[k], amount)
        return run


def mem_map(cfg):
    M, o = {}, 17408

    def al(name, nbytes, at=None):
        nonlocal o
        if at is not None:
            M[name] = at
            return at
        M[name] = o
        o += (nbytes + 31) // 32 * 32
        return M[name]

    al("ident", 256); al("maskb", 256)
    al("wq16", cfg.FT * 256); al("wk16", cfg.FT * 256); al("wv16", cfg.FT * 256)
    al("wo16", cfg.FT * 2048)
    al("scal", 256)
    al("dqs", 128); al("dqr", 128)
    al("bcrow", 512); al("bcones", 512)
    al("lnsq", 4096)
    al("lnst", 64)
    al("lnst2", 64)
    al("dqz", 4096)
    al("dqw32", 2 * 4096)
    al("dq16", 2 * 2048)
    P1 = o
    al("xtb", 2 * cfg.FT * 1024)
    al("qt", 2 * cfg.S * 2)
    al("ktp0", 2 * cfg.S * 2)
    al("ktp1", 2 * cfg.S * 2)
    al("vtt", 2 * (cfg.S // 128) * 264)
    al("expst", 6 * 2048)
    al("ctxq", 2 * (cfg.S // 128) * 256)
    p1_end = o
    al("ctxT", 0, at=P1)
    al("xq16", 0, at=P1)
    ph2 = P1 + max(cfg.FT * cfg.TM * 2, cfg.TT * 2048)
    al("y32", cfg.TT * 4096, at=ph2); ph2 += cfg.TT * 4096
    al("xmst", 0, at=ph2); ph2 += 2 * 4096
    al("lnsc", 0, at=ph2); ph2 += 2 * 4096
    al("ytq", 0, at=P1)
    al("ot", 0, at=P1)
    base3 = P1 + max(cfg.FT * cfg.TM * 2, cfg.IT * 1024)
    al("wtg", 0, at=base3); base3 += cfg.FT * 6144
    al("wtd", 2 * 2048, at=base3); base3 += 2 * 2048
    al("stg1", 0, at=base3); base3 += 2 * cfg.TM * 2
    al("fA", 0, at=base3); base3 += 2 * cfg.TM * 4
    dead0 = base3
    al("stg2", 0, at=base3); base3 += 4 * cfg.TM * 2
    al("stg3", 0, at=base3); base3 += 4 * cfg.TM * 2
    al("fB", 0, at=base3); base3 += 2 * cfg.TM * 4
    o3 = dead0
    al("yrl", 0, at=o3); o3 += 2 * 4096
    al("outb", 0, at=o3); o3 += 2 * 4096
    al("dnsc", 0, at=o3); o3 += 4 * 4096
    M["_END"] = max(p1_end, ph2, base3, o3)
    return M


def build_nc(cfg):
    nc = bass.Bass("TRN2", target_bir_lowering=False,
                   detect_race_conditions=False)
    B, S, D, I = cfg.B, cfg.S, cfg.D, cfg.I
    TM, CH, TT, TG, IT, FT, TC = (cfg.TM, cfg.CH, cfg.TT, cfg.TG, cfg.IT,
                                  cfg.FT, cfg.TC)
    M = mem_map(cfg)
    assert M["_END"] <= 229344, M["_END"]

    xt = nc.dram_tensor("xt", [D, B * S], F32, kind="ExternalInput").ap()
    xm = nc.dram_tensor("xm", [TM, D], F32, kind="ExternalInput").ap()
    wq = nc.dram_tensor("wq", [D, 128], F32, kind="ExternalInput").ap()
    wk = nc.dram_tensor("wk", [D, 128], F32, kind="ExternalInput").ap()
    wv = nc.dram_tensor("wv", [D, 128], F32, kind="ExternalInput").ap()
    wo = nc.dram_tensor("wo", [D, D], F32, kind="ExternalInput").ap()
    wgn = nc.dram_tensor("wgn", [I, D], F32, kind="ExternalInput").ap()
    wun = nc.dram_tensor("wun", [I, D], F32, kind="ExternalInput").ap()
    wdn = nc.dram_tensor("wdn", [D, I], F32, kind="ExternalInput").ap()
    maskc = nc.dram_tensor("maskc", [128, 128], F16, kind="ExternalInput").ap()
    identc = nc.dram_tensor("identc", [128, 128], F16, kind="ExternalInput").ap()
    out = nc.dram_tensor("out", [TM, D], F32, kind="ExternalOutput").ap()

    dbg = "ExternalOutput" if cfg.debug else "Internal"
    a2a_i = nc.dram_tensor("a2a_i", [8, TM, 128], F16)
    a2a_o = nc.dram_tensor("a2a_o", [8, TM, 128], F16)
    mxi = [nc.dram_tensor(f"mxi{k}", [1, 1], F32) for k in range(5)]
    mxo = [nc.dram_tensor(f"mxo{k}", [1, 1], F32) for k in range(5)]
    mxp = nc.dram_tensor("mxp", [128], F32)
    wg16d = nc.dram_tensor("wg16d", [I, D], F16, kind=dbg)
    wu16d = nc.dram_tensor("wu16d", [I, D], F16, kind=dbg)
    wd16d = nc.dram_tensor("wd16d", [D, I], F16, kind=dbg)
    graw = nc.dram_tensor("graw", [I, TM], F16, kind=dbg)
    uraw = nc.dram_tensor("uraw", [I, TM], F16, kind=dbg)
    uqd = nc.dram_tensor("uqd", [I, TM], F16, kind=dbg)
    prodd = nc.dram_tensor("prodd", [I, TM], F16, kind=dbg)
    qsd = nc.dram_tensor("qsd", [I, TM], F16, kind=dbg)
    xqd = nc.dram_tensor("xqd", [TM, D], F16, kind=dbg)
    yd = nc.dram_tensor("yd", [TM, D], F32, kind=dbg)

    def t16(name, *shape, key=None):
        return nc.alloc_sbuf_tensor_at(name, [128, *shape], F16, offset=M[key or name])

    def t32(name, *shape, key=None):
        return nc.alloc_sbuf_tensor_at(name, [128, *shape], F32, offset=M[key or name])

    ident = t16("ident", 128)
    maskb = t16("maskb", 128)
    wq16 = t16("wq16", FT, 128)
    wk16 = t16("wk16", FT, 128)
    wv16 = t16("wv16", FT, 128)
    wo16 = t16("wo16", FT, 1024)
    scal = t32("scal", 64)
    dqs = t32("dqs", 32)
    dqr = t32("dqr", 32)
    bcrow = nc.alloc_sbuf_tensor_at("bcrow", [1, 128], F32, offset=M["bcrow"])
    bcones = nc.alloc_sbuf_tensor_at("bcones", [1, 128], F32, offset=M["bcones"])
    xtb = t16("xtb", 2, FT, 512)
    qt = t16("qt", 2, S)
    ktp0 = t16("ktp0", 2, S)
    ktp1 = t16("ktp1", 2, S)
    vtt = t16("vtt", 2, S // 128, 132)
    expst = t16("expst", 6, 1024)
    ctxq = t16("ctxq", 2, S // 128, 128)
    dqz = t32("dqz", 1024)
    dqw32 = t32("dqw32", 2, 1024)
    dq16 = t16("dq16", 2, 1024)
    ctxT = t16("ctxT", FT, TM)
    xq16 = t16("xq16", TT, 1024)
    y32 = t32("y32", TT, 1024)
    xmst = t32("xmst", 2, 1024)
    lnsc = t32("lnsc", 2, 1024)
    lnsq = t32("lnsq", 1024)
    lnst = t32("lnst", 16)
    lnst2 = t32("lnst2", 16)
    ytq = t16("ytq", FT, TM)
    ot = t16("ot", IT, 512)
    wtg = t16("wtg", FT, 3072)
    wtd = t16("wtd", 2, 1024)
    stg1 = t16("stg1", 2, TM)
    stg2 = t16("stg2", 4, TM)
    stg3 = t16("stg3", 4, TM)
    fA = t32("fA", 2, TM)
    fB = t32("fB", 2, TM)
    yrl = t32("yrl", 2, 1024)
    outb = t32("outb", 2, 1024)
    dnsc = t32("dnsc", 4, 1024)

    psQ = nc.alloc_psum_tensor("psQ", [128, 1024], F32)
    psK = nc.alloc_psum_tensor("psK", [128, 1024], F32)
    psC = nc.alloc_psum_tensor("psC", [128, 1024], F32)
    psX = nc.alloc_psum_tensor("psX", [128, 1024], F32)

    SL = dict(r1=0, s1=1, r3=2, s3=3, c13=4, r2=5, s2=6, r4=7, s4=8, r5=9,
              s5=10, row=16, m1a=24, m3a=25, m2a=26, m4a=27, m5a=28, tmp=29)

    P = Prog()
    vself = {"n": 0, "sem": None}

    def vsync(v, inst):
        vself["n"] += 1
        inst.then_inc(vself["sem"], 1)
        v.wait_ge(vself["sem"], vself["n"])

    def vreduce(v, out, in_, **kw):
        inst = v.tensor_reduce(out, in_, **kw)
        vsync(v, inst)
        return inst

    def vrecip(v, out, in_):
        inst = v.reciprocal(out, in_)
        vsync(v, inst)
        return inst

    def dma_g(name, o_, i_, waits=(), ch="gW"):
        P.op("gpsimd", lambda g, o_=o_, i_=i_: g.dma_start(out=o_, in_=i_),
             waits, signal=name, amount=16, sem=ch)

    def dma_s(name, o_, i_, waits=(), tr=False, ch="ini"):
        if tr:
            P.op("sync", lambda s, o_=o_, i_=i_: s.dma_start(o_, i_, transpose=True),
                 waits, signal=name, amount=16, sem=ch)
        else:
            P.op("sync", lambda s, o_=o_, i_=i_: s.dma_start(out=o_, in_=i_),
                 waits, signal=name, amount=16, sem=ch)

    def xre(dram_ap, p=128):
        return dram_ap.rearrange("(a p) n -> p a n", p=p)

    dma_s("ld_mask", maskb[:, :], maskc)
    dma_s("ld_ident", ident[:, :], identc)
    dma_g("ld_wq", wq16[:, :, :], xre(wq))
    dma_g("ld_wk", wk16[:, :, :], xre(wk))
    dma_g("ld_wv", wv16[:, :, :], xre(wv))
    dma_g("ld_wo", wo16[:, :, :], xre(wo))

    def v_init(v):
        v.memset(bcones[:, :], 1.0)
        v.memset(ktp0[64:128, :, :], 0.0)
        v.memset(ktp1[0:64, :, :], 0.0)
        v.memset(vtt[:, :, :, 64:65], 1.0)
        v.memset(vtt[:, :, :, 129:130], 1.0)
        return v.memset(scal[:, SL["m1a"]:SL["m5a"] + 1], 0.0)
    P.op("vector", v_init, (), signal="v_init")

    nchunks = B * CH
    for i in range(nchunks):
        b, c = divmod(i, CH)
        w = [f"pj_mm{i-2}"] if i >= 2 else []
        dma_g(f"ld_xt{i}", xtb[:, i % 2, :, :],
              xre(xt)[:, :, b * S + 512 * c: b * S + 512 * c + 512], waits=w,
              ch=f"g{i % 2}")

    ndq = IT + IT + 3 * (D // 128)

    def dq_src(i):
        if i < IT:
            return xre(wgn)[:, i, :], xre(wg16d)[:, i, :]
        if i < 2 * IT:
            return xre(wun)[:, i - IT, :], xre(wu16d)[:, i - IT, :]
        r, cc = divmod(i - 2 * IT, 3)
        return (xre(wdn)[:, r, 1024 * cc:1024 * cc + 1024],
                xre(wd16d)[:, r, 1024 * cc:1024 * cc + 1024])

    def emit_dq_ld(i):
        w = [f"dq_done{i-2}"] if i >= 2 else []
        dma_s(f"dq_ld{i}", dqw32[:, i % 2, :], dq_src(i)[0], waits=w,
              ch=f"dqL{i % 2}")

    def emit_dq_st(i):
        dma_s(f"dq_st{i}", dq_src(i)[1], dq16[:, i % 2, :], waits=[f"dq_done{i}"],
              ch=f"dqS{i % 2}")

    dq_queue = []
    for i in range(ndq):
        def dq_ops(v, i=i):
            wt = dqw32[:, i % 2, :]
            for hh in range(2):
                vreduce(v, dqs[:, 16 * hh:16 * hh + 16].unsqueeze(2),
                        wt[:, 512 * hh:512 * hh + 512].rearrange("p (a n) -> p a n", n=32),
                        axis=AX.X, op=ALU.max, apply_absolute_value=True)
            vsync(v, v.tensor_scalar(dqs[:, :], dqs[:, :], IQW, 1e-12,
                                     op0=ALU.mult, op1=ALU.max))
            vrecip(v, dqr[:, :], dqs[:, :])
            v.tensor_tensor(dqz[:, :].rearrange("p (a n) -> p a n", n=32),
                            wt.rearrange("p (a n) -> p a n", n=32),
                            dqr[:, :].unsqueeze(2).broadcast_to([128, 32, 32]),
                            op=ALU.mult)
            v.tensor_scalar(dqz[:, :], dqz[:, :], RC, None, op0=ALU.add)
            return v.scalar_tensor_tensor(
                dq16[:, i % 2, :].rearrange("p (a n) -> p a n", n=32),
                dqz[:, :].rearrange("p (a n) -> p a n", n=32), -RC,
                dqs[:, :].unsqueeze(2).broadcast_to([128, 32, 32]),
                op0=ALU.add, op1=ALU.mult)
        wts = [f"dq_ld{i}"] + ([f"dq_st{i-2}"] if i >= 2 else [])
        dq_queue.append((dq_ops, wts, f"dq_done{i}"))

    dq_state = {"pop": 0, "ld": 0}

    def pop_dq(n):
        for _ in range(n):
            if dq_state["pop"] >= ndq:
                return
            i = dq_state["pop"]
            dq_state["pop"] += 1
            while dq_state["ld"] <= min(i + 2, ndq - 1):
                emit_dq_ld(dq_state["ld"])
                dq_state["ld"] += 1
            fn, w, sig = dq_queue[i]
            P.op("vector", fn, w, signal=sig)
            emit_dq_st(i)

    def av_targets(c, kt):
        return range(max(0, kt - 4 * c), 4)

    def ctx_off(m):
        return 65 * m if m < 7 else 512

    nsc = 0
    for b in range(B):
        for c in range(CH):
            i = b * CH + c

            def emit_pj(t, i=i, b=b, c=c):
                qb = psQ[:, 512 * (i % 2):512 * (i % 2) + 512]
                kb = psK[:, 512 * (i % 2):512 * (i % 2) + 512]
                vb = psC[:, 512 * (i % 2):512 * (i % 2) + 512]
                for ft in range(FT):
                    t.matmul(qb, wq16[:, ft, :], xtb[:, i % 2, ft, :],
                             start=(ft == 0), stop=(ft == FT - 1))
                for ft in range(FT):
                    t.matmul(kb, wk16[:, ft, :], xtb[:, i % 2, ft, :],
                             start=(ft == 0), stop=(ft == FT - 1))
                last = None
                for q in range(4):
                    for ft in range(FT):
                        last = t.matmul(vb[:, 128 * q:128 * q + 128],
                                        xtb[:, i % 2, ft, 128 * q:128 * q + 128],
                                        wv16[:, ft, :],
                                        start=(ft == 0), stop=(ft == FT - 1))
                return last
            w = [f"ld_xt{i}"]
            if i == 0:
                w += ["ld_wq", "ld_wk", "ld_wv", "ld_wo", "v_init",
                      "ld_mask", "ld_ident"]
            if i >= 2:
                w += [f"pj_ev{i-2}"]
            if c == 0 and b >= 1:
                w += [f"cx_ev{2 * b * CH - 1}"]
            P.op("tensor", emit_pj, w, signal=f"pj_mm{i}")

            def emit_ev(a, i=i, b=b, c=c):
                qb = psQ[:, 512 * (i % 2):512 * (i % 2) + 512]
                kb = psK[:, 512 * (i % 2):512 * (i % 2) + 512]
                vb = psC[:, 512 * (i % 2):512 * (i % 2) + 512]
                a.activation(qt[:, b % 2, 512 * c:512 * c + 512], qb, AF.Tanh)
                a.activation(ktp0[0:64, b % 2, 512 * c:512 * c + 512], kb[0:64, :], AF.Tanh)
                a.activation(ktp1[64:128, b % 2, 512 * c:512 * c + 512], kb[64:128, :], AF.Tanh)
                a.activation(vtt[:, b % 2, 4 * c:4 * c + 4, 0:64],
                             vb.rearrange("p (a n) -> p a n", n=128)[:, :, 0:64], AF.Tanh)
                return a.activation(vtt[:, b % 2, 4 * c:4 * c + 4, 65:129],
                                    vb.rearrange("p (a n) -> p a n", n=128)[:, :, 64:128],
                                    AF.Tanh)
            P.op("scalar", emit_ev, [f"pj_mm{i}"], signal=f"pj_ev{i}")

        def avbank(q):
            return (psX if q < 2 else psC)[:, 512 * (q % 2):512 * (q % 2) + 512]

        for c in range(CH):
            uidx = b * CH + c

            for h in range(2):
                blk = 2 * uidx + h
                pend = []

                def emit_av(t, ph, pg, pglob, b=b, c=c):
                    last = None
                    for ktl in (2 * pg, 2 * pg + 1):
                        slot = pglob % 6
                        half = ktl % 2
                        for q in av_targets(c, ktl):
                            last = t.matmul(
                                avbank(q)[:, 0:65],
                                expst[:, slot, 512 * half + 128 * q:512 * half + 128 * q + 128],
                                vtt[:, b % 2, ktl, 65 * ph:65 * ph + 65],
                                start=(ktl == 0), stop=(ktl == 4 * c + q))
                    return last

                for g in range(2 * c + 2):
                    glob = nsc

                    def emit_sc(t, h=h, g=g, glob=glob, b=b, c=c):
                        last = None
                        for ktl in (2 * g, 2 * g + 1):
                            bank = (psQ if glob % 2 == 0 else psK)[
                                :, 512 * (ktl % 2):512 * (ktl % 2) + 512]
                            ktp = ktp0 if h == 0 else ktp1
                            diag = ktl >= 4 * c
                            last = t.matmul(bank, ktp[:, b % 2, 128 * ktl:128 * ktl + 128],
                                            qt[:, b % 2, 512 * c:512 * c + 512],
                                            start=True, stop=not diag)
                            if diag:
                                r = ktl - 4 * c
                                last = t.matmul(bank[:, 128 * r:128 * r + 128],
                                                ident[:, :], maskb[:, :],
                                                start=False, stop=True)
                        return last
                    w = []
                    if g == 0 and blk == 0:
                        w += [f"pj_ev{CH - 1}"]
                    elif g == 0 and h == 0:
                        w += [f"pj_ev{(b + 1) * CH - 1}"]
                    if glob >= 2:
                        w += [f"at_ex{glob-2}"]
                    P.op("tensor", emit_sc, w, signal=f"at_sc{glob}")

                    def emit_ex(a, glob=glob):
                        return a.activation(expst[:, glob % 6, :],
                                            (psQ if glob % 2 == 0 else psK)[:, 0:1024],
                                            AF.Exp, scale=0.125)
                    we = [f"at_sc{glob}"]
                    if glob >= 6:
                        we += [f"at_av{glob-6}"]
                    P.op("scalar", emit_ex, we, signal=f"at_ex{glob}")

                    pend.append((h, g, glob))
                    nsc += 1
                    if len(pend) >= 2:
                        ph_, pg_, pglob_ = pend.pop(0)
                        wv = [f"at_ex{pglob_}"]
                        if pg_ == 0 and blk > 0:
                            wv += [f"cx_ev{blk-1}"]
                        P.op("tensor",
                             lambda t, ph=ph_, pg=pg_, pglob=pglob_, fn=emit_av: fn(t, ph, pg, pglob),
                             wv, signal=f"at_av{pglob_}")
                ph_, pg_, pglob_ = pend.pop(0)
                wv = [f"at_ex{pglob_}"]
                if pg_ == 0 and blk > 0:
                    wv += [f"cx_ev{blk-1}"]
                P.op("tensor",
                     lambda t, ph=ph_, pg=pg_, pglob=pglob_, fn=emit_av: fn(t, ph, pg, pglob),
                     wv, signal=f"at_av{pglob_}")

                def emit_cx(v, b=b, c=c, h=h):
                    last = None
                    for q in range(4):
                        j = 4 * c + q
                        bank = avbank(q)
                        vrecip(v, scal[:, SL["tmp"]:SL["tmp"] + 1],
                               bank[:, 64:65])
                        last = v.tensor_scalar(
                            ctxq[:, b % 2, j, 64 * h:64 * h + 64],
                            bank[:, 0:64],
                            scal[:, SL["tmp"]:SL["tmp"] + 1], None, op0=ALU.mult)
                    return last
                wcx = [f"at_av{nsc-1}"]
                if b >= 2:
                    wcx += [f"cxst{b-2}b"]
                P.op("vector", emit_cx, wcx, signal=f"cx_ev{blk}")
                pop_dq(1)

        dma_s(f"cxst{b}a", a2a_i[2 * b, :, :].rearrange("(a p) d -> p a d", p=128),
              ctxq[:, b % 2, 0:S // 256, :], waits=[f"cx_ev{2 * (b + 1) * CH - 1}"],
              ch="cx")
        dma_s(f"cxst{b}b", a2a_i[2 * b + 1, :, :].rearrange("(a p) d -> p a d", p=128),
              ctxq[:, b % 2, S // 256:S // 128, :], waits=[f"cx_ev{2 * (b + 1) * CH - 1}"],
              ch="cx")

    n_groups = nsc

    P.op("gpsimd", lambda g: g.collective_compute(
        "AllToAll", ALU.bypass, replica_groups=[list(range(8))],
        ins=[a2a_i.ap().opt()], outs=[a2a_o.ap().opt()]),
        [f"cxst{B-1}b"], signal="a2a", amount=1, sem="cc")

    for j in range(FT):
        dma_s(f"ctxT{j}", ctxT[:, j, :], a2a_o[j, :, :], waits=["a2a"], tr=True,
              ch="cT")

    pop_dq(len(dq_queue))

    for tt in range(TT):
        w = ([f"ln_c{tt-2}"] if tt >= 2 else [])
        dma_s(f"xm{tt}", xmst[:, tt % 2, :], xre(xm)[:, tt, :], waits=w,
              ch=f"xm{tt % 2}")

    for tt in range(TT):
        def emit_wo(t, tt=tt):
            ps = psQ if tt % 2 == 0 else psK
            last = None
            for ch in range(2):
                bank = ps[:, 512 * ch:512 * ch + 512]
                for j in range(FT):
                    last = t.matmul(bank, ctxT[:, j, 128 * tt:128 * tt + 128],
                                    wo16[:, j, 512 * ch:512 * ch + 512],
                                    start=(j == 0), stop=(j == FT - 1))
            return last
        if tt == 0:
            w = [f"ctxT{FT-1}", f"at_ex{n_groups-1}"]
        elif tt == 1:
            w = [f"ctxT{FT-1}"]
        else:
            w = [f"wo_ev{tt-2}"]
        P.op("tensor", emit_wo, w, signal=f"wo_mm{tt}")

        def emit_woev(a, tt=tt):
            ps = psQ if tt % 2 == 0 else psK
            a.activation(lnsc[:, tt % 2, 0:512], ps[:, 0:512], AF.Tanh)
            return a.activation(lnsc[:, tt % 2, 512:1024], ps[:, 512:1024], AF.Tanh)
        wev = [f"wo_mm{tt}"] + ([f"ln_c{tt-2}"] if tt >= 2 else [])
        P.op("scalar", emit_woev, wev, signal=f"wo_ev{tt}")

        def emit_res(v, tt=tt):
            v.tensor_tensor(y32[:, tt, :], lnsc[:, tt % 2, :], xmst[:, tt % 2, :],
                            op=ALU.add)
            return v.reduce_sum(lnst[:, tt:tt + 1], y32[:, tt, :], axis=AX.X)
        P.op("vector", emit_res, [f"wo_ev{tt}", f"xm{tt}"], signal=f"ln_c{tt}")

    for tt in range(TT):
        def emit_sq(a, tt=tt):
            return a.activation(lnsq[:, :], y32[:, tt, :], AF.Square,
                                accum_out=lnst2[:, tt:tt + 1])
        P.op("scalar", emit_sq, [f"ln_c{tt}"], signal=f"ln_sq{tt}")
    for tt in range(TT):
        def emit_var(v, tt=tt):
            m = lnst[:, tt:tt + 1]
            s2 = lnst2[:, tt:tt + 1]
            v.tensor_scalar(m, m, 1.0 / D, None, op0=ALU.mult)
            v.tensor_tensor(scal[:, SL["tmp"]:SL["tmp"] + 1], m, m, op=ALU.mult)
            v.tensor_scalar(s2, s2, 1.0 / D, None, op0=ALU.mult)
            v.tensor_tensor(s2, s2, scal[:, SL["tmp"]:SL["tmp"] + 1],
                            op=ALU.subtract)
            return v.tensor_scalar(s2, s2, 1e-5, None, op0=ALU.add)
        P.op("vector", emit_var, [f"ln_sq{tt}"], signal=f"ln_v{tt}")
    for tt in range(TT):
        def emit_rs(a, tt=tt):
            return a.activation(lnst2[:, tt:tt + 1], lnst2[:, tt:tt + 1],
                                AF.Sqrt)
        P.op("scalar", emit_rs, [f"ln_v{tt}"], signal=f"ln_rs{tt}")
    for tt in range(TT):
        def emit_y(v, tt=tt):
            vrecip(v, lnst2[:, tt:tt + 1], lnst2[:, tt:tt + 1])
            v.tensor_scalar(y32[:, tt, :], y32[:, tt, :], lnst[:, tt:tt + 1],
                            lnst2[:, tt:tt + 1], op0=ALU.subtract, op1=ALU.mult)
            vreduce(v, scal[:, SL["tmp"]:SL["tmp"] + 1], y32[:, tt, :],
                    axis=AX.X, op=ALU.max, apply_absolute_value=True)
            return v.tensor_tensor(scal[:, SL["m1a"]:SL["m1a"] + 1],
                                   scal[:, SL["m1a"]:SL["m1a"] + 1],
                                   scal[:, SL["tmp"]:SL["tmp"] + 1], op=ALU.max)
        P.op("vector", emit_y, [f"ln_rs{tt}"], signal=f"ln_y{tt}")

    dma_s("y_spill", xre(yd), y32[:, :, :], waits=[f"ln_y{TT-1}"], ch="sp")

    def mbarrier(idx, name, acc_slot, extra_c13, waits):
        dma_g(f"{name}_p1", mxp.ap().rearrange("(p n) -> p n", n=1),
              scal[:, acc_slot:acc_slot + 1], waits=waits, ch="gM")
        dma_g(f"{name}_p2", bcrow[:, :],
              mxp.ap().rearrange("(n p) -> n p", p=128), waits=[f"{name}_p1"],
              ch="gM")

        def vpar(v):
            vreduce(v, scal[0:1, SL["tmp"] + 1:SL["tmp"] + 2],
                    bcrow[:, :], axis=AX.X, op=ALU.max,
                    apply_absolute_value=True)
            return v.nop()
        P.op("vector", vpar, [f"{name}_p2"], signal=f"{name}_par")
        dma_g(f"{name}_o", mxi[idx][:, :], scal[0:1, SL["tmp"] + 1:SL["tmp"] + 2],
              waits=[f"{name}_par"], ch="gM")
        P.op("gpsimd", lambda g: g.collective_compute(
            "AllReduce", ALU.max, replica_groups=[list(range(8))],
            ins=[mxi[idx].ap().opt()], outs=[mxo[idx].ap().opt()]),
            [f"{name}_o"], signal=f"{name}_ar", amount=1, sem="cc")
        dma_g(f"{name}_i", scal[0:1, SL["row"]:SL["row"] + 1], mxo[idx][:, :],
              waits=[f"{name}_ar"], ch="gM")

        def vscale(v):
            r = SL["row"]
            vsync(v, v.tensor_scalar(scal[0:1, r + 1:r + 2], scal[0:1, r:r + 1],
                                     IQA, 1e-12, op0=ALU.mult, op1=ALU.max))
            vrecip(v, scal[0:1, r + 2:r + 3], scal[0:1, r + 1:r + 2])
            if extra_c13:
                return v.tensor_scalar(scal[0:1, r + 3:r + 4], scal[0:1, r + 1:r + 2],
                                       1.0 / 65536.0, None, op0=ALU.mult)
            return v.nop()
        P.op("vector", vscale, [f"{name}_i"], signal=f"{name}_vs")

    def mb_bcast(name, slots):
        def bcmm(t):
            last = None
            for n, (dst, ro) in enumerate(slots):
                last = t.matmul(psX[:, 960 + n:960 + n + 1], bcones[:, :],
                                scal[0:1, SL["row"] + ro:SL["row"] + ro + 1],
                                start=True, stop=True)
            return last
        P.op("tensor", bcmm, [f"{name}_vs"], signal=f"{name}_bm")

        def bccp(v):
            last = None
            for n, (dst, ro) in enumerate(slots):
                last = v.tensor_copy(scal[:, dst:dst + 1],
                                     psX[:, 960 + n:960 + n + 1])
            vsync(v, last)
            return v.nop()
        P.op("vector", bccp, [f"{name}_bm"], signal=f"{name}_bc")

    mbarrier(0, "m1", SL["m1a"], False, [f"ln_y{TT-1}"])
    mb_bcast("m1", [(SL["s1"], 1), (SL["r1"], 2)])

    for tt in range(TT):
        def emit_xq(v, tt=tt):
            v.tensor_scalar(lnsq[:, :], y32[:, tt, :], scal[:, SL["r1"]:SL["r1"] + 1],
                            RC, op0=ALU.mult, op1=ALU.add)
            return v.tensor_scalar(xq16[:, tt, :], lnsq[:, :], -RC,
                                   scal[:, SL["s1"]:SL["s1"] + 1],
                                   op0=ALU.add, op1=ALU.mult)
        P.op("vector", emit_xq, ["m1_bc"] if tt == 0 else [], signal=f"xq{tt}")
    dma_s("xq_spill", xre(xqd), xq16[:, :, :], waits=[f"xq{TT-1}"], ch="sp")

    for ft in range(FT):
        dma_s(f"ytq{ft}", ytq[:, ft, :], xqd[:, 128 * ft:128 * ft + 128],
              waits=["xq_spill"], tr=True, ch="ytq")
    for ft in range(FT):
        dma_s(f"wtgG{ft}", wtg[:, ft, :], wg16d[:, 128 * ft:128 * ft + 128],
              waits=[f"dq_st{IT-2}", f"dq_st{IT-1}"], tr=True, ch="wgG")

    def bank4(n):
        return (psQ if (n % 4) < 2 else psK)[:, 512 * (n % 2):512 * (n % 2) + 512]

    nmm = 0
    for wsel in range(2):
        for it in range(IT):
            for tc in range(TC):
                def emit_ug(t, it=it, tc=tc, nmm=nmm):
                    bk = bank4(nmm)
                    last = None
                    for ft in range(FT):
                        last = t.matmul(bk, wtg[:, ft, 128 * it:128 * it + 128],
                                        ytq[:, ft, 512 * tc:512 * tc + 512],
                                        start=(ft == 0), stop=(ft == FT - 1))
                    return last
                w = []
                if nmm == 0:
                    w = [f"ytq{FT-1}", f"wtgG{FT-1}", f"wo_ev{TT-1}"]
                if wsel == 1 and it == 0 and tc == 0:
                    w += [f"wtgU{FT-1}"]
                if nmm >= 4:
                    w += [f"ug_ev{nmm-4}"]
                P.op("tensor", emit_ug, w, signal=f"ug_mm{nmm}")

                def emit_ugev(a, wsel=wsel, it=it, tc=tc, nmm=nmm):
                    stg = stg1 if wsel == 0 else stg2
                    return a.activation(stg[:, it % 2, 512 * tc:512 * tc + 512],
                                        bank4(nmm), AF.Copy)
                wev = [f"ug_mm{nmm}"]
                if nmm >= 2 * TC:
                    wev += [f"ug_mx{nmm - 2 * TC}", f"rawsp{(nmm - 2 * TC) // (IT * TC)}_{((nmm - 2 * TC) // TC) % IT}"]
                P.op("scalar", emit_ugev, wev, signal=f"ug_ev{nmm}")

                def emit_ugmx(v, wsel=wsel, it=it, tc=tc):
                    stg = stg1 if wsel == 0 else stg2
                    acc = SL["m3a"] if wsel == 0 else SL["m2a"]
                    vreduce(v, scal[:, SL["tmp"]:SL["tmp"] + 1],
                            stg[:, it % 2, 512 * tc:512 * tc + 512],
                            axis=AX.X, op=ALU.max, apply_absolute_value=True)
                    return v.tensor_tensor(scal[:, acc:acc + 1], scal[:, acc:acc + 1],
                                           scal[:, SL["tmp"]:SL["tmp"] + 1], op=ALU.max)
                P.op("vector", emit_ugmx, [f"ug_ev{nmm}"], signal=f"ug_mx{nmm}")
                nmm += 1
            stgd = graw if wsel == 0 else uraw
            stgs = stg1 if wsel == 0 else stg2
            dma_s(f"rawsp{wsel}_{it}", xre(stgd)[:, it, :], stgs[:, it % 2, :],
                  waits=[f"ug_ev{nmm-1}"], ch="rwG" if wsel == 0 else "rwU")
        if wsel == 0:
            for ft in range(FT):
                dma_s(f"wtgU{ft}", wtg[:, ft, :], wu16d[:, 128 * ft:128 * ft + 128],
                      waits=[f"ug_mm{IT * TC - 1}", f"dq_st{2*IT-2}", f"dq_st{2*IT-1}"],
                      tr=True, ch="wgU")

    mbarrier(1, "m3", SL["m3a"], True, [f"ug_mx{IT * TC - 1}"])
    mb_bcast("m3", [(SL["s3"], 1), (SL["r3"], 2), (SL["c13"], 3)])
    mbarrier(2, "m2", SL["m2a"], False, [f"ug_mx{2 * IT * TC - 1}"])
    mb_bcast("m2", [(SL["s2"], 1), (SL["r2"], 2)])

    def emit_s1a(v, it):
        p = it % 2
        v.tensor_scalar(fA[:, p, :], stg1[:, p, :], scal[:, SL["r3"]:SL["r3"] + 1],
                        RC, op0=ALU.mult, op1=ALU.add)
        return v.tensor_scalar(fA[:, p, :], fA[:, p, :], -RC, None, op0=ALU.add)

    def emit_sig(a, it):
        p = it % 2
        a.activation(fB[:, p, :], fA[:, p, :], AF.Sigmoid,
                     scale=scal[:, SL["s3"]:SL["s3"] + 1])
        return a.activation(fB[:, p, :], fB[:, p, :], AF.Copy,
                            scale=65536.0, bias=RC)

    def emit_s1b(v, it):
        p = it % 2
        v.tensor_scalar(fB[:, p, :], fB[:, p, :], -RC, 65535.0, op0=ALU.add, op1=ALU.min)
        v.scalar_tensor_tensor(stg3[:, p, :], fA[:, p, :],
                               scal[:, SL["c13"]:SL["c13"] + 1],
                               fB[:, p, :], op0=ALU.mult, op1=ALU.mult)
        vreduce(v, scal[:, SL["tmp"]:SL["tmp"] + 1], stg3[:, p, :],
                axis=AX.X, op=ALU.max, apply_absolute_value=True)
        return v.tensor_tensor(scal[:, SL["m4a"]:SL["m4a"] + 1],
                               scal[:, SL["m4a"]:SL["m4a"] + 1],
                               scal[:, SL["tmp"]:SL["tmp"] + 1], op=ALU.max)

    def s1_dma(it):
        w = [f"rawsp0_{IT-1}"] + ([f"s1a{it-2}"] if it >= 2 else [])
        dma_s(f"s1g{it}", stg1[:, it % 2, :], xre(graw)[:, it, :], waits=w,
              ch=f"fI{it % 2}")

    for it in range(IT):
        if it >= 2:
            dma_s(f"s1p{it-2}", xre(prodd)[:, it - 2, :],
                  stg3[:, (it - 2) % 2, :], waits=[f"s1v{it-2}"], ch=f"fO{(it-2) % 2}")
        s1_dma(it)
        w = [f"s1g{it}", "m3_bc"] + ([f"s1sg{it-2}"] if it >= 2 else [])
        P.op("vector", lambda v, it=it: emit_s1a(v, it), w, signal=f"s1a{it}")
        ws = [f"s1a{it}"] + ([f"s1v{it-2}"] if it >= 2 else [])
        P.op("scalar", lambda a, it=it: emit_sig(a, it), ws, signal=f"s1sg{it}")
        if it >= 1:
            pit = it - 1
            w = [f"s1sg{pit}"] + ([f"s1p{pit-2}"] if pit >= 2 else [])
            P.op("vector", lambda v, it=pit: emit_s1b(v, it), w, signal=f"s1v{pit}")
    pit = IT - 1
    P.op("vector", lambda v, it=pit: emit_s1b(v, it),
         [f"s1sg{pit}", f"s1p{pit-2}"], signal=f"s1v{pit}")
    for it in (IT - 2, IT - 1):
        dma_s(f"s1p{it}", xre(prodd)[:, it, :], stg3[:, it % 2, :],
              waits=[f"s1v{it}"], ch=f"fO{it % 2}")

    mbarrier(3, "m4", SL["m4a"], False, [f"s1v{IT-1}"])
    mb_bcast("m4", [(SL["s4"], 1), (SL["r4"], 2)])

    for it in range(IT):
        p = it % 2
        q4 = it % 4
        w = [f"s1p{IT-2}", f"s1p{IT-1}"]
        if it >= 4:
            w += [f"s2r{it-4}"]
        dma_s(f"s2i{it}", stg3[:, q4, :], xre(prodd)[:, it, :], waits=w,
              ch=f"fI{p}")
        w2 = [f"rawsp1_{IT-1}"]
        if it >= 4:
            w2 += [f"s2v{it-4}"]
        dma_s(f"s2u{it}", stg2[:, q4, :], xre(uraw)[:, it, :], waits=w2,
              ch=f"fJ{p}")

        def emit_s2r(a, it=it, p=p, q4=q4):
            a.activation(fB[:, p, :], stg2[:, q4, :], AF.Copy,
                         scale=scal[:, SL["r2"]:SL["r2"] + 1], bias=RC)
            return a.activation(fA[:, p, :], stg3[:, q4, :], AF.Copy,
                                scale=scal[:, SL["r4"]:SL["r4"] + 1], bias=RC)
        wr = [f"s2i{it}", f"s2u{it}", "m4_bc", "m2_bc"]
        if it >= 2:
            wr += [f"s2v{it-2}"]
        else:
            wr += [f"s1v{IT-1}"]
        P.op("scalar", emit_s2r, wr, signal=f"s2r{it}")

        def emit_s2(v, it=it, p=p, q4=q4):
            v.tensor_scalar(stg2[:, q4, :], fB[:, p, :], -RC,
                            scal[:, SL["s2"]:SL["s2"] + 1],
                            op0=ALU.add, op1=ALU.mult)
            v.tensor_scalar(fA[:, p, :], fA[:, p, :], -RC, scal[:, SL["s4"]:SL["s4"] + 1],
                            op0=ALU.add, op1=ALU.mult)
            v.tensor_tensor(stg1[:, p, :], fA[:, p, :], stg2[:, q4, :], op=ALU.mult)
            vreduce(v, scal[:, SL["tmp"]:SL["tmp"] + 1], stg1[:, p, :],
                    axis=AX.X, op=ALU.max, apply_absolute_value=True)
            return v.tensor_tensor(scal[:, SL["m5a"]:SL["m5a"] + 1],
                                   scal[:, SL["m5a"]:SL["m5a"] + 1],
                                   scal[:, SL["tmp"]:SL["tmp"] + 1], op=ALU.max)
        w3 = [f"s2r{it}"]
        if it >= 2:
            w3 += [f"s2o{it-2}"]
        P.op("vector", emit_s2, w3, signal=f"s2v{it}")
        dma_s(f"s2o{it}", xre(qsd)[:, it, :], stg1[:, p, :], waits=[f"s2v{it}"],
              ch=f"fO{p}")

    mbarrier(4, "m5", SL["m5a"], False, [f"s2v{IT-1}"])
    mb_bcast("m5", [(SL["s5"], 1), (SL["r5"], 2)])

    def bank8(idx):
        tensors = [psQ, psQ, psK, psK, psC, psC, psX, psX]
        return tensors[idx][:, 512 * (idx % 2):512 * (idx % 2) + 512]

    for tg in range(TG):
        for it in range(IT):
            gi = tg * IT + it
            w = [f"s2o{IT-2}", f"s2o{IT-1}"]
            if gi >= 2:
                w += [f"s3v{gi-2}"]
            else:
                w += [f"s2v{IT-1}"]
            dma_s(f"s3i{gi}", stg1[:, it % 2, 0:512],
                  xre(qsd)[:, it, 512 * tg:512 * tg + 512], waits=w,
                  ch=f"fI{gi % 2}")
            ww = [f"dq_st{ndq-2}", f"dq_st{ndq-1}"]
            if gi >= 2:
                ww += [f"dn_it{(gi-2)//IT}_{(gi-2)%IT}"]
            dma_s(f"s3w{gi}", wtd[:, it % 2, :], wd16d[:, 128 * it:128 * it + 128],
                  waits=ww, tr=True, ch=f"fJ{gi % 2}")

            def emit_s3(v, it=it, gi=gi, tg=tg):
                v.tensor_scalar(fA[:, gi % 2, 0:512], stg1[:, it % 2, 0:512],
                                scal[:, SL["r5"]:SL["r5"] + 1], RC,
                                op0=ALU.mult, op1=ALU.add)
                return v.tensor_scalar(ot[:, it, :], fA[:, gi % 2, 0:512], -RC,
                                       scal[:, SL["s5"]:SL["s5"] + 1],
                                       op0=ALU.add, op1=ALU.mult)
            w2 = [f"s3i{gi}", "m5_bc"]
            if tg >= 1:
                w2 += [f"dn_it{tg-1}_{IT-1}"]
            P.op("vector", emit_s3, w2, signal=f"s3v{gi}")

        for it in range(IT):
            gi = tg * IT + it

            def emit_dn(t, tg=tg, it=it, gi=gi):
                last = None
                for q in range(4):
                    for ch in range(2):
                        last = t.matmul(bank8(2 * q + ch),
                                        ot[:, it, 128 * q:128 * q + 128],
                                        wtd[:, it % 2, 512 * ch:512 * ch + 512],
                                        start=(it == 0), stop=(it == IT - 1))
                return last
            w = [f"s3v{gi}", f"s3w{gi}"]
            if it == 0:
                if tg == 0:
                    w += [f"ug_ev{2 * IT * TC - 1}", f"cx_ev{2 * B * CH - 1}",
                          f"at_ex{n_groups-1}"]
                else:
                    w += [f"dn_ev{4*(tg-1)+q}" for q in range(4)]
            P.op("tensor", emit_dn, w, signal=f"dn_it{tg}_{it}")

        for q in range(4):
            tt = 4 * tg + q

            def emit_dnev(a, q=q, tt=tt):
                a.activation(dnsc[:, tt % 4, 0:512], bank8(2 * q), AF.Copy)
                return a.activation(dnsc[:, tt % 4, 512:1024], bank8(2 * q + 1), AF.Copy)
            w = [f"dn_it{tg}_{IT-1}"]
            if tt >= 4:
                w += [f"o_y{tt-4}"]
            P.op("scalar", emit_dnev, w, signal=f"dn_ev{tt}")

            w = ["xq_spill"] + ([f"o_res{tt-2}"] if tt >= 2 else [])
            dma_s(f"yr{tt}", yrl[:, tt % 2, :], xre(yd)[:, tt, :], waits=w,
                  ch=f"yr{tt % 2}")
        for q in range(4):
            tt = 4 * tg + q

            def emit_ores(v, tt=tt):
                v.tensor_tensor(dnsc[:, tt % 4, :], dnsc[:, tt % 4, :],
                                yrl[:, tt % 2, :], op=ALU.add)
                return v.reduce_sum(lnst[:, 2 * (tt % 4):2 * (tt % 4) + 1],
                                    dnsc[:, tt % 4, :], axis=AX.X)
            P.op("vector", emit_ores, [f"dn_ev{tt}", f"yr{tt}"], signal=f"o_res{tt}")
        for q in range(4):
            tt = 4 * tg + q

            def emit_osq(a, tt=tt):
                return a.activation(lnsq[:, :], dnsc[:, tt % 4, :], AF.Square,
                                    accum_out=lnst[:, 2 * (tt % 4) + 1:2 * (tt % 4) + 2])
            P.op("scalar", emit_osq, [f"o_res{tt}"], signal=f"o_sq{tt}")
        for q in range(4):
            tt = 4 * tg + q

            def emit_ovar(v, tt=tt):
                st0 = 2 * (tt % 4)
                m = lnst[:, st0:st0 + 1]
                s2 = lnst[:, st0 + 1:st0 + 2]
                v.tensor_scalar(m, m, 1.0 / D, None, op0=ALU.mult)
                v.tensor_tensor(scal[:, SL["tmp"]:SL["tmp"] + 1], m, m, op=ALU.mult)
                v.tensor_scalar(s2, s2, 1.0 / D, None, op0=ALU.mult)
                v.tensor_tensor(s2, s2, scal[:, SL["tmp"]:SL["tmp"] + 1],
                                op=ALU.subtract)
                return v.tensor_scalar(s2, s2, 1e-5, None, op0=ALU.add)
            P.op("vector", emit_ovar, [f"o_sq{tt}"], signal=f"o_var{tt}")
        for q in range(4):
            tt = 4 * tg + q

            def emit_ors(a, tt=tt):
                st0 = 2 * (tt % 4)
                return a.activation(lnst[:, st0 + 1:st0 + 2], lnst[:, st0 + 1:st0 + 2],
                                    AF.Sqrt)
            P.op("scalar", emit_ors, [f"o_var{tt}"], signal=f"o_rs{tt}")
        for q in range(4):
            tt = 4 * tg + q

            def emit_oy(v, tt=tt):
                st0 = 2 * (tt % 4)
                vrecip(v, lnst[:, st0 + 1:st0 + 2], lnst[:, st0 + 1:st0 + 2])
                return v.tensor_scalar(outb[:, tt % 2, :], dnsc[:, tt % 4, :],
                                       lnst[:, st0:st0 + 1], lnst[:, st0 + 1:st0 + 2],
                                       op0=ALU.subtract, op1=ALU.mult)
            w = [f"o_rs{tt}"] + ([f"ost{tt-2}"] if tt >= 2 else [])
            P.op("vector", emit_oy, w, signal=f"o_y{tt}")
            dma_s(f"ost{tt}", xre(out)[:, tt, :], outb[:, tt % 2, :], waits=[f"o_y{tt}"],
                  ch=f"os{tt % 2}")

    P.op("sync", lambda s: s.nop(), [f"ost{TT-2}", f"ost{TT-1}"], signal=None)

    chans = ["vself", "ini", "cx", "cT", "sp", "ytq", "wgG", "wgU", "rwG", "rwU",
             "xm0", "xm1", "dqL0", "dqL1", "dqS0", "dqS1", "fI0", "fI1",
             "fJ0", "fJ1", "fO0", "fO1", "yr0", "yr1", "os0", "os1",
             "gW", "gM", "g0", "g1", "cc",
             "c_tensor", "c_scalar", "c_vector", "c_gpsimd", "c_sync"]
    import contextlib
    with contextlib.ExitStack() as stack:
        sems = {k: stack.enter_context(nc.semaphore(k)) for k in chans}
        vself["sem"] = sems["vself"]
        block = stack.enter_context(nc.Block())
        engines = {}

        def reg(name):
            def f(e):
                engines[name] = e
                run(name)
            return f
        run = P.build_and_emit(sems, engines)
        block.sync(reg("sync"))
        block.gpsimd(reg("gpsimd"))
        block.tensor(reg("tensor"))
        block.scalar(reg("scalar"))
        block.vector(reg("vector"))
    return nc


def _in_maps(cfg, inputs):
    st = np.asarray(inputs["L0_attn_states"], dtype=np.float32)
    B, S, D = st.shape
    flat = np.ascontiguousarray(st.reshape(B * S, D))
    xt = np.ascontiguousarray(flat.T)
    wq = np.asarray(inputs["L0_attn_Wq"], np.float32)
    wk = np.asarray(inputs["L0_attn_Wk"], np.float32)
    wv = np.asarray(inputs["L0_attn_Wv"], np.float32)
    wo = np.ascontiguousarray(np.asarray(inputs["L0_attn_Wo"], np.float32))
    wgn = np.ascontiguousarray(np.asarray(inputs["L0_ffn_w_gate"], np.float32))
    wun = np.ascontiguousarray(np.asarray(inputs["L0_ffn_w_up"], np.float32))
    wdn = np.ascontiguousarray(np.asarray(inputs["L0_ffn_w_down"], np.float32))
    tri = np.triu(np.ones((128, 128), np.float32))
    maskc = ((tri - 1.0) * 30000.0).astype(np.float16)
    identc = np.eye(128, dtype=np.float16)
    TM = cfg.TM
    maps = []
    for c in range(8):
        maps.append(dict(
            xt=xt,
            xm=np.ascontiguousarray(flat[TM * c:TM * (c + 1)]),
            wq=np.ascontiguousarray(wq[:, 128 * c:128 * c + 128]),
            wk=np.ascontiguousarray(wk[:, 128 * c:128 * c + 128]),
            wv=np.ascontiguousarray(wv[:, 128 * c:128 * c + 128]),
            wo=wo, wgn=wgn, wun=wun, wdn=wdn,
            maskc=maskc, identc=identc,
        ))
    return maps


def kernel(**inputs):
    st = np.asarray(inputs["L0_attn_states"])
    B, S, D = st.shape
    cfg = CFG(S=S)
    nc = build_nc(cfg)
    maps = _in_maps(cfg, inputs)
    res = run_bass_kernel_spmd(nc, maps, core_ids=list(range(8)))
    outp = np.concatenate([np.asarray(r["out"]) for r in res.results], axis=0)
    return outp.reshape(B, S, D).astype(np.float32)


# version 06
# speedup vs baseline: 1.2343x; 1.1251x over previous
import os
os.environ.setdefault("TRN_ALLOW_ICI_ALL_TO_ALL", "1")

import numpy as np
import concourse.bass as bass
import concourse.mybir as mybir
from concourse.bass_utils import run_bass_kernel_spmd

F32 = mybir.dt.float32
F16 = mybir.dt.float16
AF = mybir.ActivationFunctionType
ALU = mybir.AluOpType
AX = mybir.AxisListType

RC = 12582912.0
IQA = 1.0 / 32767.0
IQW = 1.0 / 7.0


class CFG:
    def __init__(self, S=4096, debug=False):
        self.debug = debug
        self.B = 4
        self.S = S
        self.D = 1024
        self.I = 3072
        self.TM = self.B * S // 8
        self.CH = S // 512
        self.TT = self.TM // 128
        self.TG = self.TM // 512
        self.IT = self.I // 128
        self.FT = self.D // 128
        self.TC = self.TM // 512


class Prog:
    ENGINES = ("sync", "gpsimd", "tensor", "scalar", "vector")

    def __init__(self):
        self.ops = []

    def op(self, engine, emit, waits=(), signal=None, amount=1, sem=None):
        self.ops.append((engine, emit, tuple(waits), signal, amount, sem))

    def build_and_emit(self, sems, engines):
        counts = {k: 0 for k in sems}
        table = {}
        for engine, emit, waits, signal, amount, semk in self.ops:
            if signal is not None:
                k = semk if semk is not None else "c_" + engine
                counts[k] += amount
                assert signal not in table, f"dup event {signal}"
                table[signal] = (k, counts[k])
        for engine, emit, waits, signal, amount, semk in self.ops:
            for w in waits:
                assert w in table, f"missing event {w}"
        watermark = {e: {k: 0 for k in sems} for e in self.ENGINES}

        def run(engine):
            eng_obj = engines[engine]
            for e2, emit, waits, signal, amount, semk in self.ops:
                if e2 != engine:
                    continue
                need = {}
                for w in waits:
                    k, v = table[w]
                    need[k] = max(need.get(k, 0), v)
                for k, v in need.items():
                    if watermark[engine][k] < v:
                        eng_obj.wait_ge(sems[k], v)
                        watermark[engine][k] = v
                inst = emit(eng_obj)
                if signal is not None:
                    k = semk if semk is not None else "c_" + engine
                    assert inst is not None, f"event {signal} has no instruction"
                    inst.then_inc(sems[k], amount)
        return run


def mem_map(cfg):
    M, o = {}, 17408

    def al(name, nbytes, at=None):
        nonlocal o
        if at is not None:
            M[name] = at
            return at
        M[name] = o
        o += (nbytes + 31) // 32 * 32
        return M[name]

    al("ident", 256); al("maskb", 256)
    al("wq16", cfg.FT * 256); al("wk16", cfg.FT * 256); al("wv16", cfg.FT * 256)
    al("wo16", cfg.FT * 2048)
    al("scal", 256)
    al("dqs", 128); al("dqr", 128)
    al("bcrow", 512); al("bcones", 512)
    al("lnsq", 4096)
    al("lnst", 64)
    al("lnst2", 64)
    al("dqz", 4096)
    al("dqw32", 2 * 4096)
    al("dq16", 2 * 2048)
    P1 = o
    al("xtb", 2 * cfg.FT * 1024)
    al("qt", 2 * cfg.S * 2)
    al("ktp0", 2 * cfg.S * 2)
    al("ktp1", 2 * cfg.S * 2)
    al("vtt", 2 * (cfg.S // 128) * 264)
    al("expst", 6 * 2048)
    al("ctxq", 2 * (cfg.S // 128) * 256)
    p1_end = o
    al("ctxT", 0, at=P1)
    al("xq16", 0, at=P1)
    ph2 = P1 + max(cfg.FT * cfg.TM * 2, cfg.TT * 2048)
    al("y32", cfg.TT * 4096, at=ph2); ph2 += cfg.TT * 4096
    al("xmst", 0, at=ph2); ph2 += 2 * 4096
    al("lnsc", 0, at=ph2); ph2 += 2 * 4096
    al("ytq", 0, at=P1)
    al("ot", 0, at=P1)
    base3 = P1 + max(cfg.FT * cfg.TM * 2, cfg.IT * 1024)
    al("wtg", 0, at=base3); base3 += cfg.FT * 6144
    al("wtd", 4 * 2048, at=base3); base3 += 4 * 2048
    al("stg1", 0, at=base3); base3 += 2 * cfg.TM * 2
    al("fA", 0, at=base3); base3 += 2 * cfg.TM * 4
    dead0 = base3
    al("stg2", 0, at=base3); base3 += 4 * cfg.TM * 2
    al("stg3", 0, at=base3); base3 += 4 * cfg.TM * 2
    al("fB", 0, at=base3); base3 += 2 * cfg.TM * 4
    o3 = dead0
    al("yrl", 0, at=o3); o3 += 2 * 4096
    al("outb", 0, at=o3); o3 += 2 * 4096
    al("dnsc", 0, at=o3); o3 += 4 * 4096
    M["_END"] = max(p1_end, ph2, base3, o3)
    return M


def build_nc(cfg):
    nc = bass.Bass("TRN2", target_bir_lowering=False,
                   detect_race_conditions=False)
    B, S, D, I = cfg.B, cfg.S, cfg.D, cfg.I
    TM, CH, TT, TG, IT, FT, TC = (cfg.TM, cfg.CH, cfg.TT, cfg.TG, cfg.IT,
                                  cfg.FT, cfg.TC)
    M = mem_map(cfg)
    assert M["_END"] <= 229344, M["_END"]

    xt = nc.dram_tensor("xt", [D, B * S], F32, kind="ExternalInput").ap()
    xm = nc.dram_tensor("xm", [TM, D], F32, kind="ExternalInput").ap()
    wq = nc.dram_tensor("wq", [D, 128], F32, kind="ExternalInput").ap()
    wk = nc.dram_tensor("wk", [D, 128], F32, kind="ExternalInput").ap()
    wv = nc.dram_tensor("wv", [D, 128], F32, kind="ExternalInput").ap()
    wo = nc.dram_tensor("wo", [D, D], F32, kind="ExternalInput").ap()
    wgn = nc.dram_tensor("wgn", [I, D], F32, kind="ExternalInput").ap()
    wun = nc.dram_tensor("wun", [I, D], F32, kind="ExternalInput").ap()
    wdn = nc.dram_tensor("wdn", [D, I], F32, kind="ExternalInput").ap()
    maskc = nc.dram_tensor("maskc", [128, 128], F16, kind="ExternalInput").ap()
    identc = nc.dram_tensor("identc", [128, 128], F16, kind="ExternalInput").ap()
    out = nc.dram_tensor("out", [TM, D], F32, kind="ExternalOutput").ap()

    dbg = "ExternalOutput" if cfg.debug else "Internal"
    a2a_i = nc.dram_tensor("a2a_i", [8, TM, 128], F16)
    a2a_o = nc.dram_tensor("a2a_o", [8, TM, 128], F16)
    mxi = [nc.dram_tensor(f"mxi{k}", [1, 1], F32) for k in range(5)]
    mxo = [nc.dram_tensor(f"mxo{k}", [1, 1], F32) for k in range(5)]
    mxp = nc.dram_tensor("mxp", [128], F32)
    wg16d = nc.dram_tensor("wg16d", [I, D], F16, kind=dbg)
    wu16d = nc.dram_tensor("wu16d", [I, D], F16, kind=dbg)
    wd16d = nc.dram_tensor("wd16d", [D, I], F16, kind=dbg)
    graw = nc.dram_tensor("graw", [I, TM], F16, kind=dbg)
    uraw = nc.dram_tensor("uraw", [I, TM], F16, kind=dbg)
    uqd = nc.dram_tensor("uqd", [I, TM], F16, kind=dbg)
    prodd = nc.dram_tensor("prodd", [I, TM], F16, kind=dbg)
    qsd = nc.dram_tensor("qsd", [I, TM], F16, kind=dbg)
    xqd = nc.dram_tensor("xqd", [TM, D], F16, kind=dbg)
    yd = nc.dram_tensor("yd", [TM, D], F32, kind=dbg)

    def t16(name, *shape, key=None):
        return nc.alloc_sbuf_tensor_at(name, [128, *shape], F16, offset=M[key or name])

    def t32(name, *shape, key=None):
        return nc.alloc_sbuf_tensor_at(name, [128, *shape], F32, offset=M[key or name])

    ident = t16("ident", 128)
    maskb = t16("maskb", 128)
    wq16 = t16("wq16", FT, 128)
    wk16 = t16("wk16", FT, 128)
    wv16 = t16("wv16", FT, 128)
    wo16 = t16("wo16", FT, 1024)
    scal = t32("scal", 64)
    dqs = t32("dqs", 32)
    dqr = t32("dqr", 32)
    bcrow = nc.alloc_sbuf_tensor_at("bcrow", [1, 128], F32, offset=M["bcrow"])
    bcones = nc.alloc_sbuf_tensor_at("bcones", [1, 128], F32, offset=M["bcones"])
    xtb = t16("xtb", 2, FT, 512)
    qt = t16("qt", 2, S)
    ktp0 = t16("ktp0", 2, S)
    ktp1 = t16("ktp1", 2, S)
    vtt = t16("vtt", 2, S // 128, 132)
    expst = t16("expst", 6, 1024)
    ctxq = t16("ctxq", 2, S // 128, 128)
    dqz = t32("dqz", 1024)
    dqw32 = t32("dqw32", 2, 1024)
    dq16 = t16("dq16", 2, 1024)
    ctxT = t16("ctxT", FT, TM)
    xq16 = t16("xq16", TT, 1024)
    y32 = t32("y32", TT, 1024)
    xmst = t32("xmst", 2, 1024)
    lnsc = t32("lnsc", 2, 1024)
    lnsq = t32("lnsq", 1024)
    lnst = t32("lnst", 16)
    lnst2 = t32("lnst2", 16)
    ytq = t16("ytq", FT, TM)
    ot = t16("ot", IT, 512)
    wtg = t16("wtg", FT, 3072)
    wtd = t16("wtd", 4, 1024)
    stg1 = t16("stg1", 2, TM)
    stg2 = t16("stg2", 4, TM)
    stg3 = t16("stg3", 4, TM)
    fA = t32("fA", 2, TM)
    fB = t32("fB", 2, TM)
    yrl = t32("yrl", 2, 1024)
    outb = t32("outb", 2, 1024)
    dnsc = t32("dnsc", 4, 1024)

    psQ = nc.alloc_psum_tensor("psQ", [128, 1024], F32)
    psK = nc.alloc_psum_tensor("psK", [128, 1024], F32)
    psC = nc.alloc_psum_tensor("psC", [128, 1024], F32)
    psX = nc.alloc_psum_tensor("psX", [128, 1024], F32)

    SL = dict(r1=0, s1=1, r3=2, s3=3, c13=4, r2=5, s2=6, r4=7, s4=8, r5=9,
              s5=10, row=16, m1a=24, m3a=25, m2a=26, m4a=27, m5a=28, tmp=29)

    P = Prog()
    vself = {"n": 0, "sem": None}

    def vsync(v, inst):
        vself["n"] += 1
        inst.then_inc(vself["sem"], 1)
        v.wait_ge(vself["sem"], vself["n"])

    def vreduce(v, out, in_, **kw):
        inst = v.tensor_reduce(out, in_, **kw)
        vsync(v, inst)
        return inst

    def vrecip(v, out, in_):
        inst = v.reciprocal(out, in_)
        vsync(v, inst)
        return inst

    def dma_g(name, o_, i_, waits=(), ch="gW", tr=False):
        P.op("gpsimd",
             lambda g, o_=o_, i_=i_: g.dma_start(o_, i_, transpose=tr) if tr
             else g.dma_start(out=o_, in_=i_),
             waits, signal=name, amount=16, sem=ch)

    def dma_s(name, o_, i_, waits=(), tr=False, ch="ini"):
        if tr:
            P.op("sync", lambda s, o_=o_, i_=i_: s.dma_start(o_, i_, transpose=True),
                 waits, signal=name, amount=16, sem=ch)
        else:
            P.op("sync", lambda s, o_=o_, i_=i_: s.dma_start(out=o_, in_=i_),
                 waits, signal=name, amount=16, sem=ch)

    def xre(dram_ap, p=128):
        return dram_ap.rearrange("(a p) n -> p a n", p=p)

    dma_s("ld_mask", maskb[:, :], maskc)
    dma_s("ld_ident", ident[:, :], identc)
    dma_g("ld_wq", wq16[:, :, :], xre(wq))
    dma_g("ld_wk", wk16[:, :, :], xre(wk))
    dma_g("ld_wv", wv16[:, :, :], xre(wv))
    dma_g("ld_wo", wo16[:, :, :], xre(wo))

    def v_init(v):
        v.memset(bcones[:, :], 1.0)
        v.memset(ktp0[64:128, :, :], 0.0)
        v.memset(ktp1[0:64, :, :], 0.0)
        v.memset(vtt[:, :, :, 64:65], 1.0)
        v.memset(vtt[:, :, :, 129:130], 1.0)
        return v.memset(scal[:, SL["m1a"]:SL["m5a"] + 1], 0.0)
    P.op("vector", v_init, (), signal="v_init")

    nchunks = B * CH
    for i in range(nchunks):
        b, c = divmod(i, CH)
        w = [f"pj_mm{i-2}"] if i >= 2 else []
        dma_g(f"ld_xt{i}", xtb[:, i % 2, :, :],
              xre(xt)[:, :, b * S + 512 * c: b * S + 512 * c + 512], waits=w,
              ch=f"g{i % 2}")

    ndq = IT + IT + 3 * (D // 128)

    def dq_src(i):
        if i < IT:
            return xre(wgn)[:, i, :], xre(wg16d)[:, i, :]
        if i < 2 * IT:
            return xre(wun)[:, i - IT, :], xre(wu16d)[:, i - IT, :]
        r, cc = divmod(i - 2 * IT, 3)
        return (xre(wdn)[:, r, 1024 * cc:1024 * cc + 1024],
                xre(wd16d)[:, r, 1024 * cc:1024 * cc + 1024])

    def emit_dq_ld(i):
        w = [f"dq_done{i-2}"] if i >= 2 else []
        dma_s(f"dq_ld{i}", dqw32[:, i % 2, :], dq_src(i)[0], waits=w,
              ch=f"dqL{i % 2}")

    def emit_dq_st(i):
        dma_s(f"dq_st{i}", dq_src(i)[1], dq16[:, i % 2, :], waits=[f"dq_done{i}"],
              ch=f"dqS{i % 2}")

    dq_queue = []
    for i in range(ndq):
        def dq_ops(v, i=i):
            wt = dqw32[:, i % 2, :]
            for hh in range(2):
                vreduce(v, dqs[:, 16 * hh:16 * hh + 16].unsqueeze(2),
                        wt[:, 512 * hh:512 * hh + 512].rearrange("p (a n) -> p a n", n=32),
                        axis=AX.X, op=ALU.max, apply_absolute_value=True)
            vsync(v, v.tensor_scalar(dqs[:, :], dqs[:, :], IQW, 1e-12,
                                     op0=ALU.mult, op1=ALU.max))
            vrecip(v, dqr[:, :], dqs[:, :])
            v.tensor_tensor(dqz[:, :].rearrange("p (a n) -> p a n", n=32),
                            wt.rearrange("p (a n) -> p a n", n=32),
                            dqr[:, :].unsqueeze(2).broadcast_to([128, 32, 32]),
                            op=ALU.mult)
            v.tensor_scalar(dqz[:, :], dqz[:, :], RC, None, op0=ALU.add)
            return v.scalar_tensor_tensor(
                dq16[:, i % 2, :].rearrange("p (a n) -> p a n", n=32),
                dqz[:, :].rearrange("p (a n) -> p a n", n=32), -RC,
                dqs[:, :].unsqueeze(2).broadcast_to([128, 32, 32]),
                op0=ALU.add, op1=ALU.mult)
        wts = [f"dq_ld{i}"] + ([f"dq_st{i-2}"] if i >= 2 else [])
        dq_queue.append((dq_ops, wts, f"dq_done{i}"))

    dq_state = {"pop": 0, "ld": 0}

    def pop_dq(n):
        for _ in range(n):
            if dq_state["pop"] >= ndq:
                return
            i = dq_state["pop"]
            dq_state["pop"] += 1
            while dq_state["ld"] <= min(i + 2, ndq - 1):
                emit_dq_ld(dq_state["ld"])
                dq_state["ld"] += 1
            fn, w, sig = dq_queue[i]
            P.op("vector", fn, w, signal=sig)
            emit_dq_st(i)

    def av_targets(c, kt):
        return range(max(0, kt - 4 * c), 4)

    def ctx_off(m):
        return 65 * m if m < 7 else 512

    nsc = 0
    for b in range(B):
        for c in range(CH):
            i = b * CH + c

            def emit_pj(t, i=i, b=b, c=c):
                qb = psQ[:, 512 * (i % 2):512 * (i % 2) + 512]
                kb = psK[:, 512 * (i % 2):512 * (i % 2) + 512]
                vb = psC[:, 512 * (i % 2):512 * (i % 2) + 512]
                for ft in range(FT):
                    t.matmul(qb, wq16[:, ft, :], xtb[:, i % 2, ft, :],
                             start=(ft == 0), stop=(ft == FT - 1))
                for ft in range(FT):
                    t.matmul(kb, wk16[:, ft, :], xtb[:, i % 2, ft, :],
                             start=(ft == 0), stop=(ft == FT - 1))
                last = None
                for q in range(4):
                    for ft in range(FT):
                        last = t.matmul(vb[:, 128 * q:128 * q + 128],
                                        xtb[:, i % 2, ft, 128 * q:128 * q + 128],
                                        wv16[:, ft, :],
                                        start=(ft == 0), stop=(ft == FT - 1))
                return last
            w = [f"ld_xt{i}"]
            if i == 0:
                w += ["ld_wq", "ld_wk", "ld_wv", "ld_wo", "v_init",
                      "ld_mask", "ld_ident"]
            if i >= 2:
                w += [f"pj_ev{i-2}"]
            if c == 0 and b >= 1:
                w += [f"cx_ev{2 * b * CH - 1}"]
            P.op("tensor", emit_pj, w, signal=f"pj_mm{i}")

            def emit_ev(a, i=i, b=b, c=c):
                qb = psQ[:, 512 * (i % 2):512 * (i % 2) + 512]
                kb = psK[:, 512 * (i % 2):512 * (i % 2) + 512]
                vb = psC[:, 512 * (i % 2):512 * (i % 2) + 512]
                a.activation(qt[:, b % 2, 512 * c:512 * c + 512], qb, AF.Tanh)
                a.activation(ktp0[0:64, b % 2, 512 * c:512 * c + 512], kb[0:64, :], AF.Tanh)
                a.activation(ktp1[64:128, b % 2, 512 * c:512 * c + 512], kb[64:128, :], AF.Tanh)
                a.activation(vtt[:, b % 2, 4 * c:4 * c + 4, 0:64],
                             vb.rearrange("p (a n) -> p a n", n=128)[:, :, 0:64], AF.Tanh)
                return a.activation(vtt[:, b % 2, 4 * c:4 * c + 4, 65:129],
                                    vb.rearrange("p (a n) -> p a n", n=128)[:, :, 64:128],
                                    AF.Tanh)
            P.op("scalar", emit_ev, [f"pj_mm{i}"], signal=f"pj_ev{i}")

        def avbank(q):
            return (psX if q < 2 else psC)[:, 512 * (q % 2):512 * (q % 2) + 512]

        for c in range(CH):
            uidx = b * CH + c

            for h in range(2):
                blk = 2 * uidx + h
                pend = []

                def emit_av(t, ph, pg, pglob, b=b, c=c):
                    last = None
                    for ktl in (2 * pg, 2 * pg + 1):
                        slot = pglob % 6
                        half = ktl % 2
                        for q in av_targets(c, ktl):
                            last = t.matmul(
                                avbank(q)[:, 0:65],
                                expst[:, slot, 512 * half + 128 * q:512 * half + 128 * q + 128],
                                vtt[:, b % 2, ktl, 65 * ph:65 * ph + 65],
                                start=(ktl == 0), stop=(ktl == 4 * c + q))
                    return last

                for g in range(2 * c + 2):
                    glob = nsc

                    def emit_sc(t, h=h, g=g, glob=glob, b=b, c=c):
                        last = None
                        for ktl in (2 * g, 2 * g + 1):
                            bank = (psQ if glob % 2 == 0 else psK)[
                                :, 512 * (ktl % 2):512 * (ktl % 2) + 512]
                            ktp = ktp0 if h == 0 else ktp1
                            diag = ktl >= 4 * c
                            last = t.matmul(bank, ktp[:, b % 2, 128 * ktl:128 * ktl + 128],
                                            qt[:, b % 2, 512 * c:512 * c + 512],
                                            start=True, stop=not diag)
                            if diag:
                                r = ktl - 4 * c
                                last = t.matmul(bank[:, 128 * r:128 * r + 128],
                                                ident[:, :], maskb[:, :],
                                                start=False, stop=True)
                        return last
                    w = []
                    if g == 0 and blk == 0:
                        w += [f"pj_ev{CH - 1}"]
                    elif g == 0 and h == 0:
                        w += [f"pj_ev{(b + 1) * CH - 1}"]
                    if glob >= 2:
                        w += [f"at_ex{glob-2}"]
                    P.op("tensor", emit_sc, w, signal=f"at_sc{glob}")

                    def emit_ex(a, glob=glob):
                        return a.activation(expst[:, glob % 6, :],
                                            (psQ if glob % 2 == 0 else psK)[:, 0:1024],
                                            AF.Exp, scale=0.125)
                    we = [f"at_sc{glob}"]
                    if glob >= 6:
                        we += [f"at_av{glob-6}"]
                    P.op("scalar", emit_ex, we, signal=f"at_ex{glob}")

                    pend.append((h, g, glob))
                    nsc += 1
                    if len(pend) >= 2:
                        ph_, pg_, pglob_ = pend.pop(0)
                        wv = [f"at_ex{pglob_}"]
                        if pg_ == 0 and blk > 0:
                            wv += [f"cx_ev{blk-1}"]
                        P.op("tensor",
                             lambda t, ph=ph_, pg=pg_, pglob=pglob_, fn=emit_av: fn(t, ph, pg, pglob),
                             wv, signal=f"at_av{pglob_}")
                ph_, pg_, pglob_ = pend.pop(0)
                wv = [f"at_ex{pglob_}"]
                if pg_ == 0 and blk > 0:
                    wv += [f"cx_ev{blk-1}"]
                P.op("tensor",
                     lambda t, ph=ph_, pg=pg_, pglob=pglob_, fn=emit_av: fn(t, ph, pg, pglob),
                     wv, signal=f"at_av{pglob_}")

                def emit_cx(v, b=b, c=c, h=h):
                    last = None
                    for q in range(4):
                        j = 4 * c + q
                        bank = avbank(q)
                        vrecip(v, scal[:, SL["tmp"]:SL["tmp"] + 1],
                               bank[:, 64:65])
                        last = v.tensor_scalar(
                            ctxq[:, b % 2, j, 64 * h:64 * h + 64],
                            bank[:, 0:64],
                            scal[:, SL["tmp"]:SL["tmp"] + 1], None, op0=ALU.mult)
                    return last
                wcx = [f"at_av{nsc-1}"]
                if b >= 2:
                    wcx += [f"cxst{b-2}b"]
                P.op("vector", emit_cx, wcx, signal=f"cx_ev{blk}")
                pop_dq(1)

        dma_s(f"cxst{b}a", a2a_i[2 * b, :, :].rearrange("(a p) d -> p a d", p=128),
              ctxq[:, b % 2, 0:S // 256, :], waits=[f"cx_ev{2 * (b + 1) * CH - 1}"],
              ch="cx")
        dma_s(f"cxst{b}b", a2a_i[2 * b + 1, :, :].rearrange("(a p) d -> p a d", p=128),
              ctxq[:, b % 2, S // 256:S // 128, :], waits=[f"cx_ev{2 * (b + 1) * CH - 1}"],
              ch="cx")

    n_groups = nsc

    P.op("gpsimd", lambda g: g.collective_compute(
        "AllToAll", ALU.bypass, replica_groups=[list(range(8))],
        ins=[a2a_i.ap().opt()], outs=[a2a_o.ap().opt()]),
        [f"cxst{B-1}b"], signal="a2a", amount=1, sem="cc")

    for j in range(FT):
        dma_s(f"ctxT{j}", ctxT[:, j, :], a2a_o[j, :, :], waits=["a2a"], tr=True,
              ch="cT")

    pop_dq(len(dq_queue))

    for tt in range(TT):
        w = ([f"ln_c{tt-2}"] if tt >= 2 else [])
        dma_s(f"xm{tt}", xmst[:, tt % 2, :], xre(xm)[:, tt, :], waits=w,
              ch=f"xm{tt % 2}")

    for tt in range(TT):
        def emit_wo(t, tt=tt):
            ps = psQ if tt % 2 == 0 else psK
            last = None
            for ch in range(2):
                bank = ps[:, 512 * ch:512 * ch + 512]
                for j in range(FT):
                    last = t.matmul(bank, ctxT[:, j, 128 * tt:128 * tt + 128],
                                    wo16[:, j, 512 * ch:512 * ch + 512],
                                    start=(j == 0), stop=(j == FT - 1))
            return last
        if tt == 0:
            w = [f"ctxT{FT-1}", f"at_ex{n_groups-1}"]
        elif tt == 1:
            w = [f"ctxT{FT-1}"]
        else:
            w = [f"wo_ev{tt-2}"]
        P.op("tensor", emit_wo, w, signal=f"wo_mm{tt}")

        def emit_woev(a, tt=tt):
            ps = psQ if tt % 2 == 0 else psK
            a.activation(lnsc[:, tt % 2, 0:512], ps[:, 0:512], AF.Tanh)
            return a.activation(lnsc[:, tt % 2, 512:1024], ps[:, 512:1024], AF.Tanh)
        wev = [f"wo_mm{tt}"] + ([f"ln_c{tt-2}"] if tt >= 2 else [])
        P.op("scalar", emit_woev, wev, signal=f"wo_ev{tt}")

        def emit_res(v, tt=tt):
            v.tensor_tensor(y32[:, tt, :], lnsc[:, tt % 2, :], xmst[:, tt % 2, :],
                            op=ALU.add)
            return v.reduce_sum(lnst[:, tt:tt + 1], y32[:, tt, :], axis=AX.X)
        P.op("vector", emit_res, [f"wo_ev{tt}", f"xm{tt}"], signal=f"ln_c{tt}")

    for tt in range(TT):
        def emit_sq(a, tt=tt):
            return a.activation(lnsq[:, :], y32[:, tt, :], AF.Square,
                                accum_out=lnst2[:, tt:tt + 1])
        P.op("scalar", emit_sq, [f"ln_c{tt}"], signal=f"ln_sq{tt}")
    for tt in range(TT):
        def emit_var(v, tt=tt):
            m = lnst[:, tt:tt + 1]
            s2 = lnst2[:, tt:tt + 1]
            v.tensor_scalar(m, m, 1.0 / D, None, op0=ALU.mult)
            v.tensor_tensor(scal[:, SL["tmp"]:SL["tmp"] + 1], m, m, op=ALU.mult)
            v.tensor_scalar(s2, s2, 1.0 / D, None, op0=ALU.mult)
            v.tensor_tensor(s2, s2, scal[:, SL["tmp"]:SL["tmp"] + 1],
                            op=ALU.subtract)
            return v.tensor_scalar(s2, s2, 1e-5, None, op0=ALU.add)
        P.op("vector", emit_var, [f"ln_sq{tt}"], signal=f"ln_v{tt}")
    for tt in range(TT):
        def emit_rs(a, tt=tt):
            return a.activation(lnst2[:, tt:tt + 1], lnst2[:, tt:tt + 1],
                                AF.Sqrt)
        P.op("scalar", emit_rs, [f"ln_v{tt}"], signal=f"ln_rs{tt}")
    for tt in range(TT):
        def emit_y(v, tt=tt):
            vrecip(v, lnst2[:, tt:tt + 1], lnst2[:, tt:tt + 1])
            v.tensor_scalar(y32[:, tt, :], y32[:, tt, :], lnst[:, tt:tt + 1],
                            lnst2[:, tt:tt + 1], op0=ALU.subtract, op1=ALU.mult)
            vreduce(v, scal[:, SL["tmp"]:SL["tmp"] + 1], y32[:, tt, :],
                    axis=AX.X, op=ALU.max, apply_absolute_value=True)
            return v.tensor_tensor(scal[:, SL["m1a"]:SL["m1a"] + 1],
                                   scal[:, SL["m1a"]:SL["m1a"] + 1],
                                   scal[:, SL["tmp"]:SL["tmp"] + 1], op=ALU.max)
        P.op("vector", emit_y, [f"ln_rs{tt}"], signal=f"ln_y{tt}")

    dma_s("y_spill", xre(yd), y32[:, :, :], waits=[f"ln_y{TT-1}"], ch="sp")

    def mbarrier(idx, name, acc_slot, extra_c13, waits):
        dma_g(f"{name}_p1", mxp.ap().rearrange("(p n) -> p n", n=1),
              scal[:, acc_slot:acc_slot + 1], waits=waits, ch="gM")
        dma_g(f"{name}_p2", bcrow[:, :],
              mxp.ap().rearrange("(n p) -> n p", p=128), waits=[f"{name}_p1"],
              ch="gM")

        def vpar(v):
            vreduce(v, scal[0:1, SL["tmp"] + 1:SL["tmp"] + 2],
                    bcrow[:, :], axis=AX.X, op=ALU.max,
                    apply_absolute_value=True)
            return v.nop()
        P.op("vector", vpar, [f"{name}_p2"], signal=f"{name}_par")
        dma_g(f"{name}_o", mxi[idx][:, :], scal[0:1, SL["tmp"] + 1:SL["tmp"] + 2],
              waits=[f"{name}_par"], ch="gM")
        P.op("gpsimd", lambda g: g.collective_compute(
            "AllReduce", ALU.max, replica_groups=[list(range(8))],
            ins=[mxi[idx].ap().opt()], outs=[mxo[idx].ap().opt()]),
            [f"{name}_o"], signal=f"{name}_ar", amount=1, sem="cc")
        dma_g(f"{name}_i", scal[0:1, SL["row"]:SL["row"] + 1], mxo[idx][:, :],
              waits=[f"{name}_ar"], ch="gM")

        def vscale(v):
            r = SL["row"]
            vsync(v, v.tensor_scalar(scal[0:1, r + 1:r + 2], scal[0:1, r:r + 1],
                                     IQA, 1e-12, op0=ALU.mult, op1=ALU.max))
            vrecip(v, scal[0:1, r + 2:r + 3], scal[0:1, r + 1:r + 2])
            if extra_c13:
                return v.tensor_scalar(scal[0:1, r + 3:r + 4], scal[0:1, r + 1:r + 2],
                                       1.0 / 65536.0, None, op0=ALU.mult)
            return v.nop()
        P.op("vector", vscale, [f"{name}_i"], signal=f"{name}_vs")

    def mb_bcast(name, slots):
        def bcmm(t):
            last = None
            for n, (dst, ro) in enumerate(slots):
                last = t.matmul(psX[:, 960 + n:960 + n + 1], bcones[:, :],
                                scal[0:1, SL["row"] + ro:SL["row"] + ro + 1],
                                start=True, stop=True)
            return last
        P.op("tensor", bcmm, [f"{name}_vs"], signal=f"{name}_bm")

        def bccp(v):
            last = None
            for n, (dst, ro) in enumerate(slots):
                last = v.tensor_copy(scal[:, dst:dst + 1],
                                     psX[:, 960 + n:960 + n + 1])
            vsync(v, last)
            return v.nop()
        P.op("vector", bccp, [f"{name}_bm"], signal=f"{name}_bc")

    mbarrier(0, "m1", SL["m1a"], False, [f"ln_y{TT-1}"])
    mb_bcast("m1", [(SL["s1"], 1), (SL["r1"], 2)])

    for tt in range(TT):
        def emit_xq(v, tt=tt):
            v.tensor_scalar(lnsq[:, :], y32[:, tt, :], scal[:, SL["r1"]:SL["r1"] + 1],
                            RC, op0=ALU.mult, op1=ALU.add)
            return v.tensor_scalar(xq16[:, tt, :], lnsq[:, :], -RC,
                                   scal[:, SL["s1"]:SL["s1"] + 1],
                                   op0=ALU.add, op1=ALU.mult)
        P.op("vector", emit_xq, ["m1_bc"] if tt == 0 else [], signal=f"xq{tt}")
    dma_s("xq_spill", xre(xqd), xq16[:, :, :], waits=[f"xq{TT-1}"], ch="sp")

    for ft in range(FT):
        dma_s(f"ytq{ft}", ytq[:, ft, :], xqd[:, 128 * ft:128 * ft + 128],
              waits=["xq_spill"], tr=True, ch="ytq")
    for ft in range(FT):
        dma_s(f"wtgG{ft}", wtg[:, ft, :], wg16d[:, 128 * ft:128 * ft + 128],
              waits=[f"dq_st{IT-2}", f"dq_st{IT-1}"], tr=True, ch="wgG")

    def bank4(n):
        return (psQ if (n % 4) < 2 else psK)[:, 512 * (n % 2):512 * (n % 2) + 512]

    nmm = 0
    for wsel in range(2):
        for it in range(IT):
            for tc in range(TC):
                def emit_ug(t, it=it, tc=tc, nmm=nmm):
                    bk = bank4(nmm)
                    last = None
                    for ft in range(FT):
                        last = t.matmul(bk, wtg[:, ft, 128 * it:128 * it + 128],
                                        ytq[:, ft, 512 * tc:512 * tc + 512],
                                        start=(ft == 0), stop=(ft == FT - 1))
                    return last
                w = []
                if nmm == 0:
                    w = [f"ytq{FT-1}", f"wtgG{FT-1}", f"wo_ev{TT-1}"]
                if wsel == 1 and it == 0 and tc == 0:
                    w += [f"wtgU{FT-1}"]
                if nmm >= 4:
                    w += [f"ug_ev{nmm-4}"]
                P.op("tensor", emit_ug, w, signal=f"ug_mm{nmm}")

                def emit_ugev(a, wsel=wsel, it=it, tc=tc, nmm=nmm):
                    stg = stg1 if wsel == 0 else stg2
                    return a.activation(stg[:, it % 2, 512 * tc:512 * tc + 512],
                                        bank4(nmm), AF.Copy)
                wev = [f"ug_mm{nmm}"]
                if nmm >= 2 * TC:
                    wev += [f"ug_mx{nmm - 2 * TC}", f"rawsp{(nmm - 2 * TC) // (IT * TC)}_{((nmm - 2 * TC) // TC) % IT}"]
                P.op("scalar", emit_ugev, wev, signal=f"ug_ev{nmm}")

                def emit_ugmx(v, wsel=wsel, it=it, tc=tc):
                    stg = stg1 if wsel == 0 else stg2
                    acc = SL["m3a"] if wsel == 0 else SL["m2a"]
                    vreduce(v, scal[:, SL["tmp"]:SL["tmp"] + 1],
                            stg[:, it % 2, 512 * tc:512 * tc + 512],
                            axis=AX.X, op=ALU.max, apply_absolute_value=True)
                    return v.tensor_tensor(scal[:, acc:acc + 1], scal[:, acc:acc + 1],
                                           scal[:, SL["tmp"]:SL["tmp"] + 1], op=ALU.max)
                P.op("vector", emit_ugmx, [f"ug_ev{nmm}"], signal=f"ug_mx{nmm}")
                nmm += 1
            stgd = graw if wsel == 0 else uraw
            stgs = stg1 if wsel == 0 else stg2
            dma_s(f"rawsp{wsel}_{it}", xre(stgd)[:, it, :], stgs[:, it % 2, :],
                  waits=[f"ug_ev{nmm-1}"], ch="rwG" if wsel == 0 else "rwU")
        if wsel == 0:
            for ft in range(FT):
                dma_s(f"wtgU{ft}", wtg[:, ft, :], wu16d[:, 128 * ft:128 * ft + 128],
                      waits=[f"ug_mm{IT * TC - 1}", f"dq_st{2*IT-2}", f"dq_st{2*IT-1}"],
                      tr=True, ch="wgU")

    mbarrier(1, "m3", SL["m3a"], True, [f"ug_mx{IT * TC - 1}"])
    mb_bcast("m3", [(SL["s3"], 1), (SL["r3"], 2), (SL["c13"], 3)])
    mbarrier(2, "m2", SL["m2a"], False, [f"ug_mx{2 * IT * TC - 1}"])
    mb_bcast("m2", [(SL["s2"], 1), (SL["r2"], 2)])

    def emit_s1a(v, it):
        p = it % 2
        v.tensor_scalar(fA[:, p, :], stg1[:, p, :], scal[:, SL["r3"]:SL["r3"] + 1],
                        RC, op0=ALU.mult, op1=ALU.add)
        return v.tensor_scalar(fA[:, p, :], fA[:, p, :], -RC, None, op0=ALU.add)

    def emit_sig(a, it):
        p = it % 2
        a.activation(fB[:, p, :], fA[:, p, :], AF.Sigmoid,
                     scale=scal[:, SL["s3"]:SL["s3"] + 1])
        return a.activation(fB[:, p, :], fB[:, p, :], AF.Copy,
                            scale=65536.0, bias=RC)

    def emit_s1b(v, it):
        p = it % 2
        v.tensor_scalar(fB[:, p, :], fB[:, p, :], -RC, 65535.0, op0=ALU.add, op1=ALU.min)
        v.scalar_tensor_tensor(stg3[:, p, :], fA[:, p, :],
                               scal[:, SL["c13"]:SL["c13"] + 1],
                               fB[:, p, :], op0=ALU.mult, op1=ALU.mult)
        vreduce(v, scal[:, SL["tmp"]:SL["tmp"] + 1], stg3[:, p, :],
                axis=AX.X, op=ALU.max, apply_absolute_value=True)
        return v.tensor_tensor(scal[:, SL["m4a"]:SL["m4a"] + 1],
                               scal[:, SL["m4a"]:SL["m4a"] + 1],
                               scal[:, SL["tmp"]:SL["tmp"] + 1], op=ALU.max)

    def s1_dma(it):
        w = [f"rawsp0_{IT-1}"] + ([f"s1a{it-2}"] if it >= 2 else [])
        dma_g(f"s1g{it}", stg1[:, it % 2, :], xre(graw)[:, it, :], waits=w,
              ch=f"fI{it % 2}")

    for it in range(IT):
        if it >= 2:
            dma_s(f"s1p{it-2}", xre(prodd)[:, it - 2, :],
                  stg3[:, (it - 2) % 2, :], waits=[f"s1v{it-2}"], ch=f"fO{(it-2) % 2}")
        s1_dma(it)
        w = [f"s1g{it}", "m3_bc"] + ([f"s1sg{it-2}"] if it >= 2 else [])
        P.op("vector", lambda v, it=it: emit_s1a(v, it), w, signal=f"s1a{it}")
        ws = [f"s1a{it}"] + ([f"s1v{it-2}"] if it >= 2 else [])
        P.op("scalar", lambda a, it=it: emit_sig(a, it), ws, signal=f"s1sg{it}")
        if it >= 1:
            pit = it - 1
            w = [f"s1sg{pit}"] + ([f"s1p{pit-2}"] if pit >= 2 else [])
            P.op("vector", lambda v, it=pit: emit_s1b(v, it), w, signal=f"s1v{pit}")
    pit = IT - 1
    P.op("vector", lambda v, it=pit: emit_s1b(v, it),
         [f"s1sg{pit}", f"s1p{pit-2}"], signal=f"s1v{pit}")
    for it in (IT - 2, IT - 1):
        dma_s(f"s1p{it}", xre(prodd)[:, it, :], stg3[:, it % 2, :],
              waits=[f"s1v{it}"], ch=f"fO{it % 2}")

    mbarrier(3, "m4", SL["m4a"], False, [f"s1v{IT-1}"])
    mb_bcast("m4", [(SL["s4"], 1), (SL["r4"], 2)])

    for it in range(IT):
        p = it % 2
        q4 = it % 4
        w = [f"s1p{IT-2}", f"s1p{IT-1}"]
        if it >= 4:
            w += [f"s2r{it-4}"]
        dma_g(f"s2i{it}", stg3[:, q4, :], xre(prodd)[:, it, :], waits=w,
              ch=f"fI{p}")
        w2 = [f"rawsp1_{IT-1}"]
        if it >= 4:
            w2 += [f"s2v{it-4}"]
        dma_g(f"s2u{it}", stg2[:, q4, :], xre(uraw)[:, it, :], waits=w2,
              ch=f"fJ{p}")

        def emit_s2r(a, it=it, p=p, q4=q4):
            a.activation(fB[:, p, :], stg2[:, q4, :], AF.Copy,
                         scale=scal[:, SL["r2"]:SL["r2"] + 1], bias=RC)
            return a.activation(fA[:, p, :], stg3[:, q4, :], AF.Copy,
                                scale=scal[:, SL["r4"]:SL["r4"] + 1], bias=RC)
        wr = [f"s2i{it}", f"s2u{it}", "m4_bc", "m2_bc"]
        if it >= 2:
            wr += [f"s2v{it-2}"]
        else:
            wr += [f"s1v{IT-1}"]
        P.op("scalar", emit_s2r, wr, signal=f"s2r{it}")

        def emit_s2(v, it=it, p=p, q4=q4):
            v.tensor_scalar(stg2[:, q4, :], fB[:, p, :], -RC,
                            scal[:, SL["s2"]:SL["s2"] + 1],
                            op0=ALU.add, op1=ALU.mult)
            v.tensor_scalar(fA[:, p, :], fA[:, p, :], -RC, scal[:, SL["s4"]:SL["s4"] + 1],
                            op0=ALU.add, op1=ALU.mult)
            v.tensor_tensor(stg1[:, p, :], fA[:, p, :], stg2[:, q4, :], op=ALU.mult)
            vreduce(v, scal[:, SL["tmp"]:SL["tmp"] + 1], stg1[:, p, :],
                    axis=AX.X, op=ALU.max, apply_absolute_value=True)
            return v.tensor_tensor(scal[:, SL["m5a"]:SL["m5a"] + 1],
                                   scal[:, SL["m5a"]:SL["m5a"] + 1],
                                   scal[:, SL["tmp"]:SL["tmp"] + 1], op=ALU.max)
        w3 = [f"s2r{it}"]
        if it >= 2:
            w3 += [f"s2o{it-2}"]
        P.op("vector", emit_s2, w3, signal=f"s2v{it}")
        dma_s(f"s2o{it}", xre(qsd)[:, it, :], stg1[:, p, :], waits=[f"s2v{it}"],
              ch=f"fO{p}")

    mbarrier(4, "m5", SL["m5a"], False, [f"s2v{IT-1}"])
    mb_bcast("m5", [(SL["s5"], 1), (SL["r5"], 2)])

    def bank8(idx):
        tensors = [psQ, psQ, psK, psK, psC, psC, psX, psX]
        return tensors[idx][:, 512 * (idx % 2):512 * (idx % 2) + 512]

    for tg in range(TG):
        for it in range(IT):
            gi = tg * IT + it
            w = [f"s2o{IT-2}", f"s2o{IT-1}"]
            if gi >= 2:
                w += [f"s3v{gi-2}"]
            else:
                w += [f"s2v{IT-1}"]
            dma_g(f"s3i{gi}", stg1[:, it % 2, 0:512],
                  xre(qsd)[:, it, 512 * tg:512 * tg + 512], waits=w,
                  ch=f"fI{gi % 2}")
            ww = [f"dq_st{ndq-2}", f"dq_st{ndq-1}"]
            if gi >= 4:
                ww += [f"dn_it{(gi-4)//IT}_{(gi-4)%IT}"]
            dma_s(f"s3w{gi}", wtd[:, gi % 4, :], wd16d[:, 128 * it:128 * it + 128],
                  waits=ww, tr=True, ch=f"fJ{gi % 2}")

            def emit_s3(v, it=it, gi=gi, tg=tg):
                v.tensor_scalar(fA[:, gi % 2, 0:512], stg1[:, it % 2, 0:512],
                                scal[:, SL["r5"]:SL["r5"] + 1], RC,
                                op0=ALU.mult, op1=ALU.add)
                return v.tensor_scalar(ot[:, it, :], fA[:, gi % 2, 0:512], -RC,
                                       scal[:, SL["s5"]:SL["s5"] + 1],
                                       op0=ALU.add, op1=ALU.mult)
            w2 = [f"s3i{gi}", "m5_bc"]
            if tg >= 1:
                w2 += [f"dn_it{tg-1}_{IT-1}"]
            P.op("vector", emit_s3, w2, signal=f"s3v{gi}")

        for it in range(IT):
            gi = tg * IT + it

            def emit_dn(t, tg=tg, it=it, gi=gi):
                last = None
                for q in range(4):
                    for ch in range(2):
                        last = t.matmul(bank8(2 * q + ch),
                                        ot[:, it, 128 * q:128 * q + 128],
                                        wtd[:, gi % 4, 512 * ch:512 * ch + 512],
                                        start=(it == 0), stop=(it == IT - 1))
                return last
            w = [f"s3v{gi}", f"s3w{gi}"]
            if it == 0:
                if tg == 0:
                    w += [f"ug_ev{2 * IT * TC - 1}", f"cx_ev{2 * B * CH - 1}",
                          f"at_ex{n_groups-1}"]
                else:
                    w += [f"dn_ev{4*(tg-1)+q}" for q in range(4)]
            P.op("tensor", emit_dn, w, signal=f"dn_it{tg}_{it}")

        for q in range(4):
            tt = 4 * tg + q

            def emit_dnev(a, q=q, tt=tt):
                a.activation(dnsc[:, tt % 4, 0:512], bank8(2 * q), AF.Copy)
                return a.activation(dnsc[:, tt % 4, 512:1024], bank8(2 * q + 1), AF.Copy)
            w = [f"dn_it{tg}_{IT-1}"]
            if tt >= 4:
                w += [f"o_y{tt-4}"]
            P.op("scalar", emit_dnev, w, signal=f"dn_ev{tt}")

            w = ["xq_spill"] + ([f"o_res{tt-2}"] if tt >= 2 else [])
            dma_g(f"yr{tt}", yrl[:, tt % 2, :], xre(yd)[:, tt, :], waits=w,
                  ch=f"yr{tt % 2}")
        for q in range(4):
            tt = 4 * tg + q

            def emit_ores(v, tt=tt):
                v.tensor_tensor(dnsc[:, tt % 4, :], dnsc[:, tt % 4, :],
                                yrl[:, tt % 2, :], op=ALU.add)
                return v.reduce_sum(lnst[:, 2 * (tt % 4):2 * (tt % 4) + 1],
                                    dnsc[:, tt % 4, :], axis=AX.X)
            P.op("vector", emit_ores, [f"dn_ev{tt}", f"yr{tt}"], signal=f"o_res{tt}")
        for q in range(4):
            tt = 4 * tg + q

            def emit_osq(a, tt=tt):
                return a.activation(lnsq[:, :], dnsc[:, tt % 4, :], AF.Square,
                                    accum_out=lnst[:, 2 * (tt % 4) + 1:2 * (tt % 4) + 2])
            P.op("scalar", emit_osq, [f"o_res{tt}"], signal=f"o_sq{tt}")
        for q in range(4):
            tt = 4 * tg + q

            def emit_ovar(v, tt=tt):
                st0 = 2 * (tt % 4)
                m = lnst[:, st0:st0 + 1]
                s2 = lnst[:, st0 + 1:st0 + 2]
                v.tensor_scalar(m, m, 1.0 / D, None, op0=ALU.mult)
                v.tensor_tensor(scal[:, SL["tmp"]:SL["tmp"] + 1], m, m, op=ALU.mult)
                v.tensor_scalar(s2, s2, 1.0 / D, None, op0=ALU.mult)
                v.tensor_tensor(s2, s2, scal[:, SL["tmp"]:SL["tmp"] + 1],
                                op=ALU.subtract)
                return v.tensor_scalar(s2, s2, 1e-5, None, op0=ALU.add)
            P.op("vector", emit_ovar, [f"o_sq{tt}"], signal=f"o_var{tt}")
        for q in range(4):
            tt = 4 * tg + q

            def emit_ors(a, tt=tt):
                st0 = 2 * (tt % 4)
                return a.activation(lnst[:, st0 + 1:st0 + 2], lnst[:, st0 + 1:st0 + 2],
                                    AF.Sqrt)
            P.op("scalar", emit_ors, [f"o_var{tt}"], signal=f"o_rs{tt}")
        for q in range(4):
            tt = 4 * tg + q

            def emit_oy(v, tt=tt):
                st0 = 2 * (tt % 4)
                vrecip(v, lnst[:, st0 + 1:st0 + 2], lnst[:, st0 + 1:st0 + 2])
                return v.tensor_scalar(outb[:, tt % 2, :], dnsc[:, tt % 4, :],
                                       lnst[:, st0:st0 + 1], lnst[:, st0 + 1:st0 + 2],
                                       op0=ALU.subtract, op1=ALU.mult)
            w = [f"o_rs{tt}"] + ([f"ost{tt-2}"] if tt >= 2 else [])
            P.op("vector", emit_oy, w, signal=f"o_y{tt}")
            dma_g(f"ost{tt}", xre(out)[:, tt, :], outb[:, tt % 2, :], waits=[f"o_y{tt}"],
                  ch=f"os{tt % 2}")

    P.op("sync", lambda s: s.nop(), [f"ost{TT-2}", f"ost{TT-1}"], signal=None)

    chans = ["vself", "ini", "cx", "cT", "sp", "ytq", "wgG", "wgU", "rwG", "rwU",
             "xm0", "xm1", "dqL0", "dqL1", "dqS0", "dqS1", "fI0", "fI1",
             "fJ0", "fJ1", "fO0", "fO1", "yr0", "yr1", "os0", "os1",
             "gW", "gM", "g0", "g1", "cc",
             "c_tensor", "c_scalar", "c_vector", "c_gpsimd", "c_sync"]
    import contextlib
    with contextlib.ExitStack() as stack:
        sems = {k: stack.enter_context(nc.semaphore(k)) for k in chans}
        vself["sem"] = sems["vself"]
        block = stack.enter_context(nc.Block())
        engines = {}

        def reg(name):
            def f(e):
                engines[name] = e
                run(name)
            return f
        run = P.build_and_emit(sems, engines)
        block.sync(reg("sync"))
        block.gpsimd(reg("gpsimd"))
        block.tensor(reg("tensor"))
        block.scalar(reg("scalar"))
        block.vector(reg("vector"))
    return nc


def _in_maps(cfg, inputs):
    st = np.asarray(inputs["L0_attn_states"], dtype=np.float32)
    B, S, D = st.shape
    flat = np.ascontiguousarray(st.reshape(B * S, D))
    xt = np.ascontiguousarray(flat.T)
    wq = np.asarray(inputs["L0_attn_Wq"], np.float32)
    wk = np.asarray(inputs["L0_attn_Wk"], np.float32)
    wv = np.asarray(inputs["L0_attn_Wv"], np.float32)
    wo = np.ascontiguousarray(np.asarray(inputs["L0_attn_Wo"], np.float32))
    wgn = np.ascontiguousarray(np.asarray(inputs["L0_ffn_w_gate"], np.float32))
    wun = np.ascontiguousarray(np.asarray(inputs["L0_ffn_w_up"], np.float32))
    wdn = np.ascontiguousarray(np.asarray(inputs["L0_ffn_w_down"], np.float32))
    tri = np.triu(np.ones((128, 128), np.float32))
    maskc = ((tri - 1.0) * 30000.0).astype(np.float16)
    identc = np.eye(128, dtype=np.float16)
    TM = cfg.TM
    maps = []
    for c in range(8):
        maps.append(dict(
            xt=xt,
            xm=np.ascontiguousarray(flat[TM * c:TM * (c + 1)]),
            wq=np.ascontiguousarray(wq[:, 128 * c:128 * c + 128]),
            wk=np.ascontiguousarray(wk[:, 128 * c:128 * c + 128]),
            wv=np.ascontiguousarray(wv[:, 128 * c:128 * c + 128]),
            wo=wo, wgn=wgn, wun=wun, wdn=wdn,
            maskc=maskc, identc=identc,
        ))
    return maps


def kernel(**inputs):
    st = np.asarray(inputs["L0_attn_states"])
    B, S, D = st.shape
    cfg = CFG(S=S)
    nc = build_nc(cfg)
    maps = _in_maps(cfg, inputs)
    res = run_bass_kernel_spmd(nc, maps, core_ids=list(range(8)))
    outp = np.concatenate([np.asarray(r["out"]) for r in res.results], axis=0)
    return outp.reshape(B, S, D).astype(np.float32)


# version 09
# speedup vs baseline: 1.2752x; 1.0331x over previous
import os
os.environ.setdefault("TRN_ALLOW_ICI_ALL_TO_ALL", "1")

import numpy as np
import concourse.bass as bass
import concourse.mybir as mybir
from concourse.bass_utils import run_bass_kernel_spmd

F32 = mybir.dt.float32
F16 = mybir.dt.float16
AF = mybir.ActivationFunctionType
ALU = mybir.AluOpType
AX = mybir.AxisListType

RC = 12582912.0
IQA = 1.0 / 32767.0
IQW = 1.0 / 7.0


class CFG:
    def __init__(self, S=4096, debug=False):
        self.debug = debug
        self.B = 4
        self.S = S
        self.D = 1024
        self.I = 3072
        self.TM = self.B * S // 8
        self.CH = S // 512
        self.TT = self.TM // 128
        self.TG = self.TM // 512
        self.IT = self.I // 128
        self.FT = self.D // 128
        self.TC = self.TM // 512


class Prog:
    ENGINES = ("sync", "gpsimd", "tensor", "scalar", "vector")

    def __init__(self):
        self.ops = []

    def op(self, engine, emit, waits=(), signal=None, amount=1, sem=None):
        self.ops.append((engine, emit, tuple(waits), signal, amount, sem))

    def build_and_emit(self, sems, engines):
        counts = {k: 0 for k in sems}
        table = {}
        for engine, emit, waits, signal, amount, semk in self.ops:
            if signal is not None:
                k = semk if semk is not None else "c_" + engine
                counts[k] += amount
                assert signal not in table, f"dup event {signal}"
                table[signal] = (k, counts[k])
        for engine, emit, waits, signal, amount, semk in self.ops:
            for w in waits:
                assert w in table, f"missing event {w}"
        watermark = {e: {k: 0 for k in sems} for e in self.ENGINES}

        def run(engine):
            eng_obj = engines[engine]
            for e2, emit, waits, signal, amount, semk in self.ops:
                if e2 != engine:
                    continue
                need = {}
                for w in waits:
                    k, v = table[w]
                    need[k] = max(need.get(k, 0), v)
                for k, v in need.items():
                    if watermark[engine][k] < v:
                        eng_obj.wait_ge(sems[k], v)
                        watermark[engine][k] = v
                inst = emit(eng_obj)
                if signal is not None:
                    k = semk if semk is not None else "c_" + engine
                    assert inst is not None, f"event {signal} has no instruction"
                    inst.then_inc(sems[k], amount)
        return run


def mem_map(cfg):
    M, o = {}, 17408

    def al(name, nbytes, at=None):
        nonlocal o
        if at is not None:
            M[name] = at
            return at
        M[name] = o
        o += (nbytes + 31) // 32 * 32
        return M[name]

    al("ident", 256); al("maskb", 256)
    al("wq16", cfg.FT * 256); al("wk16", cfg.FT * 256); al("wv16", cfg.FT * 256)
    al("wo16", cfg.FT * 2048)
    al("scal", 256)
    al("dqs", 128); al("dqr", 128)
    al("bcrow", 512); al("bcones", 512)
    al("lnsq", 4096)
    al("lnst", 64)
    al("lnst2", 64)
    al("dqz", 4096)
    al("dqw32", 2 * 4096)
    al("dq16", 2 * 2048)
    P1 = o
    al("xtb", 2 * cfg.FT * 1024)
    al("qt", 2 * cfg.S * 2)
    al("ktp0", 2 * cfg.S * 2)
    al("ktp1", 2 * cfg.S * 2)
    al("vtt", 2 * (cfg.S // 128) * 264)
    al("expst", 6 * 2048)
    al("ctxq", 2 * (cfg.S // 128) * 256)
    p1_end = o
    al("ctxT", 0, at=P1)
    al("xq16", 0, at=P1)
    ph2 = P1 + max(cfg.FT * cfg.TM * 2, cfg.TT * 2048)
    al("y32", cfg.TT * 4096, at=ph2); ph2 += cfg.TT * 4096
    al("xmst", 0, at=ph2); ph2 += 2 * 4096
    al("lnsc", 0, at=ph2); ph2 += 2 * 4096
    al("ytq", 0, at=P1)
    al("ot", 0, at=P1)
    base3 = P1 + max(cfg.FT * cfg.TM * 2, cfg.IT * 1024)
    al("wtg", 0, at=base3); base3 += cfg.FT * 6144
    al("wtd", 4 * 2048, at=base3); base3 += 4 * 2048
    al("stg1", 0, at=base3); base3 += 2 * cfg.TM * 2
    al("fA", 0, at=base3); base3 += 2 * cfg.TM * 4
    dead0 = base3
    al("stg2", 0, at=base3); base3 += 4 * cfg.TM * 2
    al("stg3", 0, at=base3); base3 += 4 * cfg.TM * 2
    al("fB", 0, at=base3); base3 += 2 * cfg.TM * 4
    o3 = dead0
    al("yrl", 0, at=o3); o3 += 2 * 4096
    al("outb", 0, at=o3); o3 += 2 * 4096
    al("dnsc", 0, at=o3); o3 += 4 * 4096
    M["_END"] = max(p1_end, ph2, base3, o3)
    return M


def build_nc(cfg):
    nc = bass.Bass("TRN2", target_bir_lowering=False,
                   detect_race_conditions=False)
    B, S, D, I = cfg.B, cfg.S, cfg.D, cfg.I
    TM, CH, TT, TG, IT, FT, TC = (cfg.TM, cfg.CH, cfg.TT, cfg.TG, cfg.IT,
                                  cfg.FT, cfg.TC)
    M = mem_map(cfg)
    assert M["_END"] <= 229344, M["_END"]

    xt = nc.dram_tensor("xt", [D, B * S], F32, kind="ExternalInput").ap()
    xm = nc.dram_tensor("xm", [TM, D], F32, kind="ExternalInput").ap()
    wq = nc.dram_tensor("wq", [D, 128], F32, kind="ExternalInput").ap()
    wk = nc.dram_tensor("wk", [D, 128], F32, kind="ExternalInput").ap()
    wv = nc.dram_tensor("wv", [D, 128], F32, kind="ExternalInput").ap()
    wo = nc.dram_tensor("wo", [D, D], F32, kind="ExternalInput").ap()
    wgn = nc.dram_tensor("wgn", [I, D], F32, kind="ExternalInput").ap()
    wun = nc.dram_tensor("wun", [I, D], F32, kind="ExternalInput").ap()
    wdn = nc.dram_tensor("wdn", [D, I], F32, kind="ExternalInput").ap()
    maskc = nc.dram_tensor("maskc", [128, 128], F16, kind="ExternalInput").ap()
    identc = nc.dram_tensor("identc", [128, 128], F16, kind="ExternalInput").ap()
    out = nc.dram_tensor("out", [TM, D], F32, kind="ExternalOutput").ap()

    dbg = "ExternalOutput" if cfg.debug else "Internal"
    a2a_i = nc.dram_tensor("a2a_i", [8, TM, 128], F16)
    a2a_o = nc.dram_tensor("a2a_o", [8, TM, 128], F16)
    mxi = [nc.dram_tensor(f"mxi{k}", [1, 1], F32) for k in range(5)]
    mxo = [nc.dram_tensor(f"mxo{k}", [1, 1], F32) for k in range(5)]
    mxp = nc.dram_tensor("mxp", [128], F32)
    wg16d = nc.dram_tensor("wg16d", [I, D], F16, kind=dbg)
    wu16d = nc.dram_tensor("wu16d", [I, D], F16, kind=dbg)
    wd16d = nc.dram_tensor("wd16d", [D, I], F16, kind=dbg)
    graw = nc.dram_tensor("graw", [I, TM], F16, kind=dbg)
    uraw = nc.dram_tensor("uraw", [I, TM], F16, kind=dbg)
    uqd = nc.dram_tensor("uqd", [I, TM], F16, kind=dbg)
    prodd = nc.dram_tensor("prodd", [I, TM], F16, kind=dbg)
    qsd = nc.dram_tensor("qsd", [I, TM], F16, kind=dbg)
    xqd = nc.dram_tensor("xqd", [TM, D], F16, kind=dbg)
    yd = nc.dram_tensor("yd", [TM, D], F32, kind=dbg)

    def t16(name, *shape, key=None):
        return nc.alloc_sbuf_tensor_at(name, [128, *shape], F16, offset=M[key or name])

    def t32(name, *shape, key=None):
        return nc.alloc_sbuf_tensor_at(name, [128, *shape], F32, offset=M[key or name])

    ident = t16("ident", 128)
    maskb = t16("maskb", 128)
    wq16 = t16("wq16", FT, 128)
    wk16 = t16("wk16", FT, 128)
    wv16 = t16("wv16", FT, 128)
    wo16 = t16("wo16", FT, 1024)
    scal = t32("scal", 64)
    dqs = t32("dqs", 32)
    dqr = t32("dqr", 32)
    bcrow = nc.alloc_sbuf_tensor_at("bcrow", [1, 128], F32, offset=M["bcrow"])
    bcones = nc.alloc_sbuf_tensor_at("bcones", [1, 128], F32, offset=M["bcones"])
    xtb = t16("xtb", 2, FT, 512)
    qt = t16("qt", 2, S)
    ktp0 = t16("ktp0", 2, S)
    ktp1 = t16("ktp1", 2, S)
    vtt = t16("vtt", 2, S // 128, 132)
    expst = t16("expst", 6, 1024)
    ctxq = t16("ctxq", 2, S // 128, 128)
    dqz = t32("dqz", 1024)
    dqw32 = t32("dqw32", 2, 1024)
    dq16 = t16("dq16", 2, 1024)
    ctxT = t16("ctxT", FT, TM)
    xq16 = t16("xq16", TT, 1024)
    y32 = t32("y32", TT, 1024)
    xmst = t32("xmst", 2, 1024)
    lnsc = t32("lnsc", 2, 1024)
    lnsq = t32("lnsq", 1024)
    lnst = t32("lnst", 16)
    lnst2 = t32("lnst2", 16)
    ytq = t16("ytq", FT, TM)
    ot = t16("ot", 2, IT, 512)
    wtg = t16("wtg", FT, 3072)
    wtd = t16("wtd", 4, 1024)
    stg1 = t16("stg1", 2, TM)
    stg1v = t16("stg1v", (2 * TM) // 512, 512, key="stg1")
    stg2 = t16("stg2", 4, TM)
    stg3 = t16("stg3", 4, TM)
    fA = t32("fA", 2, TM)
    fB = t32("fB", 2, TM)
    yrl = t32("yrl", 2, 1024)
    outb = t32("outb", 2, 1024)
    dnsc = t32("dnsc", 4, 1024)

    psQ = nc.alloc_psum_tensor("psQ", [128, 1024], F32)
    psK = nc.alloc_psum_tensor("psK", [128, 1024], F32)
    psC = nc.alloc_psum_tensor("psC", [128, 1024], F32)
    psX = nc.alloc_psum_tensor("psX", [128, 1024], F32)

    SL = dict(r1=0, s1=1, r3=2, s3=3, c13=4, r2=5, s2=6, r4=7, s4=8, r5=9,
              s5=10, row=16, m1a=24, m3a=25, m2a=26, m4a=27, m5a=28, tmp=29)

    P = Prog()
    vself = {"n": 0, "sem": None}

    def vsync(v, inst):
        vself["n"] += 1
        inst.then_inc(vself["sem"], 1)
        v.wait_ge(vself["sem"], vself["n"])

    def vreduce(v, out, in_, **kw):
        inst = v.tensor_reduce(out, in_, **kw)
        vsync(v, inst)
        return inst

    def vrecip(v, out, in_):
        inst = v.reciprocal(out, in_)
        vsync(v, inst)
        return inst

    def dma_g(name, o_, i_, waits=(), ch="gW", tr=False):
        P.op("gpsimd",
             lambda g, o_=o_, i_=i_: g.dma_start(o_, i_, transpose=tr) if tr
             else g.dma_start(out=o_, in_=i_),
             waits, signal=name, amount=16, sem=ch)

    def dma_s(name, o_, i_, waits=(), tr=False, ch="ini"):
        if tr:
            P.op("sync", lambda s, o_=o_, i_=i_: s.dma_start(o_, i_, transpose=True),
                 waits, signal=name, amount=16, sem=ch)
        else:
            P.op("sync", lambda s, o_=o_, i_=i_: s.dma_start(out=o_, in_=i_),
                 waits, signal=name, amount=16, sem=ch)

    def xre(dram_ap, p=128):
        return dram_ap.rearrange("(a p) n -> p a n", p=p)

    dma_s("ld_mask", maskb[:, :], maskc)
    dma_s("ld_ident", ident[:, :], identc)
    dma_g("ld_wq", wq16[:, :, :], xre(wq))
    dma_g("ld_wk", wk16[:, :, :], xre(wk))
    dma_g("ld_wv", wv16[:, :, :], xre(wv))
    dma_g("ld_wo", wo16[:, :, :], xre(wo))

    def v_init(v):
        v.memset(bcones[:, :], 1.0)
        v.memset(ktp0[64:128, :, :], 0.0)
        v.memset(ktp1[0:64, :, :], 0.0)
        v.memset(vtt[:, :, :, 64:65], 1.0)
        v.memset(vtt[:, :, :, 129:130], 1.0)
        return v.memset(scal[:, SL["m1a"]:SL["m5a"] + 1], 0.0)
    P.op("vector", v_init, (), signal="v_init")

    nchunks = B * CH
    for i in range(nchunks):
        b, c = divmod(i, CH)
        w = [f"pj_mm{i-2}"] if i >= 2 else []
        dma_g(f"ld_xt{i}", xtb[:, i % 2, :, :],
              xre(xt)[:, :, b * S + 512 * c: b * S + 512 * c + 512], waits=w,
              ch=f"g{i % 2}")

    ndq = IT + IT + 3 * (D // 128)

    def dq_src(i):
        if i < IT:
            return xre(wgn)[:, i, :], xre(wg16d)[:, i, :]
        if i < 2 * IT:
            return xre(wun)[:, i - IT, :], xre(wu16d)[:, i - IT, :]
        r, cc = divmod(i - 2 * IT, 3)
        return (xre(wdn)[:, r, 1024 * cc:1024 * cc + 1024],
                xre(wd16d)[:, r, 1024 * cc:1024 * cc + 1024])

    def emit_dq_ld(i):
        w = [f"dq_done{i-2}"] if i >= 2 else []
        dma_s(f"dq_ld{i}", dqw32[:, i % 2, :], dq_src(i)[0], waits=w,
              ch=f"dqL{i % 2}")

    def emit_dq_st(i):
        dma_s(f"dq_st{i}", dq_src(i)[1], dq16[:, i % 2, :], waits=[f"dq_done{i}"],
              ch=f"dqS{i % 2}")

    dq_queue = []
    for i in range(ndq):
        def dq_ops(v, i=i):
            wt = dqw32[:, i % 2, :]
            for hh in range(2):
                vreduce(v, dqs[:, 16 * hh:16 * hh + 16].unsqueeze(2),
                        wt[:, 512 * hh:512 * hh + 512].rearrange("p (a n) -> p a n", n=32),
                        axis=AX.X, op=ALU.max, apply_absolute_value=True)
            vsync(v, v.tensor_scalar(dqs[:, :], dqs[:, :], IQW, 1e-12,
                                     op0=ALU.mult, op1=ALU.max))
            vrecip(v, dqr[:, :], dqs[:, :])
            v.tensor_tensor(dqz[:, :].rearrange("p (a n) -> p a n", n=32),
                            wt.rearrange("p (a n) -> p a n", n=32),
                            dqr[:, :].unsqueeze(2).broadcast_to([128, 32, 32]),
                            op=ALU.mult)
            v.tensor_scalar(dqz[:, :], dqz[:, :], RC, None, op0=ALU.add)
            return v.scalar_tensor_tensor(
                dq16[:, i % 2, :].rearrange("p (a n) -> p a n", n=32),
                dqz[:, :].rearrange("p (a n) -> p a n", n=32), -RC,
                dqs[:, :].unsqueeze(2).broadcast_to([128, 32, 32]),
                op0=ALU.add, op1=ALU.mult)
        wts = [f"dq_ld{i}"] + ([f"dq_st{i-2}"] if i >= 2 else [])
        dq_queue.append((dq_ops, wts, f"dq_done{i}"))

    dq_state = {"pop": 0, "ld": 0}

    def pop_dq(n):
        for _ in range(n):
            if dq_state["pop"] >= ndq:
                return
            i = dq_state["pop"]
            dq_state["pop"] += 1
            while dq_state["ld"] <= min(i + 2, ndq - 1):
                emit_dq_ld(dq_state["ld"])
                dq_state["ld"] += 1
            fn, w, sig = dq_queue[i]
            P.op("vector", fn, w, signal=sig)
            emit_dq_st(i)

    def av_targets(c, kt):
        return range(max(0, kt - 4 * c), 4)

    def ctx_off(m):
        return 65 * m if m < 7 else 512

    nsc = 0
    for b in range(B):
        for c in range(CH):
            i = b * CH + c

            def emit_pj(t, i=i, b=b, c=c):
                qb = psQ[:, 512 * (i % 2):512 * (i % 2) + 512]
                kb = psK[:, 512 * (i % 2):512 * (i % 2) + 512]
                vb = psC[:, 512 * (i % 2):512 * (i % 2) + 512]
                for ft in range(FT):
                    t.matmul(qb, wq16[:, ft, :], xtb[:, i % 2, ft, :],
                             start=(ft == 0), stop=(ft == FT - 1))
                for ft in range(FT):
                    t.matmul(kb, wk16[:, ft, :], xtb[:, i % 2, ft, :],
                             start=(ft == 0), stop=(ft == FT - 1))
                last = None
                for q in range(4):
                    for ft in range(FT):
                        last = t.matmul(vb[:, 128 * q:128 * q + 128],
                                        xtb[:, i % 2, ft, 128 * q:128 * q + 128],
                                        wv16[:, ft, :],
                                        start=(ft == 0), stop=(ft == FT - 1))
                return last
            w = [f"ld_xt{i}"]
            if i == 0:
                w += ["ld_wq", "ld_wk", "ld_wv", "ld_wo", "v_init",
                      "ld_mask", "ld_ident"]
            if i >= 2:
                w += [f"pj_ev{i-2}"]
            if c == 0 and b >= 1:
                w += [f"cx_ev{2 * b * CH - 1}"]
            P.op("tensor", emit_pj, w, signal=f"pj_mm{i}")

            def emit_ev(a, i=i, b=b, c=c):
                qb = psQ[:, 512 * (i % 2):512 * (i % 2) + 512]
                kb = psK[:, 512 * (i % 2):512 * (i % 2) + 512]
                vb = psC[:, 512 * (i % 2):512 * (i % 2) + 512]
                a.activation(qt[:, b % 2, 512 * c:512 * c + 512], qb, AF.Tanh)
                a.activation(ktp0[0:64, b % 2, 512 * c:512 * c + 512], kb[0:64, :], AF.Tanh)
                a.activation(ktp1[64:128, b % 2, 512 * c:512 * c + 512], kb[64:128, :], AF.Tanh)
                a.activation(vtt[:, b % 2, 4 * c:4 * c + 4, 0:64],
                             vb.rearrange("p (a n) -> p a n", n=128)[:, :, 0:64], AF.Tanh)
                return a.activation(vtt[:, b % 2, 4 * c:4 * c + 4, 65:129],
                                    vb.rearrange("p (a n) -> p a n", n=128)[:, :, 64:128],
                                    AF.Tanh)
            P.op("scalar", emit_ev, [f"pj_mm{i}"], signal=f"pj_ev{i}")

        def avbank(q):
            return (psX if q < 2 else psC)[:, 512 * (q % 2):512 * (q % 2) + 512]

        for c in range(CH):
            uidx = b * CH + c

            for h in range(2):
                blk = 2 * uidx + h
                pend = []

                def emit_av(t, ph, pg, pglob, b=b, c=c):
                    last = None
                    for ktl in (2 * pg, 2 * pg + 1):
                        slot = pglob % 6
                        half = ktl % 2
                        for q in av_targets(c, ktl):
                            last = t.matmul(
                                avbank(q)[:, 0:65],
                                expst[:, slot, 512 * half + 128 * q:512 * half + 128 * q + 128],
                                vtt[:, b % 2, ktl, 65 * ph:65 * ph + 65],
                                start=(ktl == 0), stop=(ktl == 4 * c + q))
                    return last

                for g in range(2 * c + 2):
                    glob = nsc

                    def emit_sc(t, h=h, g=g, glob=glob, b=b, c=c):
                        last = None
                        for ktl in (2 * g, 2 * g + 1):
                            bank = (psQ if glob % 2 == 0 else psK)[
                                :, 512 * (ktl % 2):512 * (ktl % 2) + 512]
                            ktp = ktp0 if h == 0 else ktp1
                            diag = ktl >= 4 * c
                            last = t.matmul(bank, ktp[:, b % 2, 128 * ktl:128 * ktl + 128],
                                            qt[:, b % 2, 512 * c:512 * c + 512],
                                            start=True, stop=not diag)
                            if diag:
                                r = ktl - 4 * c
                                last = t.matmul(bank[:, 128 * r:128 * r + 128],
                                                ident[:, :], maskb[:, :],
                                                start=False, stop=True)
                        return last
                    w = []
                    if g == 0 and blk == 0:
                        w += [f"pj_ev{CH - 1}"]
                    elif g == 0 and h == 0:
                        w += [f"pj_ev{(b + 1) * CH - 1}"]
                    if glob >= 2:
                        w += [f"at_ex{glob-2}"]
                    P.op("tensor", emit_sc, w, signal=f"at_sc{glob}")

                    def emit_ex(a, glob=glob):
                        return a.activation(expst[:, glob % 6, :],
                                            (psQ if glob % 2 == 0 else psK)[:, 0:1024],
                                            AF.Exp, scale=0.125)
                    we = [f"at_sc{glob}"]
                    if glob >= 6:
                        we += [f"at_av{glob-6}"]
                    P.op("scalar", emit_ex, we, signal=f"at_ex{glob}")

                    pend.append((h, g, glob))
                    nsc += 1
                    if len(pend) >= 2:
                        ph_, pg_, pglob_ = pend.pop(0)
                        wv = [f"at_ex{pglob_}"]
                        if pg_ == 0 and blk > 0:
                            wv += [f"cx_ev{blk-1}"]
                        P.op("tensor",
                             lambda t, ph=ph_, pg=pg_, pglob=pglob_, fn=emit_av: fn(t, ph, pg, pglob),
                             wv, signal=f"at_av{pglob_}")
                ph_, pg_, pglob_ = pend.pop(0)
                wv = [f"at_ex{pglob_}"]
                if pg_ == 0 and blk > 0:
                    wv += [f"cx_ev{blk-1}"]
                P.op("tensor",
                     lambda t, ph=ph_, pg=pg_, pglob=pglob_, fn=emit_av: fn(t, ph, pg, pglob),
                     wv, signal=f"at_av{pglob_}")

                def emit_cx(v, b=b, c=c, h=h):
                    last = None
                    for q in range(4):
                        j = 4 * c + q
                        bank = avbank(q)
                        vrecip(v, scal[:, SL["tmp"]:SL["tmp"] + 1],
                               bank[:, 64:65])
                        last = v.tensor_scalar(
                            ctxq[:, b % 2, j, 64 * h:64 * h + 64],
                            bank[:, 0:64],
                            scal[:, SL["tmp"]:SL["tmp"] + 1], None, op0=ALU.mult)
                    return last
                wcx = [f"at_av{nsc-1}"]
                if b >= 2:
                    wcx += [f"cxst{b-2}b"]
                P.op("vector", emit_cx, wcx, signal=f"cx_ev{blk}")
                pop_dq(1)

        dma_s(f"cxst{b}a", a2a_i[2 * b, :, :].rearrange("(a p) d -> p a d", p=128),
              ctxq[:, b % 2, 0:S // 256, :], waits=[f"cx_ev{2 * (b + 1) * CH - 1}"],
              ch="cx")
        dma_s(f"cxst{b}b", a2a_i[2 * b + 1, :, :].rearrange("(a p) d -> p a d", p=128),
              ctxq[:, b % 2, S // 256:S // 128, :], waits=[f"cx_ev{2 * (b + 1) * CH - 1}"],
              ch="cx")

    n_groups = nsc

    P.op("gpsimd", lambda g: g.collective_compute(
        "AllToAll", ALU.bypass, replica_groups=[list(range(8))],
        ins=[a2a_i.ap().opt()], outs=[a2a_o.ap().opt()]),
        [f"cxst{B-1}b"], signal="a2a", amount=1, sem="cc")

    for j in range(FT):
        dma_s(f"ctxT{j}", ctxT[:, j, :], a2a_o[j, :, :], waits=["a2a"], tr=True,
              ch="cT")

    pop_dq(len(dq_queue))

    for tt in range(TT):
        w = ([f"ln_c{tt-2}"] if tt >= 2 else [])
        dma_s(f"xm{tt}", xmst[:, tt % 2, :], xre(xm)[:, tt, :], waits=w,
              ch=f"xm{tt % 2}")

    for tt in range(TT):
        def emit_wo(t, tt=tt):
            ps = psQ if tt % 2 == 0 else psK
            last = None
            for ch in range(2):
                bank = ps[:, 512 * ch:512 * ch + 512]
                for j in range(FT):
                    last = t.matmul(bank, ctxT[:, j, 128 * tt:128 * tt + 128],
                                    wo16[:, j, 512 * ch:512 * ch + 512],
                                    start=(j == 0), stop=(j == FT - 1))
            return last
        if tt == 0:
            w = [f"ctxT{FT-1}", f"at_ex{n_groups-1}"]
        elif tt == 1:
            w = [f"ctxT{FT-1}"]
        else:
            w = [f"wo_ev{tt-2}"]
        P.op("tensor", emit_wo, w, signal=f"wo_mm{tt}")

        def emit_woev(a, tt=tt):
            ps = psQ if tt % 2 == 0 else psK
            a.activation(lnsc[:, tt % 2, 0:512], ps[:, 0:512], AF.Tanh)
            return a.activation(lnsc[:, tt % 2, 512:1024], ps[:, 512:1024], AF.Tanh)
        wev = [f"wo_mm{tt}"] + ([f"ln_c{tt-2}"] if tt >= 2 else [])
        P.op("scalar", emit_woev, wev, signal=f"wo_ev{tt}")

        def emit_res(v, tt=tt):
            v.tensor_tensor(y32[:, tt, :], lnsc[:, tt % 2, :], xmst[:, tt % 2, :],
                            op=ALU.add)
            return v.reduce_sum(lnst[:, tt:tt + 1], y32[:, tt, :], axis=AX.X)
        P.op("vector", emit_res, [f"wo_ev{tt}", f"xm{tt}"], signal=f"ln_c{tt}")

    for tt in range(TT):
        def emit_sq(a, tt=tt):
            return a.activation(lnsq[:, :], y32[:, tt, :], AF.Square,
                                accum_out=lnst2[:, tt:tt + 1])
        P.op("scalar", emit_sq, [f"ln_c{tt}"], signal=f"ln_sq{tt}")
    for tt in range(TT):
        def emit_var(v, tt=tt):
            m = lnst[:, tt:tt + 1]
            s2 = lnst2[:, tt:tt + 1]
            v.tensor_scalar(m, m, 1.0 / D, None, op0=ALU.mult)
            v.tensor_tensor(scal[:, SL["tmp"]:SL["tmp"] + 1], m, m, op=ALU.mult)
            v.tensor_scalar(s2, s2, 1.0 / D, None, op0=ALU.mult)
            v.tensor_tensor(s2, s2, scal[:, SL["tmp"]:SL["tmp"] + 1],
                            op=ALU.subtract)
            return v.tensor_scalar(s2, s2, 1e-5, None, op0=ALU.add)
        P.op("vector", emit_var, [f"ln_sq{tt}"], signal=f"ln_v{tt}")
    for tt in range(TT):
        def emit_rs(a, tt=tt):
            return a.activation(lnst2[:, tt:tt + 1], lnst2[:, tt:tt + 1],
                                AF.Sqrt)
        P.op("scalar", emit_rs, [f"ln_v{tt}"], signal=f"ln_rs{tt}")
    for tt in range(TT):
        def emit_y(v, tt=tt):
            vrecip(v, lnst2[:, tt:tt + 1], lnst2[:, tt:tt + 1])
            v.tensor_scalar(y32[:, tt, :], y32[:, tt, :], lnst[:, tt:tt + 1],
                            lnst2[:, tt:tt + 1], op0=ALU.subtract, op1=ALU.mult)
            vreduce(v, scal[:, SL["tmp"]:SL["tmp"] + 1], y32[:, tt, :],
                    axis=AX.X, op=ALU.max, apply_absolute_value=True)
            return v.tensor_tensor(scal[:, SL["m1a"]:SL["m1a"] + 1],
                                   scal[:, SL["m1a"]:SL["m1a"] + 1],
                                   scal[:, SL["tmp"]:SL["tmp"] + 1], op=ALU.max)
        P.op("vector", emit_y, [f"ln_rs{tt}"], signal=f"ln_y{tt}")

    dma_s("y_spill", xre(yd), y32[:, :, :], waits=[f"ln_y{TT-1}"], ch="sp")

    def mbarrier(idx, name, acc_slot, extra_c13, waits):
        dma_g(f"{name}_p1", mxp.ap().rearrange("(p n) -> p n", n=1),
              scal[:, acc_slot:acc_slot + 1], waits=waits, ch="gM")
        dma_g(f"{name}_p2", bcrow[:, :],
              mxp.ap().rearrange("(n p) -> n p", p=128), waits=[f"{name}_p1"],
              ch="gM")

        def vpar(v):
            vreduce(v, scal[0:1, SL["tmp"] + 1:SL["tmp"] + 2],
                    bcrow[:, :], axis=AX.X, op=ALU.max,
                    apply_absolute_value=True)
            return v.nop()
        P.op("vector", vpar, [f"{name}_p2"], signal=f"{name}_par")
        dma_g(f"{name}_o", mxi[idx][:, :], scal[0:1, SL["tmp"] + 1:SL["tmp"] + 2],
              waits=[f"{name}_par"], ch="gM")
        P.op("gpsimd", lambda g: g.collective_compute(
            "AllReduce", ALU.max, replica_groups=[list(range(8))],
            ins=[mxi[idx].ap().opt()], outs=[mxo[idx].ap().opt()]),
            [f"{name}_o"], signal=f"{name}_ar", amount=1, sem="cc")
        dma_g(f"{name}_i", scal[0:1, SL["row"]:SL["row"] + 1], mxo[idx][:, :],
              waits=[f"{name}_ar"], ch="gM")

        def vscale(v):
            r = SL["row"]
            vsync(v, v.tensor_scalar(scal[0:1, r + 1:r + 2], scal[0:1, r:r + 1],
                                     IQA, 1e-12, op0=ALU.mult, op1=ALU.max))
            vrecip(v, scal[0:1, r + 2:r + 3], scal[0:1, r + 1:r + 2])
            if extra_c13:
                return v.tensor_scalar(scal[0:1, r + 3:r + 4], scal[0:1, r + 1:r + 2],
                                       1.0 / 65536.0, None, op0=ALU.mult)
            return v.nop()
        P.op("vector", vscale, [f"{name}_i"], signal=f"{name}_vs")

    def mb_bcast(name, slots):
        def bcmm(t):
            last = None
            for n, (dst, ro) in enumerate(slots):
                last = t.matmul(psX[:, 960 + n:960 + n + 1], bcones[:, :],
                                scal[0:1, SL["row"] + ro:SL["row"] + ro + 1],
                                start=True, stop=True)
            return last
        P.op("tensor", bcmm, [f"{name}_vs"], signal=f"{name}_bm")

        def bccp(v):
            last = None
            for n, (dst, ro) in enumerate(slots):
                last = v.tensor_copy(scal[:, dst:dst + 1],
                                     psX[:, 960 + n:960 + n + 1])
            vsync(v, last)
            return v.nop()
        P.op("vector", bccp, [f"{name}_bm"], signal=f"{name}_bc")

    mbarrier(0, "m1", SL["m1a"], False, [f"ln_y{TT-1}"])
    mb_bcast("m1", [(SL["s1"], 1), (SL["r1"], 2)])

    for tt in range(TT):
        def emit_xq(v, tt=tt):
            v.tensor_scalar(lnsq[:, :], y32[:, tt, :], scal[:, SL["r1"]:SL["r1"] + 1],
                            RC, op0=ALU.mult, op1=ALU.add)
            return v.tensor_scalar(xq16[:, tt, :], lnsq[:, :], -RC,
                                   scal[:, SL["s1"]:SL["s1"] + 1],
                                   op0=ALU.add, op1=ALU.mult)
        P.op("vector", emit_xq, ["m1_bc"] if tt == 0 else [], signal=f"xq{tt}")
    dma_s("xq_spill", xre(xqd), xq16[:, :, :], waits=[f"xq{TT-1}"], ch="sp")

    for ft in range(FT):
        dma_s(f"ytq{ft}", ytq[:, ft, :], xqd[:, 128 * ft:128 * ft + 128],
              waits=["xq_spill"], tr=True, ch="ytq")
    for ft in range(FT):
        dma_s(f"wtgG{ft}", wtg[:, ft, :], wg16d[:, 128 * ft:128 * ft + 128],
              waits=[f"dq_st{IT-2}", f"dq_st{IT-1}"], tr=True, ch="wgG")

    def bank4(n):
        return (psQ if (n % 4) < 2 else psK)[:, 512 * (n % 2):512 * (n % 2) + 512]

    nmm = 0
    for wsel in range(2):
        for it in range(IT):
            for tc in range(TC):
                def emit_ug(t, it=it, tc=tc, nmm=nmm):
                    bk = bank4(nmm)
                    last = None
                    for ft in range(FT):
                        last = t.matmul(bk, wtg[:, ft, 128 * it:128 * it + 128],
                                        ytq[:, ft, 512 * tc:512 * tc + 512],
                                        start=(ft == 0), stop=(ft == FT - 1))
                    return last
                w = []
                if nmm == 0:
                    w = [f"ytq{FT-1}", f"wtgG{FT-1}", f"wo_ev{TT-1}"]
                if wsel == 1 and it == 0 and tc == 0:
                    w += [f"wtgU{FT-1}"]
                if nmm >= 4:
                    w += [f"ug_ev{nmm-4}"]
                P.op("tensor", emit_ug, w, signal=f"ug_mm{nmm}")

                def emit_ugev(a, wsel=wsel, it=it, tc=tc, nmm=nmm):
                    stg = stg1 if wsel == 0 else stg2
                    return a.activation(stg[:, it % 2, 512 * tc:512 * tc + 512],
                                        bank4(nmm), AF.Copy)
                wev = [f"ug_mm{nmm}"]
                if nmm >= 2 * TC:
                    wev += [f"ug_mx{nmm - 2 * TC}", f"rawsp{(nmm - 2 * TC) // (IT * TC)}_{((nmm - 2 * TC) // TC) % IT}"]
                P.op("scalar", emit_ugev, wev, signal=f"ug_ev{nmm}")

                def emit_ugmx(v, wsel=wsel, it=it, tc=tc):
                    stg = stg1 if wsel == 0 else stg2
                    acc = SL["m3a"] if wsel == 0 else SL["m2a"]
                    vreduce(v, scal[:, SL["tmp"]:SL["tmp"] + 1],
                            stg[:, it % 2, 512 * tc:512 * tc + 512],
                            axis=AX.X, op=ALU.max, apply_absolute_value=True)
                    return v.tensor_tensor(scal[:, acc:acc + 1], scal[:, acc:acc + 1],
                                           scal[:, SL["tmp"]:SL["tmp"] + 1], op=ALU.max)
                P.op("vector", emit_ugmx, [f"ug_ev{nmm}"], signal=f"ug_mx{nmm}")
                nmm += 1
            stgd = graw if wsel == 0 else uraw
            stgs = stg1 if wsel == 0 else stg2
            dma_s(f"rawsp{wsel}_{it}", xre(stgd)[:, it, :], stgs[:, it % 2, :],
                  waits=[f"ug_ev{nmm-1}"], ch="rwG" if wsel == 0 else "rwU")
        if wsel == 0:
            for ft in range(FT):
                dma_s(f"wtgU{ft}", wtg[:, ft, :], wu16d[:, 128 * ft:128 * ft + 128],
                      waits=[f"ug_mm{IT * TC - 1}", f"dq_st{2*IT-2}", f"dq_st{2*IT-1}"],
                      tr=True, ch="wgU")

    mbarrier(1, "m3", SL["m3a"], True, [f"ug_mx{IT * TC - 1}"])
    mb_bcast("m3", [(SL["s3"], 1), (SL["r3"], 2), (SL["c13"], 3)])
    mbarrier(2, "m2", SL["m2a"], False, [f"ug_mx{2 * IT * TC - 1}"])
    mb_bcast("m2", [(SL["s2"], 1), (SL["r2"], 2)])

    def emit_s1a(v, it):
        p = it % 2
        v.tensor_scalar(fA[:, p, :], stg1[:, p, :], scal[:, SL["r3"]:SL["r3"] + 1],
                        RC, op0=ALU.mult, op1=ALU.add)
        return v.tensor_scalar(fA[:, p, :], fA[:, p, :], -RC, None, op0=ALU.add)

    def emit_sig(a, it):
        p = it % 2
        a.activation(fB[:, p, :], fA[:, p, :], AF.Sigmoid,
                     scale=scal[:, SL["s3"]:SL["s3"] + 1])
        return a.activation(fB[:, p, :], fB[:, p, :], AF.Copy,
                            scale=65536.0, bias=RC)

    def emit_s1b(v, it):
        p = it % 2
        v.tensor_scalar(fB[:, p, :], fB[:, p, :], -RC, 65535.0, op0=ALU.add, op1=ALU.min)
        v.scalar_tensor_tensor(stg3[:, p, :], fA[:, p, :],
                               scal[:, SL["c13"]:SL["c13"] + 1],
                               fB[:, p, :], op0=ALU.mult, op1=ALU.mult)
        vreduce(v, scal[:, SL["tmp"]:SL["tmp"] + 1], stg3[:, p, :],
                axis=AX.X, op=ALU.max, apply_absolute_value=True)
        return v.tensor_tensor(scal[:, SL["m4a"]:SL["m4a"] + 1],
                               scal[:, SL["m4a"]:SL["m4a"] + 1],
                               scal[:, SL["tmp"]:SL["tmp"] + 1], op=ALU.max)

    def s1_dma(it):
        w = [f"rawsp0_{IT-1}"] + ([f"s1a{it-2}"] if it >= 2 else [])
        dma_g(f"s1g{it}", stg1[:, it % 2, :], xre(graw)[:, it, :], waits=w,
              ch=f"fI{it % 2}")

    for it in range(IT):
        if it >= 2:
            dma_s(f"s1p{it-2}", xre(prodd)[:, it - 2, :],
                  stg3[:, (it - 2) % 2, :], waits=[f"s1v{it-2}"], ch=f"fO{(it-2) % 2}")
        s1_dma(it)
        w = [f"s1g{it}", "m3_bc"] + ([f"s1sg{it-2}"] if it >= 2 else [])
        P.op("vector", lambda v, it=it: emit_s1a(v, it), w, signal=f"s1a{it}")
        ws = [f"s1a{it}"] + ([f"s1v{it-2}"] if it >= 2 else [])
        P.op("scalar", lambda a, it=it: emit_sig(a, it), ws, signal=f"s1sg{it}")
        if it >= 1:
            pit = it - 1
            w = [f"s1sg{pit}"] + ([f"s1p{pit-2}"] if pit >= 2 else [])
            P.op("vector", lambda v, it=pit: emit_s1b(v, it), w, signal=f"s1v{pit}")
    pit = IT - 1
    P.op("vector", lambda v, it=pit: emit_s1b(v, it),
         [f"s1sg{pit}", f"s1p{pit-2}"], signal=f"s1v{pit}")
    for it in (IT - 2, IT - 1):
        dma_s(f"s1p{it}", xre(prodd)[:, it, :], stg3[:, it % 2, :],
              waits=[f"s1v{it}"], ch=f"fO{it % 2}")

    mbarrier(3, "m4", SL["m4a"], False, [f"s1v{IT-1}"])
    mb_bcast("m4", [(SL["s4"], 1), (SL["r4"], 2)])

    for it in range(IT):
        p = it % 2
        q4 = it % 4
        w = [f"s1p{IT-2}", f"s1p{IT-1}"]
        if it >= 4:
            w += [f"s2r{it-4}"]
        dma_g(f"s2i{it}", stg3[:, q4, :], xre(prodd)[:, it, :], waits=w,
              ch=f"fI{p}")
        w2 = [f"rawsp1_{IT-1}"]
        if it >= 4:
            w2 += [f"s2v{it-4}"]
        dma_g(f"s2u{it}", stg2[:, q4, :], xre(uraw)[:, it, :], waits=w2,
              ch=f"fJ{p}")

        def emit_s2r(a, it=it, p=p, q4=q4):
            a.activation(fB[:, p, :], stg2[:, q4, :], AF.Copy,
                         scale=scal[:, SL["r2"]:SL["r2"] + 1], bias=RC)
            return a.activation(fA[:, p, :], stg3[:, q4, :], AF.Copy,
                                scale=scal[:, SL["r4"]:SL["r4"] + 1], bias=RC)
        wr = [f"s2i{it}", f"s2u{it}", "m4_bc", "m2_bc"]
        if it >= 2:
            wr += [f"s2v{it-2}"]
        else:
            wr += [f"s1v{IT-1}"]
        P.op("scalar", emit_s2r, wr, signal=f"s2r{it}")

        def emit_s2(v, it=it, p=p, q4=q4):
            v.tensor_scalar(stg2[:, q4, :], fB[:, p, :], -RC,
                            scal[:, SL["s2"]:SL["s2"] + 1],
                            op0=ALU.add, op1=ALU.mult)
            v.tensor_scalar(fA[:, p, :], fA[:, p, :], -RC, scal[:, SL["s4"]:SL["s4"] + 1],
                            op0=ALU.add, op1=ALU.mult)
            v.tensor_tensor(stg1[:, p, :], fA[:, p, :], stg2[:, q4, :], op=ALU.mult)
            vreduce(v, scal[:, SL["tmp"]:SL["tmp"] + 1], stg1[:, p, :],
                    axis=AX.X, op=ALU.max, apply_absolute_value=True)
            return v.tensor_tensor(scal[:, SL["m5a"]:SL["m5a"] + 1],
                                   scal[:, SL["m5a"]:SL["m5a"] + 1],
                                   scal[:, SL["tmp"]:SL["tmp"] + 1], op=ALU.max)
        w3 = [f"s2r{it}"]
        if it >= 2:
            w3 += [f"s2o{it-2}"]
        P.op("vector", emit_s2, w3, signal=f"s2v{it}")
        dma_s(f"s2o{it}", xre(qsd)[:, it, :], stg1[:, p, :], waits=[f"s2v{it}"],
              ch=f"fO{p}")

    mbarrier(4, "m5", SL["m5a"], False, [f"s2v{IT-1}"])
    mb_bcast("m5", [(SL["s5"], 1), (SL["r5"], 2)])

    def bank8(idx):
        tensors = [psQ, psQ, psK, psK, psC, psC, psX, psX]
        return tensors[idx][:, 512 * (idx % 2):512 * (idx % 2) + 512]

    ns3 = min(8, (2 * TM) // 512)

    def emit_ln2(tg):
        for q in range(4):
            tt = 4 * tg + q

            def emit_dnev(a, q=q, tt=tt):
                a.activation(dnsc[:, tt % 4, 0:512], bank8(2 * q), AF.Copy)
                return a.activation(dnsc[:, tt % 4, 512:1024], bank8(2 * q + 1), AF.Copy)
            w = [f"dn_it{tg}_{IT-1}"]
            if tt >= 4:
                w += [f"o_y{tt-4}"]
            P.op("scalar", emit_dnev, w, signal=f"dn_ev{tt}")

            w = ["xq_spill"] + ([f"o_res{tt-2}"] if tt >= 2 else [])
            dma_g(f"yr{tt}", yrl[:, tt % 2, :], xre(yd)[:, tt, :], waits=w,
                  ch=f"yr{tt % 2}")
        for q in range(4):
            tt = 4 * tg + q

            def emit_ores(v, tt=tt):
                v.tensor_tensor(dnsc[:, tt % 4, :], dnsc[:, tt % 4, :],
                                yrl[:, tt % 2, :], op=ALU.add)
                return v.reduce_sum(lnst[:, 2 * (tt % 4):2 * (tt % 4) + 1],
                                    dnsc[:, tt % 4, :], axis=AX.X)
            P.op("vector", emit_ores, [f"dn_ev{tt}", f"yr{tt}"], signal=f"o_res{tt}")
        for q in range(4):
            tt = 4 * tg + q

            def emit_osq(a, tt=tt):
                return a.activation(lnsq[:, :], dnsc[:, tt % 4, :], AF.Square,
                                    accum_out=lnst[:, 2 * (tt % 4) + 1:2 * (tt % 4) + 2])
            P.op("scalar", emit_osq, [f"o_res{tt}"], signal=f"o_sq{tt}")
        for q in range(4):
            tt = 4 * tg + q

            def emit_ovar(v, tt=tt):
                st0 = 2 * (tt % 4)
                m = lnst[:, st0:st0 + 1]
                s2 = lnst[:, st0 + 1:st0 + 2]
                v.tensor_scalar(m, m, 1.0 / D, None, op0=ALU.mult)
                v.tensor_tensor(scal[:, SL["tmp"]:SL["tmp"] + 1], m, m, op=ALU.mult)
                v.tensor_scalar(s2, s2, 1.0 / D, None, op0=ALU.mult)
                v.tensor_tensor(s2, s2, scal[:, SL["tmp"]:SL["tmp"] + 1],
                                op=ALU.subtract)
                return v.tensor_scalar(s2, s2, 1e-5, None, op0=ALU.add)
            P.op("vector", emit_ovar, [f"o_sq{tt}"], signal=f"o_var{tt}")
        for q in range(4):
            tt = 4 * tg + q

            def emit_ors(a, tt=tt):
                st0 = 2 * (tt % 4)
                return a.activation(lnst[:, st0 + 1:st0 + 2], lnst[:, st0 + 1:st0 + 2],
                                    AF.Sqrt)
            P.op("scalar", emit_ors, [f"o_var{tt}"], signal=f"o_rs{tt}")
        for q in range(4):
            tt = 4 * tg + q

            def emit_oy(v, tt=tt):
                st0 = 2 * (tt % 4)
                vrecip(v, lnst[:, st0 + 1:st0 + 2], lnst[:, st0 + 1:st0 + 2])
                return v.tensor_scalar(outb[:, tt % 2, :], dnsc[:, tt % 4, :],
                                       lnst[:, st0:st0 + 1], lnst[:, st0 + 1:st0 + 2],
                                       op0=ALU.subtract, op1=ALU.mult)
            w = [f"o_rs{tt}"] + ([f"ost{tt-2}"] if tt >= 2 else [])
            P.op("vector", emit_oy, w, signal=f"o_y{tt}")
            dma_g(f"ost{tt}", xre(out)[:, tt, :], outb[:, tt % 2, :], waits=[f"o_y{tt}"],
                  ch=f"os{tt % 2}")

    for tg in range(TG):
        for it in range(IT):
            gi = tg * IT + it
            w = [f"s2o{IT-2}", f"s2o{IT-1}"]
            if gi >= ns3:
                w += [f"s3v{gi-ns3}"]
            else:
                w += [f"s2v{IT-1}"]
            dma_g(f"s3i{gi}", stg1v[:, gi % ns3, :],
                  xre(qsd)[:, it, 512 * tg:512 * tg + 512], waits=w,
                  ch=f"sI{gi % ns3}")
            ww = [f"dq_st{ndq-2}", f"dq_st{ndq-1}"]
            if gi >= 4:
                ww += [f"dn_it{(gi-4)//IT}_{(gi-4)%IT}"]
            dma_s(f"s3w{gi}", wtd[:, gi % 4, :], wd16d[:, 128 * it:128 * it + 128],
                  waits=ww, tr=True, ch=f"wJ{gi % 4}")

            def emit_s3(v, it=it, gi=gi, tg=tg):
                v.tensor_scalar(fA[:, gi % 2, 0:512], stg1v[:, gi % ns3, :],
                                scal[:, SL["r5"]:SL["r5"] + 1], RC,
                                op0=ALU.mult, op1=ALU.add)
                return v.tensor_scalar(ot[:, tg % 2, it, :], fA[:, gi % 2, 0:512], -RC,
                                       scal[:, SL["s5"]:SL["s5"] + 1],
                                       op0=ALU.add, op1=ALU.mult)
            w2 = [f"s3i{gi}", "m5_bc"]
            if tg >= 2:
                w2 += [f"dn_it{tg-2}_{IT-1}"]
            P.op("vector", emit_s3, w2, signal=f"s3v{gi}")

        for it in range(IT):
            gi = tg * IT + it

            def emit_dn(t, tg=tg, it=it, gi=gi):
                last = None
                for q in range(4):
                    for ch in range(2):
                        last = t.matmul(bank8(2 * q + ch),
                                        ot[:, tg % 2, it, 128 * q:128 * q + 128],
                                        wtd[:, gi % 4, 512 * ch:512 * ch + 512],
                                        start=(it == 0), stop=(it == IT - 1))
                return last
            w = [f"s3v{gi}", f"s3w{gi}"]
            if it == 0:
                if tg == 0:
                    w += [f"ug_ev{2 * IT * TC - 1}", f"cx_ev{2 * B * CH - 1}",
                          f"at_ex{n_groups-1}"]
                else:
                    w += [f"dn_ev{4*(tg-1)+q}" for q in range(4)]
            P.op("tensor", emit_dn, w, signal=f"dn_it{tg}_{it}")

        if tg >= 1:
            emit_ln2(tg - 1)
    emit_ln2(TG - 1)

    P.op("sync", lambda s: s.nop(), [f"ost{TT-2}", f"ost{TT-1}"], signal=None)

    chans = ["vself", "ini", "cx", "cT", "sp", "ytq", "wgG", "wgU", "rwG", "rwU",
             "xm0", "xm1", "dqL0", "dqL1", "dqS0", "dqS1", "fI0", "fI1",
             "fJ0", "fJ1", "fO0", "fO1", "yr0", "yr1", "os0", "os1",
             "gW", "gM", "g0", "g1", "cc",
             "sI0", "sI1", "sI2", "sI3", "sI4", "sI5", "sI6", "sI7",
             "wJ0", "wJ1", "wJ2", "wJ3",
             "c_tensor", "c_scalar", "c_vector", "c_gpsimd", "c_sync"]
    import contextlib
    with contextlib.ExitStack() as stack:
        sems = {k: stack.enter_context(nc.semaphore(k)) for k in chans}
        vself["sem"] = sems["vself"]
        block = stack.enter_context(nc.Block())
        engines = {}

        def reg(name):
            def f(e):
                engines[name] = e
                run(name)
            return f
        run = P.build_and_emit(sems, engines)
        block.sync(reg("sync"))
        block.gpsimd(reg("gpsimd"))
        block.tensor(reg("tensor"))
        block.scalar(reg("scalar"))
        block.vector(reg("vector"))
    return nc


def _in_maps(cfg, inputs):
    st = np.asarray(inputs["L0_attn_states"], dtype=np.float32)
    B, S, D = st.shape
    flat = np.ascontiguousarray(st.reshape(B * S, D))
    xt = np.ascontiguousarray(flat.T)
    wq = np.asarray(inputs["L0_attn_Wq"], np.float32)
    wk = np.asarray(inputs["L0_attn_Wk"], np.float32)
    wv = np.asarray(inputs["L0_attn_Wv"], np.float32)
    wo = np.ascontiguousarray(np.asarray(inputs["L0_attn_Wo"], np.float32))
    wgn = np.ascontiguousarray(np.asarray(inputs["L0_ffn_w_gate"], np.float32))
    wun = np.ascontiguousarray(np.asarray(inputs["L0_ffn_w_up"], np.float32))
    wdn = np.ascontiguousarray(np.asarray(inputs["L0_ffn_w_down"], np.float32))
    tri = np.triu(np.ones((128, 128), np.float32))
    maskc = ((tri - 1.0) * 30000.0).astype(np.float16)
    identc = np.eye(128, dtype=np.float16)
    TM = cfg.TM
    maps = []
    for c in range(8):
        maps.append(dict(
            xt=xt,
            xm=np.ascontiguousarray(flat[TM * c:TM * (c + 1)]),
            wq=np.ascontiguousarray(wq[:, 128 * c:128 * c + 128]),
            wk=np.ascontiguousarray(wk[:, 128 * c:128 * c + 128]),
            wv=np.ascontiguousarray(wv[:, 128 * c:128 * c + 128]),
            wo=wo, wgn=wgn, wun=wun, wdn=wdn,
            maskc=maskc, identc=identc,
        ))
    return maps


def kernel(**inputs):
    st = np.asarray(inputs["L0_attn_states"])
    B, S, D = st.shape
    cfg = CFG(S=S)
    nc = build_nc(cfg)
    maps = _in_maps(cfg, inputs)
    res = run_bass_kernel_spmd(nc, maps, core_ids=list(range(8)))
    outp = np.concatenate([np.asarray(r["out"]) for r in res.results], axis=0)
    return outp.reshape(B, S, D).astype(np.float32)
